# version 13
# baseline (speedup 1.0000x reference)
import numpy as np
from contextlib import ExitStack

import concourse.bass as bass
import concourse.mybir as mybir
from concourse.bass_utils import run_bass_kernel_spmd

F32 = mybir.dt.float32
BF16 = mybir.dt.bfloat16
AF = mybir.ActivationFunctionType
ALU = mybir.AluOpType
AX = mybir.AxisListType

B, T, D = 32, 2048, 1024
NCORES = 8
IPC = B // NCORES
NPG = 8
NPF = 4
NKT = T // 128
QC = 512
NQC = T // QC
HID = 2730
HID2 = 5460
WIN = 512
QK_SCALE = 8.0
CHW = [128] * 21 + [42]
NCH = len(CHW)

_W1_PAIR_OFF = []
_off = 0
for _w in CHW:
    _W1_PAIR_OFF.append(_off)
    _off += 2 * _w
assert _off == HID2


def _rotary_tables():
    half = np.arange(0, 64, 2, dtype=np.float64)
    inv_freq = 1.0 / (10000.0 ** (half / 64.0))
    t = np.arange(T, dtype=np.float64)
    freqs = t[:, None] * inv_freq[None, :]
    freqs = np.concatenate([freqs, freqs], axis=1)
    base = (half + 0.4 * 64.0) / (1.4 * 64.0)
    power = (t - T // 2) / float(WIN // 2)
    scale = base[None, :] ** power[:, None]
    scale = np.concatenate([scale, scale], axis=1)
    cos, sin = np.cos(freqs), np.sin(freqs)
    Aq = (cos * scale).T
    Sq = (sin * scale).T
    Ak = (cos / scale).T
    Sk = (sin / scale).T

    def mkC(S):
        C = np.empty_like(S)
        C[0:32] = S[32:64]
        C[32:64] = -S[0:32]
        return C

    def dup(a):
        return np.concatenate([a, a], axis=0).astype(np.float32)

    return dup(Aq), dup(mkC(Sq)), dup(Ak), dup(mkC(Sk))


def _host_consts():
    c = {}
    c["ident"] = np.eye(128, dtype=np.float32)
    id2 = np.zeros((128, 64), np.float32)
    id2[0:64] = np.eye(64)
    id2[64:128] = np.eye(64)
    c["ident2"] = id2
    P = np.zeros((128, 128), np.float32)
    for p in range(128):
        blk = (p // 64) * 64
        P[blk + ((p - blk + 32) % 64), p] = 1.0
    c["perm"] = P
    c["rAq"], c["rCq"], c["rAk"], c["rCk"] = _rotary_tables()
    i = np.arange(128)
    c["mlow"] = (i[None, :] <= i[:, None]).astype(np.float32)
    c["mup"] = (i[None, :] >= i[:, None]).astype(np.float32)
    c["ones_row"] = np.ones((1, 128), np.float32)
    c["negones_row"] = -np.ones((1, 128), np.float32)
    oc = np.ones((128, 3), np.float32)
    oc[64:, 1] = 0.0
    oc[:64, 2] = 0.0
    c["onescols"] = oc
    c["epscols"] = np.array([[1e-5, 1e-24]], np.float32)
    return c


def _w1_cols():
    order = []
    off = 0
    for w in CHW:
        order.append(("a", off, w))
        order.append(("g", HID + off, w))
        off += w
    cols = []
    for _, start, w in order:
        cols.extend(range(start, start + w))
    return np.array(cols, np.int64)


_W1_COLS = _w1_cols()


def _layout_weights(inp):
    L = {}
    Wq, Wk, Wv = inp["L0_attn_Wq"], inp["L0_attn_Wk"], inp["L0_attn_Wv"]
    WQK = np.zeros((128, NPG, 128), np.float32)
    WV = np.zeros((128, NPG, 64), np.float32)
    for h in range(16):
        par, p = h % 2, h // 2
        WQK[64 * par:64 * par + 64, p, 0:64] = Wq[h]
        WQK[64 * par:64 * par + 64, p, 64:128] = Wk[h]
        WV[64 * par:64 * par + 64, p, :] = Wv[h]
    L["wqk_g"] = WQK.reshape(128, NPG * 128)
    L["wv_g"] = WV.reshape(128, NPG * 64)
    for nm, arr in (("bq", inp["L0_attn_bq"]), ("bk", inp["L0_attn_bk"]),
                    ("bv", inp["L0_attn_bv"])):
        t = np.zeros((128, 16), np.float32)
        t[0:64] = arr.T
        t[64:128] = arr.T
        L[nm] = t
    L["wo_g"] = inp["L0_attn_Wo"]
    L["bo_t"] = np.ascontiguousarray(inp["L0_attn_bo"].reshape(8, 128).T)
    wqkv = inp["L0_ffn_wqkv"]
    pm = np.zeros((2, NPF, 1024, 384), np.float32)
    for l in range(2):
        for p in range(NPF):
            pm[l, p, :, 0:128] = wqkv[l][:, 128 * p:128 * p + 128]
            pm[l, p, :, 128:256] = wqkv[l][:, 512 + 128 * p:512 + 128 * p + 128]
            pm[l, p, :, 256:384] = wqkv[l][:, 1024 + 128 * p:1024 + 128 * p + 128]
    L["wqkv_pm"] = pm
    L["wo_f"] = inp["L0_ffn_wo"]
    L["w1_pm"] = np.ascontiguousarray(inp["L0_ffn_w1"][:, :, _W1_COLS])
    b1 = inp["L0_ffn_b1"][:, _W1_COLS]
    b1c = np.zeros((2, 128, 2 * NCH), np.float32)
    for l in range(2):
        off = 0
        for j, w in enumerate(CHW):
            b1c[l, 0:w, 2 * j] = b1[l, off:off + w]
            b1c[l, 0:w, 2 * j + 1] = b1[l, off + w:off + 2 * w]
            off += 2 * w
    L["b1c"] = b1c
    L["w2_n"] = inp["L0_ffn_w2"]
    L["b2_t"] = np.stack([np.ascontiguousarray(inp["L0_ffn_b2"][l].reshape(8, 128).T)
                          for l in range(2)])
    L["lg_t"] = np.stack([np.ascontiguousarray(inp["L0_ffn_ln_g"][l].reshape(8, 128).T)
                          for l in range(2)])
    L["lb_t"] = np.stack([np.ascontiguousarray(inp["L0_ffn_ln_b"][l].reshape(8, 128).T)
                          for l in range(2)])
    ebr = np.zeros((1, 8 * 128), np.float32)
    qs, ks = inp["L0_ffn_q_scale"], inp["L0_ffn_k_scale"]
    for l in range(2):
        for qi, arr in ((0, qs[l]), (1, ks[l])):
            for par in range(2):
                i = ((l * 2 + qi) * 2 + par) * 128
                ebr[0, i + 64 * par:i + 64 * par + 64] = arr
    L["ebr"] = ebr
    qr = np.zeros((1, 256), np.float32)
    qr[0, 0:64] = qs[0]; qr[0, 64:128] = ks[0]
    qr[0, 128:192] = qs[1]; qr[0, 192:256] = ks[1]
    L["qkrows"] = qr
    return L


class Prog:
    CENG = ("tensor", "vector", "scalar", "gpsimd", "sync")

    SAME_ENGINE_SYNC = True
    SAME_ENGINE_RAW_SYNC = False

    def __init__(self, nc):
        self.nc = nc
        self.ops = {e: [] for e in self.CENG}
        self.sem = {}
        self.cnt = {}
        self.lastwait = {e: {} for e in self.CENG}
        self.bufw = {}
        self.bufr = {}
        self.psum_banks = {}
        self.nops = 0

    def add_sem(self, name, handle):
        self.sem[name] = handle
        self.cnt[name] = 0

    @staticmethod
    def _esem(eng):
        return {"tensor": "sT", "vector": "sV", "scalar": "sS",
                "gpsimd": "sP", "sync": "sY"}[eng]

    def _wait(self, eng, semname, val):
        if val <= 0:
            return
        if self.lastwait[eng].get(semname, 0) >= val:
            return
        self.lastwait[eng][semname] = val
        sem = self.sem[semname]
        self.ops[eng].append(lambda E, s=sem, v=val: E.wait_ge(s, v))
        self.nops += 1

    def bank_keys(self, t):
        nm = t.name
        if nm not in self.psum_banks:
            m = self.nc.lookup_mloc(t)
            bpp = int(np.prod(t.shape[1:])) * (4 if t.dtype == F32 else 2)
            nb = (bpp + 2047) // 2048
            self.psum_banks[nm] = tuple(f"bank{m.bank + i}" for i in range(nb))
        return self.psum_banks[nm]

    def emit(self, eng, fns, reads=(), writes=(), sem=None, inc=None, psr=(),
             psw=()):
        if callable(fns):
            fns = [fns]
        semname = sem or self._esem(eng)
        amount = inc if inc is not None else (16 if semname in ("dx", "dw", "do") else 1)
        reads = list(reads)
        writes = list(writes)
        for t in psr:
            reads.extend(self.bank_keys(t))
        for t in psw:
            writes.extend(self.bank_keys(t))
        own = None if self.SAME_ENGINE_SYNC else self._esem(eng)
        rown = None if self.SAME_ENGINE_RAW_SYNC else self._esem(eng)
        for k in reads:
            w = self.bufw.get(k)
            if w is not None and w[0] != rown:
                self._wait(eng, w[0], w[1])
            if k.startswith("bank"):
                for s, v in self.bufr.get(k, {}).items():
                    if s != semname:
                        self._wait(eng, s, v)
        for k in writes:
            w = self.bufw.get(k)
            if w is not None and w[0] != own:
                self._wait(eng, w[0], w[1])
            for s, v in self.bufr.get(k, {}).items():
                if s != own:
                    self._wait(eng, s, v)
        sem_h = self.sem[semname]
        n = len(fns)
        for i, fn in enumerate(fns):
            if i == n - 1:
                self.ops[eng].append(
                    lambda E, f=fn, s=sem_h, a=amount: f(E).then_inc(s, a))
            else:
                self.ops[eng].append(lambda E, f=fn: f(E))
        self.nops += n
        self.cnt[semname] += amount
        val = self.cnt[semname]
        for k in writes:
            self.bufw[k] = (semname, val)
            self.bufr[k] = {}
        for k in reads:
            self.bufr.setdefault(k, {})[semname] = val
        return semname, val

    def replay(self, block):
        for eng in self.CENG:
            oplist = self.ops[eng]

            def run(E, oplist=oplist):
                for f in oplist:
                    f(E)
            getattr(block, eng)(run)


HKEYS = [f"H{t}" for t in range(8)]


def build_program(n_items=IPC, dump=(), dev_stage=None):
    nc = bass.Bass("TRN2", target_bir_lowering=False)
    es = ExitStack()
    Et = es.enter_context

    d = {}
    d["x"] = nc.dram_tensor("x", [IPC, T, D], F32, kind="ExternalInput").ap()
    d["out"] = nc.dram_tensor("out", [IPC, T, D], F32, kind="ExternalOutput").ap()
    for nm, shape in [
        ("ident", [128, 128]), ("ident2", [128, 64]), ("perm", [128, 128]),
        ("rAq", [128, T]), ("rCq", [128, T]), ("rAk", [128, T]), ("rCk", [128, T]),
        ("mlow", [128, 128]), ("mup", [128, 128]),
        ("ones_row", [1, 128]), ("negones_row", [1, 128]), ("onescols", [128, 3]),
        ("wqk_g", [128, NPG * 128]), ("wv_g", [128, NPG * 64]),
        ("bq", [128, 16]), ("bk", [128, 16]), ("bv", [128, 16]),
        ("wo_g", [1024, 1024]), ("bo_t", [128, 8]),
        ("wqkv_pm", [2, NPF, 1024, 384]), ("wo_f", [2, 512, 1024]),
        ("w1_pm", [2, 1024, HID2]), ("b1c", [2, 128, 2 * NCH]),
        ("w2_n", [2, HID, 1024]), ("b2_t", [2, 128, 8]),
        ("lg_t", [2, 128, 8]), ("lb_t", [2, 128, 8]),
        ("ebr", [1, 8 * 128]), ("qkrows", [1, 256]), ("epscols", [1, 2]),
    ]:
        d[nm] = nc.dram_tensor(nm, shape, F32, kind="ExternalInput").ap()
    d["x1sp"] = nc.dram_tensor("x1sp", [8, 128, T], F32, kind="Internal").ap()
    d["w1b"] = nc.dram_tensor("w1b", [2, 1024, HID2], BF16, kind="Internal").ap()
    d["w2b"] = nc.dram_tensor("w2b", [2, HID, 1024], BF16, kind="Internal").ap()
    dumps = {}
    for nm in dump:
        shp = [128, T] if nm.startswith("t_") else [8, 128, T]
        dumps[nm] = nc.dram_tensor("dump_" + nm, shp, F32,
                                   kind="ExternalOutput").ap()

    X = [Et(nc.sbuf_tensor(f"X{t}", [128, T], F32)) for t in range(8)]
    H = [Et(nc.sbuf_tensor(f"H{t}", [128, T], BF16)) for t in range(8)]
    OFF = [Et(nc.sbuf_tensor(f"OF{t}", [128, T], BF16)) for t in range(4)]
    STG = Et(nc.sbuf_tensor("STG", [128, T], F32))
    Qp = Et(nc.sbuf_tensor("Qp", [128, T], BF16))
    Kp = Et(nc.sbuf_tensor("Kp", [128, T], BF16))
    V = Et(nc.sbuf_tensor("V", [128, T], BF16))
    VTe = Et(nc.sbuf_tensor("VTe", [128, NKT * 72], BF16))
    VTo = Et(nc.sbuf_tensor("VTo", [128, NKT * 72], BF16))
    EX = Et(nc.sbuf_tensor("EX", [128, 2048], BF16))
    GLU = Et(nc.sbuf_tensor("GLU", [128, 6 * 512], BF16))
    S0 = Et(nc.sbuf_tensor("S0", [128, 4096], BF16))
    S1 = Et(nc.sbuf_tensor("S1", [128, 4096], BF16))
    RA = {nm: Et(nc.sbuf_tensor("t" + nm, [128, T], BF16))
          for nm in ("rAq", "rCq", "rAk", "rCk")}
    IDf = Et(nc.sbuf_tensor("IDf", [128, 128], F32))
    ID2 = Et(nc.sbuf_tensor("ID2", [128, 64], BF16))
    PRM = Et(nc.sbuf_tensor("PRM", [128, 128], BF16))
    MLW = Et(nc.sbuf_tensor("MLW", [128, 128], BF16))
    MUP = Et(nc.sbuf_tensor("MUP", [128, 128], BF16))
    ONR = Et(nc.sbuf_tensor("ONR", [1, 128], BF16))
    ONRF = Et(nc.sbuf_tensor("ONRF", [1, 128], F32))
    NONR = Et(nc.sbuf_tensor("NONR", [1, 128], BF16))
    OCL = Et(nc.sbuf_tensor("OCL", [128, 3], BF16))
    OCLF = Et(nc.sbuf_tensor("OCLF", [128, 3], F32))
    WQKG = Et(nc.sbuf_tensor("WQKG", [128, NPG * 128], BF16))
    WVG = Et(nc.sbuf_tensor("WVG", [128, NPG * 64], BF16))
    BQ = Et(nc.sbuf_tensor("BQ", [128, 16], F32))
    BK = Et(nc.sbuf_tensor("BK", [128, 16], F32))
    BV = Et(nc.sbuf_tensor("BV", [128, 16], F32))
    BO = Et(nc.sbuf_tensor("BO", [128, 8], F32))
    B1C = Et(nc.sbuf_tensor("B1C", [128, 4 * NCH], F32))
    B2T = Et(nc.sbuf_tensor("B2T", [128, 16], F32))
    LGT = Et(nc.sbuf_tensor("LGT", [128, 16], F32))
    LBT = Et(nc.sbuf_tensor("LBT", [128, 16], F32))
    EBR = Et(nc.sbuf_tensor("EBR", [1, 8 * 128], BF16))
    QKR = Et(nc.sbuf_tensor("QKR", [1, 256], F32))
    EPS = Et(nc.sbuf_tensor("EPS", [1, 2], F32))
    ROWS = Et(nc.sbuf_tensor("ROWS", [1, 4 * QC], F32))
    RRF = Et(nc.sbuf_tensor("RRF", [1, 2 * QC], F32))
    ROWSB = Et(nc.sbuf_tensor("ROWSB", [1, 2 * QC], BF16))
    ROWX = Et(nc.sbuf_tensor("ROWX", [1, 16], F32))
    ROWXB = Et(nc.sbuf_tensor("ROWXB", [1, 16], BF16))
    NEGC = Et(nc.sbuf_tensor("NEGC", [128, 2], F32))
    TMPF = Et(nc.sbuf_tensor("TMPF", [128, QC], F32))
    TMPF2 = Et(nc.sbuf_tensor("TMPF2", [128, QC], F32))
    TMPB = Et(nc.sbuf_tensor("TMPB", [128, QC], BF16))

    sems = {}
    for s in ("sT", "sV", "sS", "sP", "sY", "dx", "dw", "do"):
        sems[s] = Et(nc.semaphore(s))

    block = Et(nc.Block())
    P = Prog(nc)
    for s, h in sems.items():
        P.add_sem(s, h)

    def mm(out_ap, lhsT_ap, rhs_ap, start=True, stop=True, skip=False):
        return lambda E: E.matmul(out_ap, lhsT_ap, rhs_ap, start=start,
                                  stop=stop, skip_group_check=skip)

    def psum(name, shape, dtype=F32):
        return nc.psum_tensor(name, shape, dtype)

    STGK = ["stgA", "stgB"]

    def load_consts():
        for dst, src in [(IDf[:], d["ident"][:]), (BQ[:], d["bq"][:]),
                         (BK[:], d["bk"][:]), (BV[:], d["bv"][:]),
                         (BO[:], d["bo_t"][:]),
                         (B1C[:, 0:2 * NCH], d["b1c"][0]),
                         (B1C[:, 2 * NCH:4 * NCH], d["b1c"][1]),
                         (B2T[:, 0:8], d["b2_t"][0]), (B2T[:, 8:16], d["b2_t"][1]),
                         (LGT[:, 0:8], d["lg_t"][0]), (LGT[:, 8:16], d["lg_t"][1]),
                         (LBT[:, 0:8], d["lb_t"][0]), (LBT[:, 8:16], d["lb_t"][1]),
                         (QKR[:], d["qkrows"][:]), (EPS[:], d["epscols"][:]),
                         (OCLF[:], d["onescols"][:]),
                         (ONRF[:], d["ones_row"][:])]:
            P.emit("sync", lambda E, a=dst, b=src: E.dma_start(out=a, in_=b),
                   writes=["const"], sem="dx")
        for dst, src in [(ID2[:], d["ident2"][:]), (PRM[:], d["perm"][:]),
                         (MLW[:], d["mlow"][:]), (MUP[:], d["mup"][:]),
                         (ONR[:], d["ones_row"][:]), (NONR[:], d["negones_row"][:]),
                         (OCL[:], d["onescols"][:]),
                         (WQKG[:], d["wqk_g"][:]), (WVG[:], d["wv_g"][:]),
                         (EBR[:], d["ebr"][:]),
                         (RA["rAq"][:], d["rAq"][:]), (RA["rCq"][:], d["rCq"][:]),
                         (RA["rAk"][:], d["rAk"][:]), (RA["rCk"][:], d["rCk"][:])]:
            P.emit("gpsimd", lambda E, a=dst, b=src: E.dma_start(out=a, in_=b),
                   writes=["const"], sem="dw")
        vte3 = VTe[:].rearrange("p (k c) -> p k c", c=72)
        vto3 = VTo[:].rearrange("p (k c) -> p k c", c=72)
        P.emit("vector", lambda E: E.memset(vte3[:, :, 64:65], 1.0),
               writes=["VT0_ones"])
        P.emit("vector", lambda E: E.memset(vto3[:, :, 64:65], 1.0),
               writes=["VT1_ones"])

    def calc_negc():
        with psum("pcng", [128, 512]) as pc:
            for l in range(2):
                P.emit("vector", lambda E, l=l: E.reduce_max(
                    ROWX[:, 4 * l:4 * l + 1], QKR[:, 128 * l:128 * l + 64],
                    axis=AX.X, apply_absolute_value=True),
                    reads=["const"], writes=["rowx"])
                P.emit("vector", lambda E, l=l: E.reduce_max(
                    ROWX[:, 4 * l + 1:4 * l + 2], QKR[:, 128 * l + 64:128 * l + 128],
                    axis=AX.X, apply_absolute_value=True),
                    reads=["const"], writes=["rowx"])
                P.emit("vector", lambda E, l=l: E.tensor_tensor(
                    ROWX[:, 4 * l + 2:4 * l + 3], ROWX[:, 4 * l:4 * l + 1],
                    ROWX[:, 4 * l + 1:4 * l + 2], ALU.mult),
                    reads=["rowx"], writes=["rowx"])
                P.emit("vector", lambda E, l=l: E.tensor_scalar(
                    ROWXB[:, l:l + 1], ROWX[:, 4 * l + 2:4 * l + 3],
                    QK_SCALE, None, ALU.mult, ALU.bypass),
                    reads=["rowx"], writes=["rowxb"])
                P.emit("tensor", mm(pc[:, 0:1], NONR[:], ROWXB[:, l:l + 1]),
                       reads=["rowxb", "const"], psw=[pc])
                P.emit("vector", lambda E, l=l: E.tensor_copy(
                    NEGC[:, l:l + 1], pc[:, 0:1]),
                    psr=[pc], writes=["negc"])

    def phase_in(it):
        stg2 = STG[:].rearrange("p (f c) -> p f c", f=2)
        with psum(f"ptri0_{it}", [128, 512]) as pt0, \
             psum(f"ptri1_{it}", [128, 512]) as pt1:
            for st in range(16):
                half = st % 2
                hk = STGK[half]
                src_ap = d["x"][it, 128 * st:128 * (st + 1), :]
                P.emit("sync", lambda E, b=src_ap, half=half: E.dma_start(
                    out=stg2[:, half, :], in_=b), writes=[hk], sem="dx")
                for cg in range(2):
                    pt = pt0 if cg == 0 else pt1
                    fns = [lambda E, ct=4 * cg + ci, half=half, pt=pt, ci=ci:
                           E.transpose(pt[:, 128 * ci:128 * (ci + 1)],
                                       stg2[:, half, 128 * ct:128 * ct + 128],
                                       IDf[:])
                           for ci in range(4)]
                    P.emit("tensor", fns, reads=[hk, "const"], psw=[pt])
                    for ci in range(4):
                        ct = 4 * cg + ci
                        P.emit("vector", lambda E, ct=ct, st=st, pt=pt, ci=ci:
                               E.tensor_copy(X[ct][:, 128 * st:128 * (st + 1)],
                                             pt[:, 128 * ci:128 * (ci + 1)]),
                               psr=[pt], writes=[f"X{ct}"])

    def cast_H(shifted):
        for t in range(8):
            if not shifted or t < 4:
                P.emit("vector", lambda E, t=t: E.tensor_copy(H[t][:], X[t][:]),
                       reads=[f"X{t}"], writes=[f"H{t}"])
            else:
                P.emit("vector", lambda E, t=t: E.tensor_copy(
                    H[t][:, 1:T], X[t][:, 0:T - 1]),
                    reads=[f"X{t}"], writes=[f"H{t}"])
                P.emit("vector", lambda E, t=t: E.memset(H[t][:, 0:1], 0.0),
                       writes=[f"H{t}"])

    def v_transpose(tag):
        vte3 = VTe[:].rearrange("p (k c) -> p k c", c=72)
        vto3 = VTo[:].rearrange("p (k c) -> p k c", c=72)
        with psum("ptv" + tag, [128, 1024], BF16) as ptv:
            for par, vdst3 in enumerate([vte3, vto3]):
                fns = []
                for kt in range(NKT):
                    if par == 0:
                        fns.append(lambda E, kt=kt: E.transpose(
                            ptv[:, 64 * kt:64 * (kt + 1)],
                            V[0:64, 128 * kt:128 * (kt + 1)], ID2[0:64, :]))
                    else:
                        fns.append(lambda E, kt=kt: E.transpose(
                            ptv[:, 64 * kt:64 * (kt + 1)],
                            V[64:128, 128 * kt:128 * (kt + 1)], ID2[64:128, :]))
                P.emit("tensor", fns, reads=["V", "const"], psw=[ptv])
                P.emit("vector", lambda E, v3=vdst3: E.tensor_copy(
                    v3[:, :, 0:64], ptv[:].rearrange("p (k c) -> p k c", c=64)),
                    psr=[ptv], writes=[f"VT{par}"])

    F32R = mybir.dt.float32r

    def attn_qc(o_dst, o_key, ktl_list, q0, nq_of, negc_ap, mask_of, itq,
                psc, pav_e, pav_o, pending, exp_scale=1.0, av_of=None):
        if av_of is None:
            av_of = nq_of
        av_started = [False, False]
        ngroups = 0
        for gi in range(0, len(ktl_list), 2):
            grp = ktl_list[gi:gi + 2]
            offs = []
            o = 0
            for kt, ktl in grp:
                qs, nv = nq_of(ktl)
                offs.append((kt, ktl, qs, nv, o))
                o += nv
            for par in range(2):
                b0 = 64 * par
                pskey = f"psc{par}"
                fns = [mm(psc[:, 1024 * par + oo:1024 * par + oo + nv],
                          Kp[b0:b0 + 64, 128 * kt:128 * (kt + 1)],
                          Qp[b0:b0 + 64, q0 + qs:q0 + qs + nv])
                       for kt, ktl, qs, nv, oo in offs]
                P.emit("tensor", fns, reads=["Qp", "Kp"],
                       writes=[P.bank_keys(psc)[2 * par],
                               P.bank_keys(psc)[2 * par + 1]])
                if negc_ap is not None:
                    fn = lambda E, par=par, o=o: E.activation(
                        EX[:, 1024 * par:1024 * par + o],
                        psc[:, 1024 * par:1024 * par + o], AF.Exp,
                        bias=negc_ap, scale=exp_scale)
                else:
                    fn = lambda E, par=par, o=o: E.activation(
                        EX[:, 1024 * par:1024 * par + o],
                        psc[:, 1024 * par:1024 * par + o], AF.Exp,
                        scale=exp_scale)
                P.emit("scalar", [fn], reads=["negc",
                                              P.bank_keys(psc)[2 * par],
                                              P.bank_keys(psc)[2 * par + 1]],
                       writes=[f"EX{par}"])
            for par in range(2):
                for kt, ktl, qs, nv, oo in offs:
                    mk = mask_of(ktl)
                    if mk is None:
                        continue
                    Mt, qb = mk
                    bo = 1024 * par + oo + (qb - qs)
                    P.emit("vector",
                           lambda E, bo=bo, Mt=Mt: E.tensor_tensor(
                               EX[:, bo:bo + 128], EX[:, bo:bo + 128],
                               Mt[:], ALU.mult),
                           reads=[f"EX{par}", "const"], writes=[f"EX{par}"])
            for par, pav, vt, vtk in ((0, pav_e, VTe, "VT0"),
                                      (1, pav_o, VTo, "VT1")):
                fns = []
                for kt, ktl, qs, nv, oo in offs:
                    st = not av_started[par]
                    av_started[par] = True
                    aqs, anv = av_of(ktl)
                    eo = 1024 * par + oo + aqs - qs
                    fns.append(mm(pav[0:65, aqs:aqs + anv],
                                  vt[:, 72 * kt:72 * kt + 65],
                                  EX[:, eo:eo + anv],
                                  start=st, stop=False, skip=True))
                P.emit("tensor", fns, reads=[f"EX{par}", vtk], psw=[pav])
            ngroups += 1
            if ngroups == 2 and pending:
                for fn in pending:
                    fn()
                pending = []
        for fn in pending:
            fn()

        def mk_combine(par, pav):
            tav = TMPF[0:65, :] if par == 0 else TMPF2[0:65, :]
            tkey = "tmpf" if par == 0 else "tmpf2"
            rrb = ROWSB[:, 0:QC] if par == 0 else ROWSB[:, QC:2 * QC]
            rbk = "rowsbA" if par == 0 else "rowsbB"

            def emit_combine():
                P.emit("scalar", lambda E: E.copy(tav, pav[0:65, 0:QC]),
                       psr=[pav], writes=[tkey])

                rrf = RRF[:, 0:QC] if par == 0 else RRF[:, QC:2 * QC]
                P.emit("scalar", lambda E, tav=tav, rrf=rrf: E.activation(
                    rrf, tav[64:65, :], AF.Ln),
                    reads=[tkey], writes=[rbk + "f"])
                P.emit("scalar", lambda E, rrb=rrb, rrf=rrf: E.activation(
                    rrb, rrf, AF.Exp, scale=-1.0),
                    reads=[rbk + "f"], writes=[rbk])
                P.emit("tensor", mm(pav[0:64, 0:QC], ONR[0:1, 0:64], rrb),
                       reads=[rbk, "const"], psw=[pav])
                P.emit("vector", lambda E, par=par: E.tensor_tensor(
                    o_dst(par), tav[0:64, :], pav[0:64, 0:QC], ALU.mult),
                    reads=[tkey], psr=[pav], writes=[o_key])
            return emit_combine
        return [mk_combine(0, pav_e), mk_combine(1, pav_o)]

    def global_pair(it, pair):
        hkey = f"H{pair}"
        with psum(f"pqe{it}{pair}", [128, T]) as pqe, \
             psum(f"pqo{it}{pair}", [128, T]) as pqo:
            for par, pq in ((0, pqe), (1, pqo)):
                b0 = 64 * par
                fns = [mm(pq[:, QC * q:QC * (q + 1)],
                          WQKG[b0:b0 + 64, 128 * pair:128 * (pair + 1)],
                          H[pair][b0:b0 + 64, QC * q:QC * (q + 1)])
                       for q in range(NQC)]
                P.emit("tensor", fns, reads=[hkey, "const"], psw=[pq])
            for par, pq in ((0, pqe), (1, pqo)):
                h = 2 * pair + par
                P.emit("scalar", lambda E, pq=pq, par=par, h=h: E.activation(
                    Qp[64 * par:64 * par + 64, :], pq[0:64, :], AF.Identity,
                    bias=BQ[0:64, h:h + 1]),
                    psr=[pq], reads=["const"], writes=["Qp"])
                P.emit("scalar", lambda E, pq=pq, par=par, h=h: E.activation(
                    Kp[64 * par:64 * par + 64, :], pq[64:128, :], AF.Identity,
                    bias=BK[64:128, h:h + 1]),
                    psr=[pq], reads=["const"], writes=["Kp"])
        with psum(f"pve{it}{pair}", [64, QC]) as pve, \
             psum(f"pvo{it}{pair}", [128, QC]) as pvo:
            for q in range(NQC):
                P.emit("tensor", [mm(pve[:, :],
                                     WVG[0:64, 64 * pair:64 * (pair + 1)],
                                     H[pair][0:64, QC * q:QC * (q + 1)])],
                       reads=[hkey, "const"], psw=[pve])
                P.emit("scalar", lambda E, q=q, h=2 * pair: E.activation(
                    V[0:64, QC * q:QC * (q + 1)], pve[:, :], AF.Identity,
                    bias=BV[0:64, h:h + 1]),
                    psr=[pve], reads=["const"], writes=["V"])
                P.emit("tensor", [mm(pvo[64:128, :],
                                     WVG[64:128, 64 * pair:64 * (pair + 1)],
                                     H[pair][64:128, QC * q:QC * (q + 1)])],
                       reads=[hkey, "const"], psw=[pvo])
                P.emit("scalar", lambda E, q=q, h=2 * pair + 1: E.activation(
                    V[64:128, QC * q:QC * (q + 1)], pvo[64:128, :], AF.Identity,
                    bias=BV[64:128, h:h + 1]),
                    psr=[pvo], reads=["const"], writes=["V"])
        v_transpose(f"g{it}_{pair}")
        with psum(f"gsc{it}{pair}", [128, 2048]) as psc, \
             psum(f"gae{it}{pair}", [65, 512]) as pavAe, \
             psum(f"gao{it}{pair}", [65, 512]) as pavAo, \
             psum(f"gbe{it}{pair}", [65, 512]) as pavBe, \
             psum(f"gbo{it}{pair}", [65, 512]) as pavBo:
            pending = []
            for q in range(NQC):
                def o_dst(par, q=q, pair=pair):
                    return H[pair][64 * par:64 * par + 64, QC * q:QC * (q + 1)]
                pe, po = (pavAe, pavAo) if q % 2 == 0 else (pavBe, pavBo)
                pending = attn_qc(o_dst, hkey, [(kt, kt) for kt in range(NKT)],
                                  QC * q, lambda ktl: (0, QC), None,
                                  lambda ktl: None, itq=f"g{it}_{pair}_{q}",
                                  psc=psc, pav_e=pe, pav_o=po,
                                  pending=pending, exp_scale=0.125)
            for fn in pending:
                fn()

    class Stream:
        def __init__(self):
            self.slots = [S0, S1]
            self.queue = []
            self.next_issue = 0
            self.assign = {}

        def push(self, name, fns):
            if callable(fns):
                fns = [fns]
            self.queue.append((name, fns))

        def issue_upto(self, n):
            while self.next_issue < min(n, len(self.queue)):
                i = self.next_issue
                name, fns = self.queue[i]
                slot = self.slots[i % 2]
                skey = f"slot{i % 2}"
                self.assign[name] = (slot, skey)
                for fn in fns:
                    P.emit("gpsimd", lambda E, fn=fn, slot=slot: fn(E, slot),
                           writes=[skey], sem="dw")
                self.next_issue += 1

        def use(self, name):
            idx = self.names.index(name)
            self.issue_upto(idx + 2)
            return self.assign[name]

        @property
        def names(self):
            return [n for n, _ in self.queue]

    def wo_global(it, stream):
        with psum(f"pwo{it}", [128, QC]) as pw, \
             psum(f"pwo2{it}", [128, QC]) as pw2:
            for q in range(NQC):
                for ch in range(4):
                    slot, skey = stream.use(f"wo{q}_{ch}")
                    for oc in (2 * ch, 2 * ch + 1):
                        pt = pw if (oc + q) % 2 == 0 else pw2
                        c0 = 128 * (oc - 2 * ch)
                        fns = [mm(pt[:, :],
                                  slot[:, 256 * ic + c0:256 * ic + c0 + 128],
                                  H[ic][:, QC * q:QC * (q + 1)],
                                  start=(ic == 0), stop=(ic == 7))
                               for ic in range(8)]
                        P.emit("tensor", fns, reads=[skey] + HKEYS, psw=[pt])
                        P.emit("vector", lambda E, pt=pt, oc=oc, q=q:
                               E.scalar_tensor_tensor(
                                   X[oc][:, QC * q:QC * (q + 1)], pt[:, :],
                                   BO[:, oc:oc + 1], X[oc][:, QC * q:QC * (q + 1)],
                                   ALU.add, ALU.add),
                               psr=[pt], reads=["const", f"X{oc}"],
                               writes=[f"X{oc}"])

    def module_ln(it, tag, spill):
        mu = ROWS[:, 0:QC]
        rstd = ROWS[:, QC:2 * QC]
        tmp = ROWS[:, 2 * QC:3 * QC]
        with psum(f"plrA{it}{tag}", [1, 1024]) as plrA, \
             psum(f"plrB{it}{tag}", [1, 1024]) as plrB, \
             psum(f"pm1A{it}{tag}", [128, QC]) as pm1A, \
             psum(f"pm2A{it}{tag}", [128, QC]) as pm2A, \
             psum(f"pm1B{it}{tag}", [128, QC]) as pm1B, \
             psum(f"pm2B{it}{tag}", [128, QC]) as pm2B:
            def stats(q):
                plr = plrA if q % 2 == 0 else plrB
                for t in range(8):
                    cb = V[:, 0:QC] if t % 2 == 0 else V[:, QC:2 * QC]
                    cbk = "V" if t % 2 == 0 else "V"
                    sq = TMPB if t % 2 == 0 else V[:, 2 * QC:3 * QC]
                    sqk = "tmpb" if t % 2 == 0 else "V"
                    P.emit("vector", lambda E, t=t, q=q, cb=cb: E.tensor_copy(
                        cb, X[t][:, QC * q:QC * (q + 1)]),
                        reads=[f"X{t}"], writes=[cbk])
                    P.emit("tensor", [mm(plr[:, 0:512], OCL[:, 0:1], cb,
                                         start=(t == 0), stop=(t == 7))],
                           reads=[cbk, "const"], psw=[plr])
                    P.emit("vector", lambda E, cb=cb, sq=sq: E.tensor_tensor(
                        sq[:, 0:QC], cb, cb, ALU.mult),
                        reads=[cbk], writes=[sqk])
                    P.emit("tensor", [mm(plr[:, 512:1024], OCL[:, 0:1],
                                         sq[:, 0:QC], start=(t == 0),
                                         stop=(t == 7))],
                           reads=[sqk, "const"], psw=[plr])

            def finish(q):
                plr = plrA if q % 2 == 0 else plrB
                pb1 = pm1A if q % 2 == 0 else pm1B
                pb2 = pm2A if q % 2 == 0 else pm2B
                P.emit("vector", lambda E, plr=plr: E.tensor_scalar(
                    mu, plr[:, 0:512], 1.0 / D, None, ALU.mult, ALU.bypass),
                    psr=[plr], writes=["rows"])
                P.emit("vector", lambda E: E.tensor_tensor(
                    tmp, mu, mu, ALU.mult), reads=["rows"], writes=["rows"])
                P.emit("vector", lambda E, plr=plr: E.scalar_tensor_tensor(
                    rstd, plr[:, 512:1024], 1.0 / D, tmp, ALU.mult, ALU.subtract),
                    psr=[plr], reads=["rows"], writes=["rows"])
                P.emit("scalar", lambda E: E.activation(
                    rstd, rstd, AF.Ln, bias=EPS[:, 0:1]),
                    reads=["rows", "const"], writes=["rows"])
                P.emit("scalar", lambda E: E.activation(
                    rstd, rstd, AF.Exp, scale=-0.5),
                    reads=["rows"], writes=["rows"])
                P.emit("tensor", mm(pb1[:, :], ONRF[:], mu),
                       reads=["rows", "const"], psw=[pb1])
                P.emit("tensor", mm(pb2[:, :], ONRF[:], rstd),
                       reads=["rows", "const"], psw=[pb2])
                for t in range(8):
                    P.emit("vector", lambda E, t=t, q=q, pb1=pb1: E.tensor_tensor(
                        X[t][:, QC * q:QC * (q + 1)], X[t][:, QC * q:QC * (q + 1)],
                        pb1[:, :], ALU.subtract),
                        reads=[f"X{t}"], psr=[pb1], writes=[f"X{t}"])
                    P.emit("vector", lambda E, t=t, q=q, pb2=pb2: E.tensor_tensor(
                        X[t][:, QC * q:QC * (q + 1)], X[t][:, QC * q:QC * (q + 1)],
                        pb2[:, :], ALU.mult),
                        reads=[f"X{t}"], psr=[pb2], writes=[f"X{t}"])
                if spill:
                    for t in range(8):
                        P.emit("sync", lambda E, t=t, q=q: E.dma_start(
                            out=d["x1sp"][t][:, QC * q:QC * (q + 1)],
                            in_=X[t][:, QC * q:QC * (q + 1)]),
                            reads=[f"X{t}"], writes=["x1d"], sem="dx")

            stats(0)
            for q in range(1, NQC):
                stats(q)
                finish(q - 1)
            finish(NQC - 1)

    def inner_ln(it, l):
        lgv = LGT[:, 8 * l:8 * l + 8]
        lbv = LBT[:, 8 * l:8 * l + 8]
        mu = ROWS[:, 0:QC]
        rstd = ROWS[:, QC:2 * QC]
        tmp = ROWS[:, 2 * QC:3 * QC]
        with psum(f"pilA{it}{l}", [1, 1024]) as plrA, \
             psum(f"pilB{it}{l}", [1, 1024]) as plrB, \
             psum(f"pi1A{it}{l}", [128, QC]) as pi1A, \
             psum(f"pi2A{it}{l}", [128, QC]) as pi2A, \
             psum(f"pi1B{it}{l}", [128, QC]) as pi1B, \
             psum(f"pi2B{it}{l}", [128, QC]) as pi2B:
            def stats(q):
                plr = plrA if q % 2 == 0 else plrB
                for t in range(8):
                    cb = V[:, 0:QC] if t % 2 == 0 else V[:, QC:2 * QC]
                    cbk = "V" if t % 2 == 0 else "V"
                    sq = TMPB if t % 2 == 0 else V[:, 2 * QC:3 * QC]
                    sqk = "tmpb" if t % 2 == 0 else "V"
                    if t < 4:
                        cl, n0 = 0, 512
                        P.emit("vector", lambda E, t=t, q=q, cb=cb: E.tensor_copy(
                            cb, X[t][:, QC * q:QC * (q + 1)]),
                            reads=[f"X{t}"], writes=[cbk])
                    elif q == 0:
                        cl, n0 = 1, 511
                        P.emit("vector", lambda E, t=t, cb=cb: E.tensor_copy(
                            cb[:, 1:QC], X[t][:, 0:QC - 1]),
                            reads=[f"X{t}"], writes=[cbk])
                    else:
                        cl, n0 = 0, 512
                        P.emit("vector", lambda E, t=t, q=q, cb=cb: E.tensor_copy(
                            cb, X[t][:, QC * q - 1:QC * (q + 1) - 1]),
                            reads=[f"X{t}"], writes=[cbk])
                    P.emit("tensor", [mm(plr[:, cl:cl + n0], OCL[:, 0:1],
                                         cb[:, cl:cl + n0],
                                         start=(t == 0), stop=(t == 7))],
                           reads=[cbk, "const"], psw=[plr])
                    P.emit("vector", lambda E, cb=cb, sq=sq, cl=cl, n0=n0:
                           E.tensor_tensor(sq[:, cl:cl + n0], cb[:, cl:cl + n0],
                                           cb[:, cl:cl + n0], ALU.mult),
                           reads=[cbk], writes=[sqk])
                    P.emit("tensor", [mm(plr[:, 512 + cl:512 + cl + n0],
                                         OCL[:, 0:1], sq[:, cl:cl + n0],
                                         start=(t == 0), stop=(t == 7))],
                           reads=[sqk, "const"], psw=[plr])

            def finish(q):
                plr = plrA if q % 2 == 0 else plrB
                pb1 = pi1A if q % 2 == 0 else pi1B
                pb2 = pi2A if q % 2 == 0 else pi2B
                P.emit("vector", lambda E, plr=plr: E.tensor_scalar(
                    mu, plr[:, 0:512], 1.0 / D, None, ALU.mult, ALU.bypass),
                    psr=[plr], writes=["rows"])
                P.emit("vector", lambda E: E.tensor_tensor(
                    tmp, mu, mu, ALU.mult), reads=["rows"], writes=["rows"])
                P.emit("vector", lambda E, plr=plr: E.scalar_tensor_tensor(
                    rstd, plr[:, 512:1024], 1.0 / D, tmp, ALU.mult, ALU.subtract),
                    psr=[plr], reads=["rows"], writes=["rows"])
                P.emit("scalar", lambda E: E.activation(
                    rstd, rstd, AF.Ln, bias=EPS[:, 0:1]),
                    reads=["rows", "const"], writes=["rows"])
                P.emit("scalar", lambda E: E.activation(
                    rstd, rstd, AF.Exp, scale=-0.5), reads=["rows"], writes=["rows"])
                P.emit("vector", lambda E: E.tensor_copy(ROWSB[:, 0:512], mu),
                       reads=["rows"], writes=["rowsbA"])
                P.emit("vector", lambda E: E.tensor_copy(ROWSB[:, 512:1024], rstd),
                       reads=["rows"], writes=["rowsbB"])
                P.emit("tensor", mm(pb1[:, :], ONR[:], ROWSB[:, 0:512]),
                       reads=["rowsbA", "const"], psw=[pb1])
                P.emit("tensor", mm(pb2[:, :], ONR[:], ROWSB[:, 512:1024]),
                       reads=["rowsbB", "const"], psw=[pb2])
                for t in range(8):
                    dst = H[t][:, QC * q:QC * (q + 1)]
                    if t < 4:
                        P.emit("vector", lambda E, t=t, q=q, pb1=pb1:
                               E.tensor_tensor(
                                   TMPF[:], X[t][:, QC * q:QC * (q + 1)],
                                   pb1[:, :], ALU.subtract),
                               reads=[f"X{t}"], psr=[pb1], writes=["tmpf"])
                        P.emit("vector", lambda E, pb2=pb2: E.tensor_tensor(
                            TMPF[:], TMPF[:], pb2[:, :], ALU.mult),
                            reads=["tmpf"], psr=[pb2], writes=["tmpf"])
                    else:
                        if q == 0:
                            P.emit("vector", lambda E, t=t, pb1=pb1:
                                   E.tensor_tensor(
                                       TMPF[:, 1:QC], X[t][:, 0:QC - 1],
                                       pb1[:, 1:QC], ALU.subtract),
                                   reads=[f"X{t}"], psr=[pb1], writes=["tmpf"])
                            P.emit("vector", lambda E, pb2=pb2: E.tensor_tensor(
                                TMPF[:, 1:QC], TMPF[:, 1:QC], pb2[:, 1:QC],
                                ALU.mult),
                                reads=["tmpf"], psr=[pb2], writes=["tmpf"])
                            P.emit("vector", lambda E, pb1=pb1: E.tensor_scalar(
                                TMPF[:, 0:1], pb1[:, 0:1], -1.0, None,
                                ALU.mult, ALU.bypass),
                                psr=[pb1], writes=["tmpf"])
                            P.emit("vector", lambda E, pb2=pb2: E.tensor_tensor(
                                TMPF[:, 0:1], TMPF[:, 0:1], pb2[:, 0:1],
                                ALU.mult),
                                reads=["tmpf"], psr=[pb2], writes=["tmpf"])
                        else:
                            P.emit("vector", lambda E, t=t, q=q, pb1=pb1:
                                   E.tensor_tensor(
                                       TMPF[:], X[t][:, QC * q - 1:QC * (q + 1) - 1],
                                       pb1[:, :], ALU.subtract),
                                   reads=[f"X{t}"], psr=[pb1], writes=["tmpf"])
                            P.emit("vector", lambda E, pb2=pb2: E.tensor_tensor(
                                TMPF[:], TMPF[:], pb2[:, :], ALU.mult),
                                reads=["tmpf"], psr=[pb2], writes=["tmpf"])
                    P.emit("vector", lambda E, dst=dst, t=t: E.tensor_scalar(
                        dst, TMPF[:], lgv[:, t:t + 1], lbv[:, t:t + 1],
                        ALU.mult, ALU.add),
                        reads=["tmpf", "const"], writes=[f"H{t}"])

            stats(0)
            for q in range(1, NQC):
                stats(q)
                finish(q - 1)
            finish(NQC - 1)

    def ffn_layer(it, l, stream):
        cast_H(shifted=True)
        dbg = (l == 0 and "hs" in dumps)
        if dbg:
            for t in range(8):
                P.emit("gpsimd", lambda E, t=t: E.dma_start(
                    out=dumps["hs"][t], in_=H[t][:]), reads=[f"H{t}"], sem="do")
        for pair in range(NPF):
            slot, skey = stream.use(f"qkv{l}{pair}")
            with psum(f"pj{it}{l}{pair}", [128, QC]) as pj1, \
                 psum(f"pj2{it}{l}{pair}", [128, QC]) as pj2:
                for oi, dstsel in ((0, "q"), (1, "k"), (2, "v")):
                    for q in range(NQC):
                        pt = pj1 if (oi * NQC + q) % 2 == 0 else pj2
                        fns = [mm(pt[:, :],
                                  slot[:, 384 * ic + 128 * oi:
                                       384 * ic + 128 * oi + 128],
                                  H[ic][:, QC * q:QC * (q + 1)],
                                  start=(ic == 0), stop=(ic == 7))
                               for ic in range(8)]
                        P.emit("tensor", fns, reads=[skey] + HKEYS, psw=[pt])
                        if dstsel == "q":
                            P.emit("scalar", lambda E, pt=pt, q=q: E.copy(
                                Qp[:, QC * q:QC * (q + 1)], pt[:, :]),
                                psr=[pt], writes=["Qp"])
                        elif dstsel == "k":
                            P.emit("scalar", lambda E, pt=pt, q=q: E.copy(
                                Kp[:, QC * q:QC * (q + 1)], pt[:, :]),
                                psr=[pt], writes=["Kp"])
                        else:
                            P.emit("scalar", lambda E, pt=pt, q=q: E.copy(
                                V[0:64, QC * q:QC * (q + 1)], pt[0:64, :]),
                                psr=[pt], writes=["V"])
                            P.emit("scalar", lambda E, pt=pt, q=q: E.copy(
                                V[64:128, QC * q:QC * (q + 1)], pt[64:128, :]),
                                psr=[pt], writes=["V"])
            for qki, (buf, bkey, Atab, Ctab) in enumerate(
                    [(Qp, "Qp", RA["rAq"], RA["rCq"]),
                     (Kp, "Kp", RA["rAk"], RA["rCk"])]):
                with psum(f"pl2{it}{l}{pair}{qki}", [1, 1024]) as pl2, \
                     psum(f"pbc{it}{l}{pair}{qki}", [128, QC]) as pbc, \
                     psum(f"psw{it}{l}{pair}{qki}", [128, QC]) as psw_:
                    for q in range(NQC):
                        P.emit("vector", lambda E, buf=buf, q=q: E.tensor_tensor(
                            TMPB[:], buf[:, QC * q:QC * (q + 1)],
                            buf[:, QC * q:QC * (q + 1)], ALU.mult),
                            reads=[bkey], writes=["tmpb"])
                        P.emit("tensor", [mm(pl2[:, 0:512], OCL[:, 1:2], TMPB[:])],
                               reads=["tmpb", "const"], psw=[pl2])
                        P.emit("tensor", [mm(pl2[:, 512:1024], OCL[:, 2:3], TMPB[:])],
                               reads=["tmpb", "const"], psw=[pl2])
                        for par in range(2):
                            seg = pl2[:, 512 * par:512 * par + 512]
                            dst = ROWS[:, 0:512] if par == 0 else ROWS[:, 512:1024]
                            P.emit("scalar", lambda E, seg=seg, dst=dst: E.activation(
                                dst, seg, AF.Ln, bias=EPS[:, 1:2]),
                                psr=[pl2], reads=["const"], writes=["rows"])
                            P.emit("scalar", lambda E, dst=dst: E.activation(
                                dst, dst, AF.Exp, scale=-0.5),
                                reads=["rows"], writes=["rows"])
                            P.emit("vector",
                                   lambda E, dst=dst, par=par: E.tensor_copy(
                                       ROWSB[:, 512 * par:512 * par + 512], dst),
                                   reads=["rows"], writes=["rowsb"])
                        ebi = ((l * 2 + qki) * 2) * 128
                        P.emit("tensor",
                               [mm(pbc[:, :], EBR[0:1, ebi:ebi + 128],
                                   ROWSB[:, 0:512], start=True, stop=False,
                                   skip=True),
                                mm(pbc[:, :], EBR[0:1, ebi + 128:ebi + 256],
                                   ROWSB[:, 512:1024], start=False, stop=True,
                                   skip=True)],
                               reads=["rowsb", "const"], psw=[pbc])
                        P.emit("vector", lambda E, buf=buf, q=q: E.tensor_tensor(
                            TMPB[:], buf[:, QC * q:QC * (q + 1)], pbc[:, :],
                            ALU.mult),
                            reads=[bkey], psr=[pbc], writes=["tmpb"])
                        P.emit("vector", lambda E, q=q, Atab=Atab: E.tensor_tensor(
                            TMPF[:], TMPB[:], Atab[:, QC * q:QC * (q + 1)],
                            ALU.mult),
                            reads=["tmpb", "const"], writes=["tmpf"])
                        P.emit("vector", lambda E, q=q, Ctab=Ctab: E.tensor_tensor(
                            TMPB[:], TMPB[:], Ctab[:, QC * q:QC * (q + 1)],
                            ALU.mult),
                            reads=["tmpb", "const"], writes=["tmpb"])
                        P.emit("tensor", mm(psw_[:, :], PRM[:], TMPB[:]),
                               reads=["tmpb", "const"], psw=[psw_])
                        P.emit("vector", lambda E, buf=buf, q=q: E.tensor_tensor(
                            buf[:, QC * q:QC * (q + 1)], TMPF[:], psw_[:, :],
                            ALU.add),
                            reads=["tmpf"], psr=[psw_], writes=[bkey])
            if dbg and pair == 0 and "t_qp" in dumps:
                for nm, tsr, keys in (("t_qp", Qp, ["Qp"]), ("t_kp", Kp, ["Kp"]),
                                      ("t_vv", V, ["V", "V"])):
                    P.emit("gpsimd", lambda E, nm=nm, tsr=tsr:
                           E.dma_start(out=dumps[nm][:, 0:tsr.shape[1]],
                                       in_=tsr[:]), reads=keys, sem="do")
            v_transpose(f"f{it}{l}{pair}")
            with psum(f"fsc{it}{l}{pair}", [128, 2048]) as psc, \
                 psum(f"fae{it}{l}{pair}", [65, 512]) as pavAe, \
                 psum(f"fao{it}{l}{pair}", [65, 512]) as pavAo, \
                 psum(f"fbe{it}{l}{pair}", [65, 512]) as pavBe, \
                 psum(f"fbo{it}{l}{pair}", [65, 512]) as pavBo:
                pending = []
                for w in range(4):
                    if w == 0:
                        ktl_list = [(kt, kt + 4) for kt in range(4)]
                    else:
                        ktl_list = [(4 * (w - 1) + i, i) for i in range(8)]

                    def nq_of(ktl):
                        return (0, QC)

                    def av_of(ktl):
                        if ktl <= 3:
                            return (0, 128 * (ktl + 1))
                        return (128 * (ktl - 4), QC - 128 * (ktl - 4))

                    def mask_of(ktl):
                        if ktl <= 3:
                            return (MLW, 128 * ktl)
                        return (MUP, 128 * (ktl - 4))

                    def o_dst(par, pair=pair, w=w):
                        return OFF[pair][64 * par:64 * par + 64,
                                         WIN * w:WIN * (w + 1)]

                    pe, po = (pavAe, pavAo) if w % 2 == 0 else (pavBe, pavBo)
                    pending = attn_qc(o_dst, f"OF{pair}", ktl_list, WIN * w,
                                      nq_of, NEGC[:, l:l + 1], mask_of,
                                      itq=f"f{it}{l}{pair}{w}",
                                      psc=psc, pav_e=pe, pav_o=po,
                                      pending=pending, exp_scale=QK_SCALE,
                                      av_of=av_of)
                for fn in pending:
                    fn()
        with psum(f"pwf{it}{l}", [128, QC]) as pw, \
             psum(f"pwf2{it}{l}", [128, QC]) as pw2:
            for q in range(NQC):
                for oc in range(8):
                    slot, skey = stream.use(f"wof{l}_{q}_{oc // 4}")
                    pt = pw if (oc + q) % 2 == 0 else pw2
                    c0 = 128 * (oc % 4)
                    fns = [mm(pt[:, :],
                              slot[:, 512 * ic + c0:512 * ic + c0 + 128],
                              OFF[ic][:, QC * q:QC * (q + 1)],
                              start=(ic == 0), stop=(ic == 3))
                           for ic in range(4)]
                    P.emit("tensor", fns,
                           reads=[skey] + [f"OF{i}" for i in range(4)], psw=[pt])
                    P.emit("vector", lambda E, pt=pt, oc=oc, q=q: E.tensor_tensor(
                        X[oc][:, QC * q:QC * (q + 1)],
                        X[oc][:, QC * q:QC * (q + 1)], pt[:, :], ALU.add),
                        psr=[pt], reads=[f"X{oc}"], writes=[f"X{oc}"])
        if dbg and "t_o0f" in dumps:
            P.emit("gpsimd", lambda E: E.dma_start(
                out=dumps["t_o0f"][:], in_=OFF[0][:]), reads=["OF0"], sem="do")
        if dbg:
            dump_X("xwo")
        inner_ln(it, l)
        if dbg and "hln" in dumps:
            for t in range(8):
                P.emit("gpsimd", lambda E, t=t: E.dma_start(
                    out=dumps["hln"][t], in_=H[t][:]), reads=[f"H{t}"], sem="do")
        w1_w2(it, l, stream)

    def w1_w2(it, l, stream):
        b1v = B1C[:, 2 * NCH * l:2 * NCH * (l + 1)]
        b2v = B2T[:, 8 * l:8 * l + 8]

        def glu_ap(j, wj):
            if j < 16:
                return OFF[j // 4][0:wj, 512 * (j % 4):512 * (j % 4) + 512], \
                    f"OF{j // 4}"
            return GLU[0:wj, 512 * (j - 16):512 * (j - 16) + 512], "glu"

        for q in range(NQC):
            with psum(f"pa{it}{l}{q}", [128, QC]) as pa, \
                 psum(f"pg{it}{l}{q}", [128, QC]) as pg, \
                 psum(f"pa2{it}{l}{q}", [128, QC]) as pa2, \
                 psum(f"pg2{it}{l}{q}", [128, QC]) as pg2:
                for j, wj in enumerate(CHW):
                    jj, sub = j // 2, j % 2
                    slot, skey = stream.use(f"w1_{l}_{q}_{jj}")
                    wid = 2 * CHW[2 * jj] + 2 * CHW[2 * jj + 1]
                    sb0 = 2 * CHW[2 * jj] * sub
                    pta = pa if j % 2 == 0 else pa2
                    ptg = pg if j % 2 == 0 else pg2
                    fa = [mm(pta[0:wj, :],
                             slot[:, wid * ic + sb0:wid * ic + sb0 + wj],
                             H[ic][:, QC * q:QC * (q + 1)],
                             start=(ic == 0), stop=(ic == 7)) for ic in range(8)]
                    fg = [mm(ptg[0:wj, :],
                             slot[:, wid * ic + sb0 + wj:wid * ic + sb0 + 2 * wj],
                             H[ic][:, QC * q:QC * (q + 1)],
                             start=(ic == 0), stop=(ic == 7)) for ic in range(8)]
                    P.emit("tensor", fa, reads=[skey] + HKEYS, psw=[pta])
                    P.emit("tensor", fg, reads=[skey] + HKEYS, psw=[ptg])
                    P.emit("scalar", lambda E, ptg=ptg, wj=wj, j=j: E.activation(
                        TMPB[0:wj, :], ptg[0:wj, :], AF.Gelu,
                        bias=b1v[0:wj, 2 * j + 1:2 * j + 2]),
                        psr=[ptg], reads=["const"], writes=["tmpb"])
                    P.emit("vector", lambda E, pta=pta, wj=wj, j=j: E.tensor_scalar(
                        TMPF[0:wj, :], pta[0:wj, :], b1v[0:wj, 2 * j:2 * j + 1],
                        None, ALU.add, ALU.bypass),
                        psr=[pta], reads=["const"], writes=["tmpf"])
                    gap, gkey = glu_ap(j, wj)
                    P.emit("vector", lambda E, gap=gap, wj=wj: E.tensor_tensor(
                        gap, TMPF[0:wj, :], TMPB[0:wj, :], ALU.mult),
                        reads=["tmpf", "tmpb"], writes=[gkey])
            with ExitStack() as es2:
                pws = [es2.enter_context(psum(f"pw2_{it}{l}{q}_{o}", [128, QC]))
                       for o in range(8)]
                for bi in range(8):
                    slot, skey = stream.use(f"w2_{l}_{q}_{bi}")
                    js = list(range(3 * bi, min(3 * bi + 3, NCH)))
                    for oc in range(8):
                        fns = []
                        rkeys = [skey]
                        for ji, j in enumerate(js):
                            wj = CHW[j]
                            gap, gkey = glu_ap(j, wj)
                            rkeys.append(gkey)
                            fns.append(mm(
                                pws[oc][:, :],
                                slot[0:wj, 1024 * ji + 128 * oc:
                                     1024 * ji + 128 * oc + 128],
                                gap,
                                start=(bi == 0 and ji == 0), stop=False,
                                skip=True))
                        P.emit("tensor", fns, reads=rkeys, psw=[pws[oc]])
                for oc in range(8):
                    P.emit("vector", lambda E, oc=oc, q=q: E.scalar_tensor_tensor(
                        X[oc][:, QC * q:QC * (q + 1)], pws[oc][:, :],
                        b2v[:, oc:oc + 1], X[oc][:, QC * q:QC * (q + 1)],
                        ALU.add, ALU.add),
                        psr=[pws[oc]], reads=["const", f"X{oc}"], writes=[f"X{oc}"])

    def finale(it):
        for t in range(8):
            P.emit("sync", lambda E, t=t: E.dma_start(out=STG[:], in_=d["x1sp"][t]),
                   writes=STGK, sem="dx")
            P.emit("vector", lambda E, t=t: E.tensor_tensor(
                X[t][:], X[t][:], STG[:], ALU.add),
                reads=[f"X{t}"] + STGK, writes=[f"X{t}"])
        module_ln(it, "fin", spill=False)
        stg2 = STG[:].rearrange("p (f c) -> p f c", f=2)
        with psum(f"pot{it}", [128, 512]) as pt0, \
             psum(f"pot2{it}", [128, 512]) as pt1:
            for st in range(NKT):
                half = st % 2
                hk = STGK[half]
                for cg in range(2):
                    pt = pt0 if cg == 0 else pt1
                    fns = [lambda E, ct=4 * cg + ci, ci=ci, st=st, pt=pt:
                           E.transpose(pt[:, 128 * ci:128 * (ci + 1)],
                                       X[ct][:, 128 * st:128 * (st + 1)], IDf[:])
                           for ci in range(4)]
                    P.emit("tensor", fns,
                           reads=[f"X{4 * cg + i}" for i in range(4)] + ["const"],
                           psw=[pt])
                    P.emit("vector", lambda E, pt=pt, half=half, cg=cg:
                           E.tensor_copy(stg2[:, half, 512 * cg:512 * (cg + 1)],
                                         pt[:, :]),
                           psr=[pt], writes=[hk])
                P.emit("sync", lambda E, st=st, half=half, it=it: E.dma_start(
                    out=d["out"][it, 128 * st:128 * (st + 1), :],
                    in_=stg2[:, half, :]),
                    reads=[hk], writes=[], sem="do")

    def dump_X(nm):
        if nm not in dumps:
            return
        for t in range(8):
            P.emit("sync", lambda E, t=t, nm=nm: E.dma_start(
                out=dumps[nm][t], in_=X[t][:]),
                reads=[f"X{t}"], sem="do")

    def convert_w1w2():
        ci = [0]

        def conv(src_ap, dst_ap, nt, no, wkey):
            slot = S0 if ci[0] % 2 == 0 else S1
            skey = f"slot{ci[0] % 2}"
            ci[0] += 1
            sv = slot[:, 0:nt * no].rearrange("p (t o) -> p t o", o=no)
            P.emit("gpsimd", lambda E: E.dma_start(
                out=sv, in_=src_ap), writes=[skey], sem="dw")
            P.emit("sync", lambda E: E.dma_start(
                out=dst_ap, in_=sv), reads=[skey],
                writes=[wkey], sem="dx")
        for l in range(2):
            for c in range(15):
                c0 = 364 * c
                conv(d["w1_pm"][l][:, c0:c0 + 364].rearrange(
                        "(t p) o -> p t o", p=128),
                     d["w1b"][l][:, c0:c0 + 364].rearrange(
                        "(t p) o -> p t o", p=128),
                     8, 364, f"w1b{l}")
            for c in range(7):
                r0 = 384 * c
                conv(d["w2_n"][l][r0:r0 + 384, :].rearrange(
                        "(c p) o -> p c o", p=128),
                     d["w2b"][l][r0:r0 + 384, :].rearrange(
                        "(c p) o -> p c o", p=128),
                     3, 1024, f"w2b{l}")
            slot = S0 if ci[0] % 2 == 0 else S1
            skey = f"slot{ci[0] % 2}"
            ci[0] += 1
            P.emit("gpsimd", lambda E, slot=slot, l=l: E.dma_start(
                out=slot[0:42, 0:1024], in_=d["w2_n"][l][2688:2730, :]),
                writes=[skey], sem="dw")
            P.emit("sync", lambda E, slot=slot, l=l: E.dma_start(
                out=d["w2b"][l][2688:2730, :], in_=slot[0:42, 0:1024]),
                reads=[skey], writes=[f"w2b{l}"], sem="dx")

    load_consts()
    convert_w1w2()
    calc_negc()
    for it in range(n_items):
        stream = Stream()
        for q in range(NQC):
            for ch in range(4):
                stream.push(f"wo{q}_{ch}", lambda E, slot, ch=ch: E.dma_start(
                    out=slot[:, 0:2048].rearrange("p (t o) -> p t o", o=256),
                    in_=d["wo_g"][:, 256 * ch:256 * (ch + 1)].rearrange(
                        "(t p) o -> p t o", p=128)))
        for l in range(2):
            for pair in range(NPF):
                stream.push(f"qkv{l}{pair}",
                            lambda E, slot, l=l, pair=pair: E.dma_start(
                                out=slot[:, 0:8 * 384].rearrange(
                                    "p (t o) -> p t o", o=384),
                                in_=d["wqkv_pm"][l, pair].rearrange(
                                    "(t p) o -> p t o", p=128)))
            for q in range(NQC):
                for hh in range(2):
                    stream.push(f"wof{l}_{q}_{hh}",
                                lambda E, slot, l=l, hh=hh: E.dma_start(
                                    out=slot[:, 0:2048].rearrange(
                                        "p (t o) -> p t o", o=512),
                                    in_=d["wo_f"][l][:, 512 * hh:512 * (hh + 1)]
                                    .rearrange("(t p) o -> p t o", p=128)))
            for q in range(NQC):
                for jj in range(11):
                    j0, j1 = 2 * jj, 2 * jj + 1
                    o0 = _W1_PAIR_OFF[j0]
                    wid = 2 * CHW[j0] + 2 * CHW[j1]
                    stream.push(f"w1_{l}_{q}_{jj}",
                                lambda E, slot, l=l, o0=o0, wid=wid: E.dma_start(
                                    out=slot[:, 0:8 * wid].rearrange(
                                        "p (t o) -> p t o", o=wid),
                                    in_=d["w1b"][l][:, o0:o0 + wid].rearrange(
                                        "(t p) o -> p t o", p=128)))
                for bi in range(7):
                    r0 = 384 * bi
                    stream.push(f"w2_{l}_{q}_{bi}",
                                lambda E, slot, l=l, r0=r0: E.dma_start(
                                    out=slot[:, 0:3072].rearrange(
                                        "p (c o) -> p c o", o=1024),
                                    in_=d["w2b"][l][r0:r0 + 384, :].rearrange(
                                        "(c p) o -> p c o", p=128)))
                stream.push(f"w2_{l}_{q}_7", lambda E, slot, l=l: E.dma_start(
                    out=slot[0:42, 0:1024], in_=d["w2b"][l][2688:2730, :]))
        stream.issue_upto(2)
        phase_in(it)
        cast_H(shifted=False)
        if dev_stage == "gpair0":
            global_pair(it, 0)
            for nm, tsr, keys in (("t_qp", Qp, ["Qp"]), ("t_kp", Kp, ["Kp"]),
                                  ("t_vv", V, ["V", "V"]),
                                  ("t_vte", VTe, ["VT0"]),
                                  ("t_o0", H[0], ["H0"]),
                                  ("t_x0", X[0], ["X0"])):
                if nm in dumps:
                    w = tsr.shape[1]
                    P.emit("gpsimd", lambda E, a=dumps[nm][:, 0:w], b=tsr[:]:
                           E.dma_start(out=a, in_=b), reads=keys, sem="do")
            break
        for pair in range(NPG):
            global_pair(it, pair)
        if dev_stage == "gattn":
            for t in range(8):
                if "o_all" in dumps:
                    P.emit("gpsimd", lambda E, t=t: E.dma_start(
                        out=dumps["o_all"][t], in_=H[t][:]),
                        reads=[f"H{t}"], sem="do")
        wo_global(it, stream)
        if dev_stage == "gattn":
            dump_X("post_attn")
            break
        dump_X("post_attn" if False else "post_attn")
        module_ln(it, "l1", spill=True)
        dump_X("post_ln1")
        for l in range(2):
            ffn_layer(it, l, stream)
            dump_X(f"post_ffn{l}")
        finale(it)
    P._wait("sync", "do", P.cnt["do"])
    P.replay(block)
    es.close()
    return nc, P, dumps


def kernel(**inputs):
    x = np.ascontiguousarray(inputs["L0_attn_x"], dtype=np.float32)
    shared = {}
    for k, v in _host_consts().items():
        shared[k] = np.ascontiguousarray(v, np.float32)
    for k, v in _layout_weights(inputs).items():
        shared[k] = np.ascontiguousarray(v, np.float32)
    nc, _, _ = build_program()
    in_maps = []
    for c in range(NCORES):
        m = dict(shared)
        m["x"] = np.ascontiguousarray(x[IPC * c:IPC * (c + 1)])
        in_maps.append(m)
    res = run_bass_kernel_spmd(nc, in_maps, core_ids=list(range(NCORES)))
    return np.concatenate([r["out"] for r in res.results], axis=0)


# version 15 (final)
# speedup vs baseline: 1.1729x; 1.1729x over previous
import numpy as np
from contextlib import ExitStack

import concourse.bass as bass
import concourse.mybir as mybir
from concourse.bass_utils import run_bass_kernel_spmd

F32 = mybir.dt.float32
BF16 = mybir.dt.bfloat16
AF = mybir.ActivationFunctionType
ALU = mybir.AluOpType
AX = mybir.AxisListType

B, T, D = 32, 2048, 1024
NCORES = 8
IPC = B // NCORES
NPG = 8
NPF = 4
NKT = T // 128
QC = 512
NQC = T // QC
HID = 2730
HID2 = 5460
WIN = 512
QK_SCALE = 8.0
CHW = [128] * 21 + [42]
NCH = len(CHW)

_W1_PAIR_OFF = []
_off = 0
for _w in CHW:
    _W1_PAIR_OFF.append(_off)
    _off += 2 * _w
assert _off == HID2


def _rotary_tables():
    half = np.arange(0, 64, 2, dtype=np.float64)
    inv_freq = 1.0 / (10000.0 ** (half / 64.0))
    t = np.arange(T, dtype=np.float64)
    freqs = t[:, None] * inv_freq[None, :]
    freqs = np.concatenate([freqs, freqs], axis=1)
    base = (half + 0.4 * 64.0) / (1.4 * 64.0)
    power = (t - T // 2) / float(WIN // 2)
    scale = base[None, :] ** power[:, None]
    scale = np.concatenate([scale, scale], axis=1)
    cos, sin = np.cos(freqs), np.sin(freqs)
    Aq = (cos * scale).T
    Sq = (sin * scale).T
    Ak = (cos / scale).T
    Sk = (sin / scale).T

    def mkC(S):
        C = np.empty_like(S)
        C[0:32] = S[32:64]
        C[32:64] = -S[0:32]
        return C

    def dup(a):
        return np.concatenate([a, a], axis=0).astype(np.float32)

    return dup(Aq), dup(mkC(Sq)), dup(Ak), dup(mkC(Sk))


def _host_consts():
    c = {}
    c["ident"] = np.eye(128, dtype=np.float32)
    id2 = np.zeros((128, 64), np.float32)
    id2[0:64] = np.eye(64)
    id2[64:128] = np.eye(64)
    c["ident2"] = id2
    P = np.zeros((128, 128), np.float32)
    for p in range(128):
        blk = (p // 64) * 64
        P[blk + ((p - blk + 32) % 64), p] = 1.0
    c["perm"] = P
    c["rAq"], c["rCq"], c["rAk"], c["rCk"] = _rotary_tables()
    i = np.arange(128)
    c["mlow"] = (i[None, :] <= i[:, None]).astype(np.float32)
    c["mup"] = (i[None, :] >= i[:, None]).astype(np.float32)
    c["ones_row"] = np.ones((1, 128), np.float32)
    c["negones_row"] = -np.ones((1, 128), np.float32)
    oc = np.ones((128, 3), np.float32)
    oc[64:, 1] = 0.0
    oc[:64, 2] = 0.0
    c["onescols"] = oc
    c["epscols"] = np.array([[1e-5, 1e-24]], np.float32)
    return c


def _w1_cols():
    order = []
    off = 0
    for w in CHW:
        order.append(("a", off, w))
        order.append(("g", HID + off, w))
        off += w
    cols = []
    for _, start, w in order:
        cols.extend(range(start, start + w))
    return np.array(cols, np.int64)


_W1_COLS = _w1_cols()


def _layout_weights(inp):
    L = {}
    Wq, Wk, Wv = inp["L0_attn_Wq"], inp["L0_attn_Wk"], inp["L0_attn_Wv"]
    WQK = np.zeros((128, NPG, 128), np.float32)
    WV = np.zeros((128, NPG, 64), np.float32)
    for h in range(16):
        par, p = h % 2, h // 2
        WQK[64 * par:64 * par + 64, p, 0:64] = Wq[h]
        WQK[64 * par:64 * par + 64, p, 64:128] = Wk[h]
        WV[64 * par:64 * par + 64, p, :] = Wv[h]
    L["wqk_g"] = WQK.reshape(128, NPG * 128)
    L["wv_g"] = WV.reshape(128, NPG * 64)
    for nm, arr in (("bq", inp["L0_attn_bq"]), ("bk", inp["L0_attn_bk"]),
                    ("bv", inp["L0_attn_bv"])):
        t = np.zeros((128, 16), np.float32)
        t[0:64] = arr.T
        t[64:128] = arr.T
        L[nm] = t
    L["wo_g"] = inp["L0_attn_Wo"]
    L["bo_t"] = np.ascontiguousarray(inp["L0_attn_bo"].reshape(8, 128).T)
    wqkv = inp["L0_ffn_wqkv"]
    pm = np.zeros((2, NPF, 1024, 384), np.float32)
    for l in range(2):
        for p in range(NPF):
            pm[l, p, :, 0:128] = wqkv[l][:, 128 * p:128 * p + 128]
            pm[l, p, :, 128:256] = wqkv[l][:, 512 + 128 * p:512 + 128 * p + 128]
            pm[l, p, :, 256:384] = wqkv[l][:, 1024 + 128 * p:1024 + 128 * p + 128]
    L["wqkv_pm"] = pm
    L["wo_f"] = inp["L0_ffn_wo"]
    L["w1_pm"] = np.ascontiguousarray(inp["L0_ffn_w1"][:, :, _W1_COLS])
    b1 = inp["L0_ffn_b1"][:, _W1_COLS]
    b1c = np.zeros((2, 128, 2 * NCH), np.float32)
    for l in range(2):
        off = 0
        for j, w in enumerate(CHW):
            b1c[l, 0:w, 2 * j] = b1[l, off:off + w]
            b1c[l, 0:w, 2 * j + 1] = b1[l, off + w:off + 2 * w]
            off += 2 * w
    L["b1c"] = b1c
    L["w2_n"] = inp["L0_ffn_w2"]
    L["b2_t"] = np.stack([np.ascontiguousarray(inp["L0_ffn_b2"][l].reshape(8, 128).T)
                          for l in range(2)])
    L["lg_t"] = np.stack([np.ascontiguousarray(inp["L0_ffn_ln_g"][l].reshape(8, 128).T)
                          for l in range(2)])
    L["lb_t"] = np.stack([np.ascontiguousarray(inp["L0_ffn_ln_b"][l].reshape(8, 128).T)
                          for l in range(2)])
    ebr = np.zeros((1, 8 * 128), np.float32)
    qs, ks = inp["L0_ffn_q_scale"], inp["L0_ffn_k_scale"]
    for l in range(2):
        for qi, arr in ((0, qs[l]), (1, ks[l])):
            for par in range(2):
                i = ((l * 2 + qi) * 2 + par) * 128
                ebr[0, i + 64 * par:i + 64 * par + 64] = arr
    L["ebr"] = ebr
    qr = np.zeros((1, 256), np.float32)
    qr[0, 0:64] = qs[0]; qr[0, 64:128] = ks[0]
    qr[0, 128:192] = qs[1]; qr[0, 192:256] = ks[1]
    L["qkrows"] = qr
    return L


class Prog:
    CENG = ("tensor", "vector", "scalar", "gpsimd", "sync")

    SAME_ENGINE_SYNC = True
    SAME_ENGINE_RAW_SYNC = False

    def __init__(self, nc):
        self.nc = nc
        self.ops = {e: [] for e in self.CENG}
        self.sem = {}
        self.cnt = {}
        self.lastwait = {e: {} for e in self.CENG}
        self.bufw = {}
        self.bufr = {}
        self.psum_banks = {}
        self.nops = 0

    def add_sem(self, name, handle):
        self.sem[name] = handle
        self.cnt[name] = 0

    @staticmethod
    def _esem(eng):
        return {"tensor": "sT", "vector": "sV", "scalar": "sS",
                "gpsimd": "sP", "sync": "sY"}[eng]

    def _wait(self, eng, semname, val):
        if val <= 0:
            return
        if self.lastwait[eng].get(semname, 0) >= val:
            return
        self.lastwait[eng][semname] = val
        sem = self.sem[semname]
        self.ops[eng].append(lambda E, s=sem, v=val: E.wait_ge(s, v))
        self.nops += 1

    def bank_keys(self, t):
        nm = t.name
        if nm not in self.psum_banks:
            m = self.nc.lookup_mloc(t)
            bpp = int(np.prod(t.shape[1:])) * (4 if t.dtype == F32 else 2)
            nb = (bpp + 2047) // 2048
            self.psum_banks[nm] = tuple(f"bank{m.bank + i}" for i in range(nb))
        return self.psum_banks[nm]

    def emit(self, eng, fns, reads=(), writes=(), sem=None, inc=None, psr=(),
             psw=()):
        if callable(fns):
            fns = [fns]
        semname = sem or self._esem(eng)
        amount = inc if inc is not None else (16 if semname in ("dx", "dw", "do") else 1)
        reads = list(reads)
        writes = list(writes)
        for t in psr:
            reads.extend(self.bank_keys(t))
        for t in psw:
            writes.extend(self.bank_keys(t))
        own = None if self.SAME_ENGINE_SYNC else self._esem(eng)
        rown = None if self.SAME_ENGINE_RAW_SYNC else self._esem(eng)
        for k in reads:
            w = self.bufw.get(k)
            if w is not None and w[0] != rown:
                self._wait(eng, w[0], w[1])
            if k.startswith("bank"):
                for s, v in self.bufr.get(k, {}).items():
                    if s != semname:
                        self._wait(eng, s, v)
        for k in writes:
            w = self.bufw.get(k)
            if w is not None and w[0] != own:
                self._wait(eng, w[0], w[1])
            for s, v in self.bufr.get(k, {}).items():
                if s != own:
                    self._wait(eng, s, v)
        sem_h = self.sem[semname]
        n = len(fns)
        for i, fn in enumerate(fns):
            if i == n - 1:
                self.ops[eng].append(
                    lambda E, f=fn, s=sem_h, a=amount: f(E).then_inc(s, a))
            else:
                self.ops[eng].append(lambda E, f=fn: f(E))
        self.nops += n
        self.cnt[semname] += amount
        val = self.cnt[semname]
        for k in writes:
            self.bufw[k] = (semname, val)
            self.bufr[k] = {}
        for k in reads:
            self.bufr.setdefault(k, {})[semname] = val
        return semname, val

    def replay(self, block):
        for eng in self.CENG:
            oplist = self.ops[eng]

            def run(E, oplist=oplist):
                for f in oplist:
                    f(E)
            getattr(block, eng)(run)


HKEYS = [f"H{t}" for t in range(8)]


def build_program(n_items=IPC, dump=(), dev_stage=None):
    nc = bass.Bass("TRN2", target_bir_lowering=False)
    es = ExitStack()
    Et = es.enter_context

    d = {}
    d["x"] = nc.dram_tensor("x", [IPC, T, D], F32, kind="ExternalInput").ap()
    d["out"] = nc.dram_tensor("out", [IPC, T, D], F32, kind="ExternalOutput").ap()
    for nm, shape in [
        ("ident", [128, 128]), ("ident2", [128, 64]), ("perm", [128, 128]),
        ("rAq", [128, T]), ("rCq", [128, T]), ("rAk", [128, T]), ("rCk", [128, T]),
        ("mlow", [128, 128]), ("mup", [128, 128]),
        ("ones_row", [1, 128]), ("negones_row", [1, 128]), ("onescols", [128, 3]),
        ("wqk_g", [128, NPG * 128]), ("wv_g", [128, NPG * 64]),
        ("bq", [128, 16]), ("bk", [128, 16]), ("bv", [128, 16]),
        ("wo_g", [1024, 1024]), ("bo_t", [128, 8]),
        ("wqkv_pm", [2, NPF, 1024, 384]), ("wo_f", [2, 512, 1024]),
        ("w1_pm", [2, 1024, HID2]), ("b1c", [2, 128, 2 * NCH]),
        ("w2_n", [2, HID, 1024]), ("b2_t", [2, 128, 8]),
        ("lg_t", [2, 128, 8]), ("lb_t", [2, 128, 8]),
        ("ebr", [1, 8 * 128]), ("qkrows", [1, 256]), ("epscols", [1, 2]),
    ]:
        d[nm] = nc.dram_tensor(nm, shape, F32, kind="ExternalInput").ap()
    d["x1sp"] = nc.dram_tensor("x1sp", [8, 128, T], F32, kind="Internal").ap()
    d["w1b"] = nc.dram_tensor("w1b", [2, 1024, HID2], BF16, kind="Internal").ap()
    d["w2b"] = nc.dram_tensor("w2b", [2, HID, 1024], BF16, kind="Internal").ap()
    dumps = {}
    for nm in dump:
        shp = [128, T] if nm.startswith("t_") else [8, 128, T]
        dumps[nm] = nc.dram_tensor("dump_" + nm, shp, F32,
                                   kind="ExternalOutput").ap()

    X = [Et(nc.sbuf_tensor(f"X{t}", [128, T], F32)) for t in range(8)]
    H = [Et(nc.sbuf_tensor(f"H{t}", [128, T], BF16)) for t in range(8)]
    OFF = [Et(nc.sbuf_tensor(f"OF{t}", [128, T], BF16)) for t in range(4)]
    STG = Et(nc.sbuf_tensor("STG", [128, T], F32))
    Qp = Et(nc.sbuf_tensor("Qp", [128, T], BF16))
    Kp = Et(nc.sbuf_tensor("Kp", [128, T], BF16))
    V = Et(nc.sbuf_tensor("V", [128, T], BF16))
    VTe = Et(nc.sbuf_tensor("VTe", [128, NKT * 72], BF16))
    VTo = Et(nc.sbuf_tensor("VTo", [128, NKT * 72], BF16))
    EX = Et(nc.sbuf_tensor("EX", [128, 2048], BF16))
    GLU = Et(nc.sbuf_tensor("GLU", [128, 6 * 512], BF16))
    S0 = Et(nc.sbuf_tensor("S0", [128, 4096], BF16))
    S1 = Et(nc.sbuf_tensor("S1", [128, 4096], BF16))
    RA = {nm: Et(nc.sbuf_tensor("t" + nm, [128, T], BF16))
          for nm in ("rAq", "rCq", "rAk", "rCk")}
    IDf = Et(nc.sbuf_tensor("IDf", [128, 128], F32))
    ID2 = Et(nc.sbuf_tensor("ID2", [128, 64], BF16))
    PRM = Et(nc.sbuf_tensor("PRM", [128, 128], BF16))
    MLW = Et(nc.sbuf_tensor("MLW", [128, 128], BF16))
    MUP = Et(nc.sbuf_tensor("MUP", [128, 128], BF16))
    ONR = Et(nc.sbuf_tensor("ONR", [1, 128], BF16))
    ONRF = Et(nc.sbuf_tensor("ONRF", [1, 128], F32))
    NONR = Et(nc.sbuf_tensor("NONR", [1, 128], BF16))
    OCL = Et(nc.sbuf_tensor("OCL", [128, 3], BF16))
    OCLF = Et(nc.sbuf_tensor("OCLF", [128, 3], F32))
    WQKG = Et(nc.sbuf_tensor("WQKG", [128, NPG * 128], BF16))
    WVG = Et(nc.sbuf_tensor("WVG", [128, NPG * 64], BF16))
    BQ = Et(nc.sbuf_tensor("BQ", [128, 16], F32))
    BK = Et(nc.sbuf_tensor("BK", [128, 16], F32))
    BV = Et(nc.sbuf_tensor("BV", [128, 16], F32))
    BO = Et(nc.sbuf_tensor("BO", [128, 8], F32))
    B1C = Et(nc.sbuf_tensor("B1C", [128, 4 * NCH], F32))
    B2T = Et(nc.sbuf_tensor("B2T", [128, 16], F32))
    LGT = Et(nc.sbuf_tensor("LGT", [128, 16], F32))
    LBT = Et(nc.sbuf_tensor("LBT", [128, 16], F32))
    EBR = Et(nc.sbuf_tensor("EBR", [1, 8 * 128], BF16))
    QKR = Et(nc.sbuf_tensor("QKR", [1, 256], F32))
    EPS = Et(nc.sbuf_tensor("EPS", [1, 2], F32))
    ROWS = Et(nc.sbuf_tensor("ROWS", [1, 4 * QC], F32))
    RRF = Et(nc.sbuf_tensor("RRF", [1, 2 * QC], F32))
    ROWSB = Et(nc.sbuf_tensor("ROWSB", [1, 2 * QC], BF16))
    ROWX = Et(nc.sbuf_tensor("ROWX", [1, 16], F32))
    ROWXB = Et(nc.sbuf_tensor("ROWXB", [1, 16], BF16))
    NEGC = Et(nc.sbuf_tensor("NEGC", [128, 2], F32))
    TMPF = Et(nc.sbuf_tensor("TMPF", [128, QC], F32))
    TMPF2 = Et(nc.sbuf_tensor("TMPF2", [128, QC], F32))
    TMPB = Et(nc.sbuf_tensor("TMPB", [128, QC], BF16))

    sems = {}
    for s in ("sT", "sV", "sS", "sP", "sY", "dx", "dw", "do"):
        sems[s] = Et(nc.semaphore(s))

    block = Et(nc.Block())
    P = Prog(nc)
    for s, h in sems.items():
        P.add_sem(s, h)

    def mm(out_ap, lhsT_ap, rhs_ap, start=True, stop=True, skip=False):
        return lambda E: E.matmul(out_ap, lhsT_ap, rhs_ap, start=start,
                                  stop=stop, skip_group_check=skip)

    def psum(name, shape, dtype=F32):
        return nc.psum_tensor(name, shape, dtype)

    STGK = ["stgA", "stgB"]

    def load_consts():
        for dst, src in [(IDf[:], d["ident"][:]), (BQ[:], d["bq"][:]),
                         (BK[:], d["bk"][:]), (BV[:], d["bv"][:]),
                         (BO[:], d["bo_t"][:]),
                         (B1C[:, 0:2 * NCH], d["b1c"][0]),
                         (B1C[:, 2 * NCH:4 * NCH], d["b1c"][1]),
                         (B2T[:, 0:8], d["b2_t"][0]), (B2T[:, 8:16], d["b2_t"][1]),
                         (LGT[:, 0:8], d["lg_t"][0]), (LGT[:, 8:16], d["lg_t"][1]),
                         (LBT[:, 0:8], d["lb_t"][0]), (LBT[:, 8:16], d["lb_t"][1]),
                         (QKR[:], d["qkrows"][:]), (EPS[:], d["epscols"][:]),
                         (OCLF[:], d["onescols"][:]),
                         (ONRF[:], d["ones_row"][:])]:
            P.emit("sync", lambda E, a=dst, b=src: E.dma_start(out=a, in_=b),
                   writes=["const"], sem="dx")
        for dst, src in [(ID2[:], d["ident2"][:]), (PRM[:], d["perm"][:]),
                         (MLW[:], d["mlow"][:]), (MUP[:], d["mup"][:]),
                         (ONR[:], d["ones_row"][:]), (NONR[:], d["negones_row"][:]),
                         (OCL[:], d["onescols"][:]),
                         (WQKG[:], d["wqk_g"][:]), (WVG[:], d["wv_g"][:]),
                         (EBR[:], d["ebr"][:]),
                         (RA["rAq"][:], d["rAq"][:]), (RA["rCq"][:], d["rCq"][:]),
                         (RA["rAk"][:], d["rAk"][:]), (RA["rCk"][:], d["rCk"][:])]:
            P.emit("gpsimd", lambda E, a=dst, b=src: E.dma_start(out=a, in_=b),
                   writes=["const"], sem="dw")
        vte3 = VTe[:].rearrange("p (k c) -> p k c", c=72)
        vto3 = VTo[:].rearrange("p (k c) -> p k c", c=72)
        P.emit("vector", lambda E: E.memset(vte3[:, :, 64:65], 1.0),
               writes=["VT0_ones"])
        P.emit("vector", lambda E: E.memset(vto3[:, :, 64:65], 1.0),
               writes=["VT1_ones"])

    def calc_negc():
        with psum("pcng", [128, 512]) as pc:
            for l in range(2):
                P.emit("vector", lambda E, l=l: E.reduce_max(
                    ROWX[:, 4 * l:4 * l + 1], QKR[:, 128 * l:128 * l + 64],
                    axis=AX.X, apply_absolute_value=True),
                    reads=["const"], writes=["rowx"])
                P.emit("vector", lambda E, l=l: E.reduce_max(
                    ROWX[:, 4 * l + 1:4 * l + 2], QKR[:, 128 * l + 64:128 * l + 128],
                    axis=AX.X, apply_absolute_value=True),
                    reads=["const"], writes=["rowx"])
                P.emit("vector", lambda E, l=l: E.tensor_tensor(
                    ROWX[:, 4 * l + 2:4 * l + 3], ROWX[:, 4 * l:4 * l + 1],
                    ROWX[:, 4 * l + 1:4 * l + 2], ALU.mult),
                    reads=["rowx"], writes=["rowx"])
                P.emit("vector", lambda E, l=l: E.tensor_scalar(
                    ROWXB[:, l:l + 1], ROWX[:, 4 * l + 2:4 * l + 3],
                    QK_SCALE, None, ALU.mult, ALU.bypass),
                    reads=["rowx"], writes=["rowxb"])
                P.emit("tensor", mm(pc[:, 0:1], NONR[:], ROWXB[:, l:l + 1]),
                       reads=["rowxb", "const"], psw=[pc])
                P.emit("vector", lambda E, l=l: E.tensor_copy(
                    NEGC[:, l:l + 1], pc[:, 0:1]),
                    psr=[pc], writes=["negc"])

    def phase_in(it):
        stg2 = STG[:].rearrange("p (f c) -> p f c", f=2)
        with psum(f"ptri0_{it}", [128, 512]) as pt0, \
             psum(f"ptri1_{it}", [128, 512]) as pt1:
            for st in range(16):
                half = st % 2
                hk = STGK[half]
                src_ap = d["x"][it, 128 * st:128 * (st + 1), :]
                P.emit("sync", lambda E, b=src_ap, half=half: E.dma_start(
                    out=stg2[:, half, :], in_=b), writes=[hk], sem="dx")
                for cg in range(2):
                    pt = pt0 if cg == 0 else pt1
                    fns = [lambda E, ct=4 * cg + ci, half=half, pt=pt, ci=ci:
                           E.transpose(pt[:, 128 * ci:128 * (ci + 1)],
                                       stg2[:, half, 128 * ct:128 * ct + 128],
                                       IDf[:])
                           for ci in range(4)]
                    P.emit("tensor", fns, reads=[hk, "const"], psw=[pt])
                    for ci in range(4):
                        ct = 4 * cg + ci
                        P.emit("vector", lambda E, ct=ct, st=st, pt=pt, ci=ci:
                               E.tensor_copy(X[ct][:, 128 * st:128 * (st + 1)],
                                             pt[:, 128 * ci:128 * (ci + 1)]),
                               psr=[pt], writes=[f"X{ct}"])

    def cast_H(shifted):
        for t in range(8):
            if not shifted or t < 4:
                P.emit("vector", lambda E, t=t: E.tensor_copy(H[t][:], X[t][:]),
                       reads=[f"X{t}"], writes=[f"H{t}"])
            else:
                P.emit("vector", lambda E, t=t: E.tensor_copy(
                    H[t][:, 1:T], X[t][:, 0:T - 1]),
                    reads=[f"X{t}"], writes=[f"H{t}"])
                P.emit("vector", lambda E, t=t: E.memset(H[t][:, 0:1], 0.0),
                       writes=[f"H{t}"])

    def v_transpose(tag):
        vte3 = VTe[:].rearrange("p (k c) -> p k c", c=72)
        vto3 = VTo[:].rearrange("p (k c) -> p k c", c=72)
        with psum("ptv" + tag, [128, 1024], BF16) as ptv:
            for par, vdst3 in enumerate([vte3, vto3]):
                fns = []
                for kt in range(NKT):
                    if par == 0:
                        fns.append(lambda E, kt=kt: E.transpose(
                            ptv[:, 64 * kt:64 * (kt + 1)],
                            V[0:64, 128 * kt:128 * (kt + 1)], ID2[0:64, :]))
                    else:
                        fns.append(lambda E, kt=kt: E.transpose(
                            ptv[:, 64 * kt:64 * (kt + 1)],
                            V[64:128, 128 * kt:128 * (kt + 1)], ID2[64:128, :]))
                P.emit("tensor", fns, reads=["V", "const"], psw=[ptv])
                P.emit("vector", lambda E, v3=vdst3: E.tensor_copy(
                    v3[:, :, 0:64], ptv[:].rearrange("p (k c) -> p k c", c=64)),
                    psr=[ptv], writes=[f"VT{par}"])

    F32R = mybir.dt.float32r

    def attn_qc(o_dst, o_key, ktl_list, q0, nq_of, negc_ap, mask_of, itq,
                psc, pav_e, pav_o, pending, exp_scale=1.0, av_of=None):
        if av_of is None:
            av_of = nq_of
        av_started = [False, False]
        ngroups = 0
        for gi in range(0, len(ktl_list), 2):
            grp = ktl_list[gi:gi + 2]
            offs = []
            o = 0
            for kt, ktl in grp:
                qs, nv = nq_of(ktl)
                offs.append((kt, ktl, qs, nv, o))
                o += nv
            for par in range(2):
                b0 = 64 * par
                pskey = f"psc{par}"
                fns = [mm(psc[:, 1024 * par + oo:1024 * par + oo + nv],
                          Kp[b0:b0 + 64, 128 * kt:128 * (kt + 1)],
                          Qp[b0:b0 + 64, q0 + qs:q0 + qs + nv])
                       for kt, ktl, qs, nv, oo in offs]
                P.emit("tensor", fns, reads=["Qp", "Kp"],
                       writes=[P.bank_keys(psc)[2 * par],
                               P.bank_keys(psc)[2 * par + 1]])
                if negc_ap is not None:
                    fn = lambda E, par=par, o=o: E.activation(
                        EX[:, 1024 * par:1024 * par + o],
                        psc[:, 1024 * par:1024 * par + o], AF.Exp,
                        bias=negc_ap, scale=exp_scale)
                else:
                    fn = lambda E, par=par, o=o: E.activation(
                        EX[:, 1024 * par:1024 * par + o],
                        psc[:, 1024 * par:1024 * par + o], AF.Exp,
                        scale=exp_scale)
                P.emit("scalar", [fn], reads=["negc",
                                              P.bank_keys(psc)[2 * par],
                                              P.bank_keys(psc)[2 * par + 1]],
                       writes=[f"EX{par}"])
            for par in range(2):
                for kt, ktl, qs, nv, oo in offs:
                    mk = mask_of(ktl)
                    if mk is None:
                        continue
                    Mt, qb = mk
                    bo = 1024 * par + oo + (qb - qs)
                    P.emit("vector",
                           lambda E, bo=bo, Mt=Mt: E.tensor_tensor(
                               EX[:, bo:bo + 128], EX[:, bo:bo + 128],
                               Mt[:], ALU.mult),
                           reads=[f"EX{par}", "const"], writes=[f"EX{par}"])
            for par, pav, vt, vtk in ((0, pav_e, VTe, "VT0"),
                                      (1, pav_o, VTo, "VT1")):
                fns = []
                for kt, ktl, qs, nv, oo in offs:
                    st = not av_started[par]
                    av_started[par] = True
                    aqs, anv = av_of(ktl)
                    eo = 1024 * par + oo + aqs - qs
                    fns.append(mm(pav[0:65, aqs:aqs + anv],
                                  vt[:, 72 * kt:72 * kt + 65],
                                  EX[:, eo:eo + anv],
                                  start=st, stop=False, skip=True))
                P.emit("tensor", fns, reads=[f"EX{par}", vtk], psw=[pav])
            ngroups += 1
            if ngroups == 4 and pending:
                for fn in pending:
                    fn()
                pending = []
        for fn in pending:
            fn()

        def mk_combine(par, pav):
            tav = TMPF[0:65, :] if par == 0 else TMPF2[0:65, :]
            tkey = "tmpf" if par == 0 else "tmpf2"
            rrb = ROWSB[:, 0:QC] if par == 0 else ROWSB[:, QC:2 * QC]
            rbk = "rowsbA" if par == 0 else "rowsbB"

            def emit_combine():
                P.emit("scalar", lambda E: E.copy(tav, pav[0:65, 0:QC]),
                       psr=[pav], writes=[tkey])

                rrf = RRF[:, 0:QC] if par == 0 else RRF[:, QC:2 * QC]
                P.emit("scalar", lambda E, tav=tav, rrf=rrf: E.activation(
                    rrf, tav[64:65, :], AF.Ln),
                    reads=[tkey], writes=[rbk + "f"])
                P.emit("scalar", lambda E, rrb=rrb, rrf=rrf: E.activation(
                    rrb, rrf, AF.Exp, scale=-1.0),
                    reads=[rbk + "f"], writes=[rbk])
                P.emit("tensor", mm(pav[0:64, 0:QC], ONR[0:1, 0:64], rrb),
                       reads=[rbk, "const"], psw=[pav])
                P.emit("vector", lambda E, par=par: E.tensor_tensor(
                    o_dst(par), tav[0:64, :], pav[0:64, 0:QC], ALU.mult),
                    reads=[tkey], psr=[pav], writes=[o_key])
            return emit_combine
        return [mk_combine(0, pav_e), mk_combine(1, pav_o)]

    def global_pair(it, pair):
        hkey = f"H{pair}"
        with psum(f"pqe{it}{pair}", [128, T]) as pqe, \
             psum(f"pqo{it}{pair}", [128, T]) as pqo:
            for par, pq in ((0, pqe), (1, pqo)):
                b0 = 64 * par
                fns = [mm(pq[:, QC * q:QC * (q + 1)],
                          WQKG[b0:b0 + 64, 128 * pair:128 * (pair + 1)],
                          H[pair][b0:b0 + 64, QC * q:QC * (q + 1)])
                       for q in range(NQC)]
                P.emit("tensor", fns, reads=[hkey, "const"], psw=[pq])
            for par, pq in ((0, pqe), (1, pqo)):
                h = 2 * pair + par
                P.emit("scalar", lambda E, pq=pq, par=par, h=h: E.activation(
                    Qp[64 * par:64 * par + 64, :], pq[0:64, :], AF.Identity,
                    bias=BQ[0:64, h:h + 1]),
                    psr=[pq], reads=["const"], writes=["Qp"])
                P.emit("scalar", lambda E, pq=pq, par=par, h=h: E.activation(
                    Kp[64 * par:64 * par + 64, :], pq[64:128, :], AF.Identity,
                    bias=BK[64:128, h:h + 1]),
                    psr=[pq], reads=["const"], writes=["Kp"])
        with psum(f"pve{it}{pair}", [64, QC]) as pve, \
             psum(f"pvo{it}{pair}", [128, QC]) as pvo:
            for q in range(NQC):
                P.emit("tensor", [mm(pve[:, :],
                                     WVG[0:64, 64 * pair:64 * (pair + 1)],
                                     H[pair][0:64, QC * q:QC * (q + 1)])],
                       reads=[hkey, "const"], psw=[pve])
                P.emit("scalar", lambda E, q=q, h=2 * pair: E.activation(
                    V[0:64, QC * q:QC * (q + 1)], pve[:, :], AF.Identity,
                    bias=BV[0:64, h:h + 1]),
                    psr=[pve], reads=["const"], writes=["V"])
                P.emit("tensor", [mm(pvo[64:128, :],
                                     WVG[64:128, 64 * pair:64 * (pair + 1)],
                                     H[pair][64:128, QC * q:QC * (q + 1)])],
                       reads=[hkey, "const"], psw=[pvo])
                P.emit("scalar", lambda E, q=q, h=2 * pair + 1: E.activation(
                    V[64:128, QC * q:QC * (q + 1)], pvo[64:128, :], AF.Identity,
                    bias=BV[64:128, h:h + 1]),
                    psr=[pvo], reads=["const"], writes=["V"])
        v_transpose(f"g{it}_{pair}")
        with psum(f"gsc{it}{pair}", [128, 2048]) as psc, \
             psum(f"gae{it}{pair}", [65, 512]) as pavAe, \
             psum(f"gao{it}{pair}", [65, 512]) as pavAo, \
             psum(f"gbe{it}{pair}", [65, 512]) as pavBe, \
             psum(f"gbo{it}{pair}", [65, 512]) as pavBo:
            pending = []
            for q in range(NQC):
                def o_dst(par, q=q, pair=pair):
                    return H[pair][64 * par:64 * par + 64, QC * q:QC * (q + 1)]
                pe, po = (pavAe, pavAo) if q % 2 == 0 else (pavBe, pavBo)
                pending = attn_qc(o_dst, hkey, [(kt, kt) for kt in range(NKT)],
                                  QC * q, lambda ktl: (0, QC), None,
                                  lambda ktl: None, itq=f"g{it}_{pair}_{q}",
                                  psc=psc, pav_e=pe, pav_o=po,
                                  pending=pending, exp_scale=0.125)
            for fn in pending:
                fn()

    class Stream:
        def __init__(self):
            self.slots = [S0, S1]
            self.queue = []
            self.next_issue = 0
            self.assign = {}

        def push(self, name, fns):
            if callable(fns):
                fns = [fns]
            self.queue.append((name, fns))

        def issue_upto(self, n):
            while self.next_issue < min(n, len(self.queue)):
                i = self.next_issue
                name, fns = self.queue[i]
                slot = self.slots[i % 2]
                skey = f"slot{i % 2}"
                self.assign[name] = (slot, skey)
                for fn in fns:
                    P.emit("gpsimd", lambda E, fn=fn, slot=slot: fn(E, slot),
                           writes=[skey], sem="dw")
                self.next_issue += 1

        def use(self, name):
            idx = self.names.index(name)
            self.issue_upto(idx + 2)
            return self.assign[name]

        @property
        def names(self):
            return [n for n, _ in self.queue]

    def wo_global_q(it, stream, pw, pw2, q):
        if True:
            if True:
                for ch in range(4):
                    slot, skey = stream.use(f"wo{q}_{ch}")
                    for oc in (2 * ch, 2 * ch + 1):
                        pt = pw if (oc + q) % 2 == 0 else pw2
                        c0 = 128 * (oc - 2 * ch)
                        fns = [mm(pt[:, :],
                                  slot[:, 256 * ic + c0:256 * ic + c0 + 128],
                                  H[ic][:, QC * q:QC * (q + 1)],
                                  start=(ic == 0), stop=(ic == 7))
                               for ic in range(8)]
                        P.emit("tensor", fns, reads=[skey] + HKEYS, psw=[pt])
                        P.emit("vector", lambda E, pt=pt, oc=oc, q=q:
                               E.scalar_tensor_tensor(
                                   X[oc][:, QC * q:QC * (q + 1)], pt[:, :],
                                   BO[:, oc:oc + 1], X[oc][:, QC * q:QC * (q + 1)],
                                   ALU.add, ALU.add),
                               psr=[pt], reads=["const", f"X{oc}"],
                               writes=[f"X{oc}"])

    def module_ln(it, tag, spill, weave=None):
        mu = ROWS[:, 0:QC]
        rstd = ROWS[:, QC:2 * QC]
        tmp = ROWS[:, 2 * QC:3 * QC]
        with psum(f"plrA{it}{tag}", [1, 1024]) as plrA, \
             psum(f"plrB{it}{tag}", [1, 1024]) as plrB, \
             psum(f"pm1A{it}{tag}", [128, QC]) as pm1A, \
             psum(f"pm2A{it}{tag}", [128, QC]) as pm2A:
            pm1B, pm2B = pm1A, pm2A
            def stats(q):
                plr = plrA if q % 2 == 0 else plrB
                for t in range(8):
                    cb = V[:, 0:QC] if t % 2 == 0 else V[:, QC:2 * QC]
                    cbk = "V" if t % 2 == 0 else "V"
                    sq = TMPB if t % 2 == 0 else V[:, 2 * QC:3 * QC]
                    sqk = "tmpb" if t % 2 == 0 else "V"
                    P.emit("vector", lambda E, t=t, q=q, cb=cb: E.tensor_copy(
                        cb, X[t][:, QC * q:QC * (q + 1)]),
                        reads=[f"X{t}"], writes=[cbk])
                    P.emit("tensor", [mm(plr[:, 0:512], OCL[:, 0:1], cb,
                                         start=(t == 0), stop=(t == 7))],
                           reads=[cbk, "const"], psw=[plr])
                    P.emit("vector", lambda E, cb=cb, sq=sq: E.tensor_tensor(
                        sq[:, 0:QC], cb, cb, ALU.mult),
                        reads=[cbk], writes=[sqk])
                    P.emit("tensor", [mm(plr[:, 512:1024], OCL[:, 0:1],
                                         sq[:, 0:QC], start=(t == 0),
                                         stop=(t == 7))],
                           reads=[sqk, "const"], psw=[plr])

            def finish(q):
                plr = plrA if q % 2 == 0 else plrB
                pb1 = pm1A if q % 2 == 0 else pm1B
                pb2 = pm2A if q % 2 == 0 else pm2B
                P.emit("vector", lambda E, plr=plr: E.tensor_scalar(
                    mu, plr[:, 0:512], 1.0 / D, None, ALU.mult, ALU.bypass),
                    psr=[plr], writes=["rows"])
                P.emit("vector", lambda E: E.tensor_tensor(
                    tmp, mu, mu, ALU.mult), reads=["rows"], writes=["rows"])
                P.emit("vector", lambda E, plr=plr: E.scalar_tensor_tensor(
                    rstd, plr[:, 512:1024], 1.0 / D, tmp, ALU.mult, ALU.subtract),
                    psr=[plr], reads=["rows"], writes=["rows"])
                P.emit("scalar", lambda E: E.activation(
                    rstd, rstd, AF.Ln, bias=EPS[:, 0:1]),
                    reads=["rows", "const"], writes=["rows"])
                P.emit("scalar", lambda E: E.activation(
                    rstd, rstd, AF.Exp, scale=-0.5),
                    reads=["rows"], writes=["rows"])
                P.emit("tensor", mm(pb1[:, :], ONRF[:], mu),
                       reads=["rows", "const"], psw=[pb1])
                P.emit("tensor", mm(pb2[:, :], ONRF[:], rstd),
                       reads=["rows", "const"], psw=[pb2])
                for t in range(8):
                    P.emit("vector", lambda E, t=t, q=q, pb1=pb1: E.tensor_tensor(
                        X[t][:, QC * q:QC * (q + 1)], X[t][:, QC * q:QC * (q + 1)],
                        pb1[:, :], ALU.subtract),
                        reads=[f"X{t}"], psr=[pb1], writes=[f"X{t}"])
                    P.emit("vector", lambda E, t=t, q=q, pb2=pb2: E.tensor_tensor(
                        X[t][:, QC * q:QC * (q + 1)], X[t][:, QC * q:QC * (q + 1)],
                        pb2[:, :], ALU.mult),
                        reads=[f"X{t}"], psr=[pb2], writes=[f"X{t}"])
                if spill:
                    for t in range(8):
                        P.emit("sync", lambda E, t=t, q=q: E.dma_start(
                            out=d["x1sp"][t][:, QC * q:QC * (q + 1)],
                            in_=X[t][:, QC * q:QC * (q + 1)]),
                            reads=[f"X{t}"], writes=["x1d"], sem="dx")

            if weave is None:
                stats(0)
                for q in range(1, NQC):
                    stats(q)
                    finish(q - 1)
                finish(NQC - 1)
            else:
                for q in range(NQC):
                    weave(q)
                    stats(q)
                    if q > 0:
                        finish(q - 1)
                finish(NQC - 1)

    def inner_ln(it, l):
        lgv = LGT[:, 8 * l:8 * l + 8]
        lbv = LBT[:, 8 * l:8 * l + 8]
        mu = ROWS[:, 0:QC]
        rstd = ROWS[:, QC:2 * QC]
        tmp = ROWS[:, 2 * QC:3 * QC]
        with psum(f"pilA{it}{l}", [1, 1024]) as plrA, \
             psum(f"pilB{it}{l}", [1, 1024]) as plrB, \
             psum(f"pi1A{it}{l}", [128, QC]) as pi1A, \
             psum(f"pi2A{it}{l}", [128, QC]) as pi2A, \
             psum(f"pi1B{it}{l}", [128, QC]) as pi1B, \
             psum(f"pi2B{it}{l}", [128, QC]) as pi2B:
            def stats(q):
                plr = plrA if q % 2 == 0 else plrB
                for t in range(8):
                    cb = V[:, 0:QC] if t % 2 == 0 else V[:, QC:2 * QC]
                    cbk = "V" if t % 2 == 0 else "V"
                    sq = TMPB if t % 2 == 0 else V[:, 2 * QC:3 * QC]
                    sqk = "tmpb" if t % 2 == 0 else "V"
                    if t < 4:
                        cl, n0 = 0, 512
                        P.emit("vector", lambda E, t=t, q=q, cb=cb: E.tensor_copy(
                            cb, X[t][:, QC * q:QC * (q + 1)]),
                            reads=[f"X{t}"], writes=[cbk])
                    elif q == 0:
                        cl, n0 = 1, 511
                        P.emit("vector", lambda E, t=t, cb=cb: E.tensor_copy(
                            cb[:, 1:QC], X[t][:, 0:QC - 1]),
                            reads=[f"X{t}"], writes=[cbk])
                    else:
                        cl, n0 = 0, 512
                        P.emit("vector", lambda E, t=t, q=q, cb=cb: E.tensor_copy(
                            cb, X[t][:, QC * q - 1:QC * (q + 1) - 1]),
                            reads=[f"X{t}"], writes=[cbk])
                    P.emit("tensor", [mm(plr[:, cl:cl + n0], OCL[:, 0:1],
                                         cb[:, cl:cl + n0],
                                         start=(t == 0), stop=(t == 7))],
                           reads=[cbk, "const"], psw=[plr])
                    P.emit("vector", lambda E, cb=cb, sq=sq, cl=cl, n0=n0:
                           E.tensor_tensor(sq[:, cl:cl + n0], cb[:, cl:cl + n0],
                                           cb[:, cl:cl + n0], ALU.mult),
                           reads=[cbk], writes=[sqk])
                    P.emit("tensor", [mm(plr[:, 512 + cl:512 + cl + n0],
                                         OCL[:, 0:1], sq[:, cl:cl + n0],
                                         start=(t == 0), stop=(t == 7))],
                           reads=[sqk, "const"], psw=[plr])

            def finish(q):
                plr = plrA if q % 2 == 0 else plrB
                pb1 = pi1A if q % 2 == 0 else pi1B
                pb2 = pi2A if q % 2 == 0 else pi2B
                P.emit("vector", lambda E, plr=plr: E.tensor_scalar(
                    mu, plr[:, 0:512], 1.0 / D, None, ALU.mult, ALU.bypass),
                    psr=[plr], writes=["rows"])
                P.emit("vector", lambda E: E.tensor_tensor(
                    tmp, mu, mu, ALU.mult), reads=["rows"], writes=["rows"])
                P.emit("vector", lambda E, plr=plr: E.scalar_tensor_tensor(
                    rstd, plr[:, 512:1024], 1.0 / D, tmp, ALU.mult, ALU.subtract),
                    psr=[plr], reads=["rows"], writes=["rows"])
                P.emit("scalar", lambda E: E.activation(
                    rstd, rstd, AF.Ln, bias=EPS[:, 0:1]),
                    reads=["rows", "const"], writes=["rows"])
                P.emit("scalar", lambda E: E.activation(
                    rstd, rstd, AF.Exp, scale=-0.5), reads=["rows"], writes=["rows"])
                P.emit("vector", lambda E: E.tensor_copy(ROWSB[:, 0:512], mu),
                       reads=["rows"], writes=["rowsbA"])
                P.emit("vector", lambda E: E.tensor_copy(ROWSB[:, 512:1024], rstd),
                       reads=["rows"], writes=["rowsbB"])
                P.emit("tensor", mm(pb1[:, :], ONR[:], ROWSB[:, 0:512]),
                       reads=["rowsbA", "const"], psw=[pb1])
                P.emit("tensor", mm(pb2[:, :], ONR[:], ROWSB[:, 512:1024]),
                       reads=["rowsbB", "const"], psw=[pb2])
                for t in range(8):
                    dst = H[t][:, QC * q:QC * (q + 1)]
                    if t < 4:
                        P.emit("vector", lambda E, t=t, q=q, pb1=pb1:
                               E.tensor_tensor(
                                   TMPF[:], X[t][:, QC * q:QC * (q + 1)],
                                   pb1[:, :], ALU.subtract),
                               reads=[f"X{t}"], psr=[pb1], writes=["tmpf"])
                        P.emit("vector", lambda E, pb2=pb2: E.tensor_tensor(
                            TMPF[:], TMPF[:], pb2[:, :], ALU.mult),
                            reads=["tmpf"], psr=[pb2], writes=["tmpf"])
                    else:
                        if q == 0:
                            P.emit("vector", lambda E, t=t, pb1=pb1:
                                   E.tensor_tensor(
                                       TMPF[:, 1:QC], X[t][:, 0:QC - 1],
                                       pb1[:, 1:QC], ALU.subtract),
                                   reads=[f"X{t}"], psr=[pb1], writes=["tmpf"])
                            P.emit("vector", lambda E, pb2=pb2: E.tensor_tensor(
                                TMPF[:, 1:QC], TMPF[:, 1:QC], pb2[:, 1:QC],
                                ALU.mult),
                                reads=["tmpf"], psr=[pb2], writes=["tmpf"])
                            P.emit("vector", lambda E, pb1=pb1: E.tensor_scalar(
                                TMPF[:, 0:1], pb1[:, 0:1], -1.0, None,
                                ALU.mult, ALU.bypass),
                                psr=[pb1], writes=["tmpf"])
                            P.emit("vector", lambda E, pb2=pb2: E.tensor_tensor(
                                TMPF[:, 0:1], TMPF[:, 0:1], pb2[:, 0:1],
                                ALU.mult),
                                reads=["tmpf"], psr=[pb2], writes=["tmpf"])
                        else:
                            P.emit("vector", lambda E, t=t, q=q, pb1=pb1:
                                   E.tensor_tensor(
                                       TMPF[:], X[t][:, QC * q - 1:QC * (q + 1) - 1],
                                       pb1[:, :], ALU.subtract),
                                   reads=[f"X{t}"], psr=[pb1], writes=["tmpf"])
                            P.emit("vector", lambda E, pb2=pb2: E.tensor_tensor(
                                TMPF[:], TMPF[:], pb2[:, :], ALU.mult),
                                reads=["tmpf"], psr=[pb2], writes=["tmpf"])
                    P.emit("vector", lambda E, dst=dst, t=t: E.tensor_scalar(
                        dst, TMPF[:], lgv[:, t:t + 1], lbv[:, t:t + 1],
                        ALU.mult, ALU.add),
                        reads=["tmpf", "const"], writes=[f"H{t}"])

            stats(0)
            for q in range(1, NQC):
                stats(q)
                finish(q - 1)
            finish(NQC - 1)

    def ffn_layer(it, l, stream):
        cast_H(shifted=True)
        dbg = (l == 0 and "hs" in dumps)
        if dbg:
            for t in range(8):
                P.emit("gpsimd", lambda E, t=t: E.dma_start(
                    out=dumps["hs"][t], in_=H[t][:]), reads=[f"H{t}"], sem="do")
        for pair in range(NPF):
            slot, skey = stream.use(f"qkv{l}{pair}")
            with psum(f"pj{it}{l}{pair}", [128, QC]) as pj1, \
                 psum(f"pj2{it}{l}{pair}", [128, QC]) as pj2:
                for oi, dstsel in ((0, "q"), (1, "k"), (2, "v")):
                    for q in range(NQC):
                        pt = pj1 if (oi * NQC + q) % 2 == 0 else pj2
                        fns = [mm(pt[:, :],
                                  slot[:, 384 * ic + 128 * oi:
                                       384 * ic + 128 * oi + 128],
                                  H[ic][:, QC * q:QC * (q + 1)],
                                  start=(ic == 0), stop=(ic == 7))
                               for ic in range(8)]
                        P.emit("tensor", fns, reads=[skey] + HKEYS, psw=[pt])
                        if dstsel == "q":
                            P.emit("scalar", lambda E, pt=pt, q=q: E.copy(
                                Qp[:, QC * q:QC * (q + 1)], pt[:, :]),
                                psr=[pt], writes=["Qp"])
                        elif dstsel == "k":
                            P.emit("scalar", lambda E, pt=pt, q=q: E.copy(
                                Kp[:, QC * q:QC * (q + 1)], pt[:, :]),
                                psr=[pt], writes=["Kp"])
                        else:
                            P.emit("scalar", lambda E, pt=pt, q=q: E.copy(
                                V[0:64, QC * q:QC * (q + 1)], pt[0:64, :]),
                                psr=[pt], writes=["V"])
                            P.emit("scalar", lambda E, pt=pt, q=q: E.copy(
                                V[64:128, QC * q:QC * (q + 1)], pt[64:128, :]),
                                psr=[pt], writes=["V"])
            for qki, (buf, bkey, Atab, Ctab) in enumerate(
                    [(Qp, "Qp", RA["rAq"], RA["rCq"]),
                     (Kp, "Kp", RA["rAk"], RA["rCk"])]):
                with psum(f"pl2{it}{l}{pair}{qki}", [1, 1024]) as pl2, \
                     psum(f"pbc{it}{l}{pair}{qki}", [128, QC]) as pbc, \
                     psum(f"psw{it}{l}{pair}{qki}", [128, QC]) as psw_:
                    for q in range(NQC):
                        P.emit("vector", lambda E, buf=buf, q=q: E.tensor_tensor(
                            TMPB[:], buf[:, QC * q:QC * (q + 1)],
                            buf[:, QC * q:QC * (q + 1)], ALU.mult),
                            reads=[bkey], writes=["tmpb"])
                        P.emit("tensor", [mm(pl2[:, 0:512], OCL[:, 1:2], TMPB[:])],
                               reads=["tmpb", "const"], psw=[pl2])
                        P.emit("tensor", [mm(pl2[:, 512:1024], OCL[:, 2:3], TMPB[:])],
                               reads=["tmpb", "const"], psw=[pl2])
                        for par in range(2):
                            seg = pl2[:, 512 * par:512 * par + 512]
                            dst = ROWS[:, 0:512] if par == 0 else ROWS[:, 512:1024]
                            P.emit("scalar", lambda E, seg=seg, dst=dst: E.activation(
                                dst, seg, AF.Ln, bias=EPS[:, 1:2]),
                                psr=[pl2], reads=["const"], writes=["rows"])
                            P.emit("scalar", lambda E, dst=dst: E.activation(
                                dst, dst, AF.Exp, scale=-0.5),
                                reads=["rows"], writes=["rows"])
                            P.emit("vector",
                                   lambda E, dst=dst, par=par: E.tensor_copy(
                                       ROWSB[:, 512 * par:512 * par + 512], dst),
                                   reads=["rows"], writes=["rowsb"])
                        ebi = ((l * 2 + qki) * 2) * 128
                        P.emit("tensor",
                               [mm(pbc[:, :], EBR[0:1, ebi:ebi + 128],
                                   ROWSB[:, 0:512], start=True, stop=False,
                                   skip=True),
                                mm(pbc[:, :], EBR[0:1, ebi + 128:ebi + 256],
                                   ROWSB[:, 512:1024], start=False, stop=True,
                                   skip=True)],
                               reads=["rowsb", "const"], psw=[pbc])
                        P.emit("vector", lambda E, buf=buf, q=q: E.tensor_tensor(
                            TMPB[:], buf[:, QC * q:QC * (q + 1)], pbc[:, :],
                            ALU.mult),
                            reads=[bkey], psr=[pbc], writes=["tmpb"])
                        P.emit("vector", lambda E, q=q, Atab=Atab: E.tensor_tensor(
                            TMPF[:], TMPB[:], Atab[:, QC * q:QC * (q + 1)],
                            ALU.mult),
                            reads=["tmpb", "const"], writes=["tmpf"])
                        P.emit("vector", lambda E, q=q, Ctab=Ctab: E.tensor_tensor(
                            TMPB[:], TMPB[:], Ctab[:, QC * q:QC * (q + 1)],
                            ALU.mult),
                            reads=["tmpb", "const"], writes=["tmpb"])
                        P.emit("tensor", mm(psw_[:, :], PRM[:], TMPB[:]),
                               reads=["tmpb", "const"], psw=[psw_])
                        P.emit("vector", lambda E, buf=buf, q=q: E.tensor_tensor(
                            buf[:, QC * q:QC * (q + 1)], TMPF[:], psw_[:, :],
                            ALU.add),
                            reads=["tmpf"], psr=[psw_], writes=[bkey])
            if dbg and pair == 0 and "t_qp" in dumps:
                for nm, tsr, keys in (("t_qp", Qp, ["Qp"]), ("t_kp", Kp, ["Kp"]),
                                      ("t_vv", V, ["V", "V"])):
                    P.emit("gpsimd", lambda E, nm=nm, tsr=tsr:
                           E.dma_start(out=dumps[nm][:, 0:tsr.shape[1]],
                                       in_=tsr[:]), reads=keys, sem="do")
            v_transpose(f"f{it}{l}{pair}")
            with psum(f"fsc{it}{l}{pair}", [128, 2048]) as psc, \
                 psum(f"fae{it}{l}{pair}", [65, 512]) as pavAe, \
                 psum(f"fao{it}{l}{pair}", [65, 512]) as pavAo, \
                 psum(f"fbe{it}{l}{pair}", [65, 512]) as pavBe, \
                 psum(f"fbo{it}{l}{pair}", [65, 512]) as pavBo:
                pending = []
                for w in range(4):
                    if w == 0:
                        ktl_list = [(kt, kt + 4) for kt in range(4)]
                    else:
                        ktl_list = [(4 * (w - 1) + i, i) for i in range(8)]

                    def nq_of(ktl):
                        return (0, QC)

                    def av_of(ktl):
                        if ktl <= 3:
                            return (0, 128 * (ktl + 1))
                        return (128 * (ktl - 4), QC - 128 * (ktl - 4))

                    def mask_of(ktl):
                        if ktl <= 3:
                            return (MLW, 128 * ktl)
                        return (MUP, 128 * (ktl - 4))

                    def o_dst(par, pair=pair, w=w):
                        return OFF[pair][64 * par:64 * par + 64,
                                         WIN * w:WIN * (w + 1)]

                    pe, po = (pavAe, pavAo) if w % 2 == 0 else (pavBe, pavBo)
                    pending = attn_qc(o_dst, f"OF{pair}", ktl_list, WIN * w,
                                      nq_of, NEGC[:, l:l + 1], mask_of,
                                      itq=f"f{it}{l}{pair}{w}",
                                      psc=psc, pav_e=pe, pav_o=po,
                                      pending=pending, exp_scale=QK_SCALE,
                                      av_of=av_of)
                for fn in pending:
                    fn()
        with psum(f"pwf{it}{l}", [128, QC]) as pw, \
             psum(f"pwf2{it}{l}", [128, QC]) as pw2:
            for q in range(NQC):
                for oc in range(8):
                    slot, skey = stream.use(f"wof{l}_{q}_{oc // 4}")
                    pt = pw if (oc + q) % 2 == 0 else pw2
                    c0 = 128 * (oc % 4)
                    fns = [mm(pt[:, :],
                              slot[:, 512 * ic + c0:512 * ic + c0 + 128],
                              OFF[ic][:, QC * q:QC * (q + 1)],
                              start=(ic == 0), stop=(ic == 3))
                           for ic in range(4)]
                    P.emit("tensor", fns,
                           reads=[skey] + [f"OF{i}" for i in range(4)], psw=[pt])
                    P.emit("vector", lambda E, pt=pt, oc=oc, q=q: E.tensor_tensor(
                        X[oc][:, QC * q:QC * (q + 1)],
                        X[oc][:, QC * q:QC * (q + 1)], pt[:, :], ALU.add),
                        psr=[pt], reads=[f"X{oc}"], writes=[f"X{oc}"])
        if dbg and "t_o0f" in dumps:
            P.emit("gpsimd", lambda E: E.dma_start(
                out=dumps["t_o0f"][:], in_=OFF[0][:]), reads=["OF0"], sem="do")
        if dbg:
            dump_X("xwo")
        inner_ln(it, l)
        if dbg and "hln" in dumps:
            for t in range(8):
                P.emit("gpsimd", lambda E, t=t: E.dma_start(
                    out=dumps["hln"][t], in_=H[t][:]), reads=[f"H{t}"], sem="do")
        w1_w2(it, l, stream)

    def w1_w2(it, l, stream):
        b1v = B1C[:, 2 * NCH * l:2 * NCH * (l + 1)]
        b2v = B2T[:, 8 * l:8 * l + 8]

        def glu_ap(j, wj):
            if j < 16:
                return OFF[j // 4][0:wj, 512 * (j % 4):512 * (j % 4) + 512], \
                    f"OF{j // 4}"
            return GLU[0:wj, 512 * (j - 16):512 * (j - 16) + 512], "glu"

        for q in range(NQC):
            with psum(f"pa{it}{l}{q}", [128, QC]) as pa, \
                 psum(f"pg{it}{l}{q}", [128, QC]) as pg, \
                 psum(f"pa2{it}{l}{q}", [128, QC]) as pa2, \
                 psum(f"pg2{it}{l}{q}", [128, QC]) as pg2:
                for j, wj in enumerate(CHW):
                    jj, sub = j // 2, j % 2
                    slot, skey = stream.use(f"w1_{l}_{q}_{jj}")
                    wid = 2 * CHW[2 * jj] + 2 * CHW[2 * jj + 1]
                    sb0 = 2 * CHW[2 * jj] * sub
                    pta = pa if j % 2 == 0 else pa2
                    ptg = pg if j % 2 == 0 else pg2
                    fa = [mm(pta[0:wj, :],
                             slot[:, wid * ic + sb0:wid * ic + sb0 + wj],
                             H[ic][:, QC * q:QC * (q + 1)],
                             start=(ic == 0), stop=(ic == 7)) for ic in range(8)]
                    fg = [mm(ptg[0:wj, :],
                             slot[:, wid * ic + sb0 + wj:wid * ic + sb0 + 2 * wj],
                             H[ic][:, QC * q:QC * (q + 1)],
                             start=(ic == 0), stop=(ic == 7)) for ic in range(8)]
                    P.emit("tensor", fa, reads=[skey] + HKEYS, psw=[pta])
                    P.emit("tensor", fg, reads=[skey] + HKEYS, psw=[ptg])
                    P.emit("scalar", lambda E, ptg=ptg, wj=wj, j=j: E.activation(
                        TMPB[0:wj, :], ptg[0:wj, :], AF.Gelu,
                        bias=b1v[0:wj, 2 * j + 1:2 * j + 2]),
                        psr=[ptg], reads=["const"], writes=["tmpb"])
                    P.emit("vector", lambda E, pta=pta, wj=wj, j=j: E.tensor_scalar(
                        TMPF[0:wj, :], pta[0:wj, :], b1v[0:wj, 2 * j:2 * j + 1],
                        None, ALU.add, ALU.bypass),
                        psr=[pta], reads=["const"], writes=["tmpf"])
                    gap, gkey = glu_ap(j, wj)
                    P.emit("vector", lambda E, gap=gap, wj=wj: E.tensor_tensor(
                        gap, TMPF[0:wj, :], TMPB[0:wj, :], ALU.mult),
                        reads=["tmpf", "tmpb"], writes=[gkey])
            with ExitStack() as es2:
                pws = [es2.enter_context(psum(f"pw2_{it}{l}{q}_{o}", [128, QC]))
                       for o in range(8)]
                for bi in range(8):
                    slot, skey = stream.use(f"w2_{l}_{q}_{bi}")
                    js = list(range(3 * bi, min(3 * bi + 3, NCH)))
                    for oc in range(8):
                        fns = []
                        rkeys = [skey]
                        for ji, j in enumerate(js):
                            wj = CHW[j]
                            gap, gkey = glu_ap(j, wj)
                            rkeys.append(gkey)
                            fns.append(mm(
                                pws[oc][:, :],
                                slot[0:wj, 1024 * ji + 128 * oc:
                                     1024 * ji + 128 * oc + 128],
                                gap,
                                start=(bi == 0 and ji == 0), stop=False,
                                skip=True))
                        P.emit("tensor", fns, reads=rkeys, psw=[pws[oc]])
                for oc in range(8):
                    P.emit("vector", lambda E, oc=oc, q=q: E.scalar_tensor_tensor(
                        X[oc][:, QC * q:QC * (q + 1)], pws[oc][:, :],
                        b2v[:, oc:oc + 1], X[oc][:, QC * q:QC * (q + 1)],
                        ALU.add, ALU.add),
                        psr=[pws[oc]], reads=["const", f"X{oc}"], writes=[f"X{oc}"])

    def finale(it):
        for t in range(8):
            P.emit("sync", lambda E, t=t: E.dma_start(out=STG[:], in_=d["x1sp"][t]),
                   writes=STGK, sem="dx")
            P.emit("vector", lambda E, t=t: E.tensor_tensor(
                X[t][:], X[t][:], STG[:], ALU.add),
                reads=[f"X{t}"] + STGK, writes=[f"X{t}"])
        module_ln(it, "fin", spill=False)
        stg2 = STG[:].rearrange("p (f c) -> p f c", f=2)
        with psum(f"pot{it}", [128, 512]) as pt0, \
             psum(f"pot2{it}", [128, 512]) as pt1:
            for st in range(NKT):
                half = st % 2
                hk = STGK[half]
                for cg in range(2):
                    pt = pt0 if cg == 0 else pt1
                    fns = [lambda E, ct=4 * cg + ci, ci=ci, st=st, pt=pt:
                           E.transpose(pt[:, 128 * ci:128 * (ci + 1)],
                                       X[ct][:, 128 * st:128 * (st + 1)], IDf[:])
                           for ci in range(4)]
                    P.emit("tensor", fns,
                           reads=[f"X{4 * cg + i}" for i in range(4)] + ["const"],
                           psw=[pt])
                    P.emit("vector", lambda E, pt=pt, half=half, cg=cg:
                           E.tensor_copy(stg2[:, half, 512 * cg:512 * (cg + 1)],
                                         pt[:, :]),
                           psr=[pt], writes=[hk])
                P.emit("sync", lambda E, st=st, half=half, it=it: E.dma_start(
                    out=d["out"][it, 128 * st:128 * (st + 1), :],
                    in_=stg2[:, half, :]),
                    reads=[hk], writes=[], sem="do")

    def dump_X(nm):
        if nm not in dumps:
            return
        for t in range(8):
            P.emit("sync", lambda E, t=t, nm=nm: E.dma_start(
                out=dumps[nm][t], in_=X[t][:]),
                reads=[f"X{t}"], sem="do")

    def convert_w1w2():
        ci = [0]

        def conv(src_ap, dst_ap, nt, no, wkey):
            slot = S0 if ci[0] % 2 == 0 else S1
            skey = f"slot{ci[0] % 2}"
            ci[0] += 1
            sv = slot[:, 0:nt * no].rearrange("p (t o) -> p t o", o=no)
            P.emit("gpsimd", lambda E: E.dma_start(
                out=sv, in_=src_ap), writes=[skey], sem="dw")
            P.emit("sync", lambda E: E.dma_start(
                out=dst_ap, in_=sv), reads=[skey],
                writes=[wkey], sem="dx")
        for l in range(2):
            for c in range(15):
                c0 = 364 * c
                conv(d["w1_pm"][l][:, c0:c0 + 364].rearrange(
                        "(t p) o -> p t o", p=128),
                     d["w1b"][l][:, c0:c0 + 364].rearrange(
                        "(t p) o -> p t o", p=128),
                     8, 364, f"w1b{l}")
            for c in range(7):
                r0 = 384 * c
                conv(d["w2_n"][l][r0:r0 + 384, :].rearrange(
                        "(c p) o -> p c o", p=128),
                     d["w2b"][l][r0:r0 + 384, :].rearrange(
                        "(c p) o -> p c o", p=128),
                     3, 1024, f"w2b{l}")
            slot = S0 if ci[0] % 2 == 0 else S1
            skey = f"slot{ci[0] % 2}"
            ci[0] += 1
            P.emit("gpsimd", lambda E, slot=slot, l=l: E.dma_start(
                out=slot[0:42, 0:1024], in_=d["w2_n"][l][2688:2730, :]),
                writes=[skey], sem="dw")
            P.emit("sync", lambda E, slot=slot, l=l: E.dma_start(
                out=d["w2b"][l][2688:2730, :], in_=slot[0:42, 0:1024]),
                reads=[skey], writes=[f"w2b{l}"], sem="dx")

    load_consts()
    convert_w1w2()
    calc_negc()
    for it in range(n_items):
        stream = Stream()
        for q in range(NQC):
            for ch in range(4):
                stream.push(f"wo{q}_{ch}", lambda E, slot, ch=ch: E.dma_start(
                    out=slot[:, 0:2048].rearrange("p (t o) -> p t o", o=256),
                    in_=d["wo_g"][:, 256 * ch:256 * (ch + 1)].rearrange(
                        "(t p) o -> p t o", p=128)))
        for l in range(2):
            for pair in range(NPF):
                stream.push(f"qkv{l}{pair}",
                            lambda E, slot, l=l, pair=pair: E.dma_start(
                                out=slot[:, 0:8 * 384].rearrange(
                                    "p (t o) -> p t o", o=384),
                                in_=d["wqkv_pm"][l, pair].rearrange(
                                    "(t p) o -> p t o", p=128)))
            for q in range(NQC):
                for hh in range(2):
                    stream.push(f"wof{l}_{q}_{hh}",
                                lambda E, slot, l=l, hh=hh: E.dma_start(
                                    out=slot[:, 0:2048].rearrange(
                                        "p (t o) -> p t o", o=512),
                                    in_=d["wo_f"][l][:, 512 * hh:512 * (hh + 1)]
                                    .rearrange("(t p) o -> p t o", p=128)))
            for q in range(NQC):
                for jj in range(11):
                    j0, j1 = 2 * jj, 2 * jj + 1
                    o0 = _W1_PAIR_OFF[j0]
                    wid = 2 * CHW[j0] + 2 * CHW[j1]
                    stream.push(f"w1_{l}_{q}_{jj}",
                                lambda E, slot, l=l, o0=o0, wid=wid: E.dma_start(
                                    out=slot[:, 0:8 * wid].rearrange(
                                        "p (t o) -> p t o", o=wid),
                                    in_=d["w1b"][l][:, o0:o0 + wid].rearrange(
                                        "(t p) o -> p t o", p=128)))
                for bi in range(7):
                    r0 = 384 * bi
                    stream.push(f"w2_{l}_{q}_{bi}",
                                lambda E, slot, l=l, r0=r0: E.dma_start(
                                    out=slot[:, 0:3072].rearrange(
                                        "p (c o) -> p c o", o=1024),
                                    in_=d["w2b"][l][r0:r0 + 384, :].rearrange(
                                        "(c p) o -> p c o", p=128)))
                stream.push(f"w2_{l}_{q}_7", lambda E, slot, l=l: E.dma_start(
                    out=slot[0:42, 0:1024], in_=d["w2b"][l][2688:2730, :]))
        stream.issue_upto(2)
        phase_in(it)
        cast_H(shifted=False)
        if dev_stage == "gpair0":
            global_pair(it, 0)
            for nm, tsr, keys in (("t_qp", Qp, ["Qp"]), ("t_kp", Kp, ["Kp"]),
                                  ("t_vv", V, ["V", "V"]),
                                  ("t_vte", VTe, ["VT0"]),
                                  ("t_o0", H[0], ["H0"]),
                                  ("t_x0", X[0], ["X0"])):
                if nm in dumps:
                    w = tsr.shape[1]
                    P.emit("gpsimd", lambda E, a=dumps[nm][:, 0:w], b=tsr[:]:
                           E.dma_start(out=a, in_=b), reads=keys, sem="do")
            break
        for pair in range(NPG):
            global_pair(it, pair)
        if dev_stage == "gattn":
            for t in range(8):
                if "o_all" in dumps:
                    P.emit("gpsimd", lambda E, t=t: E.dma_start(
                        out=dumps["o_all"][t], in_=H[t][:]),
                        reads=[f"H{t}"], sem="do")
        with psum(f"pwo{it}", [128, QC]) as pw, \
             psum(f"pwo2{it}", [128, QC]) as pw2:
            if dev_stage == "gattn":
                for q in range(NQC):
                    wo_global_q(it, stream, pw, pw2, q)
                dump_X("post_attn")
                break

            def weave(q, pw=pw, pw2=pw2):
                wo_global_q(it, stream, pw, pw2, q)
            module_ln(it, "l1", spill=True, weave=weave)
        dump_X("post_ln1")
        for l in range(2):
            ffn_layer(it, l, stream)
            dump_X(f"post_ffn{l}")
        finale(it)
    P._wait("sync", "do", P.cnt["do"])
    P.replay(block)
    es.close()
    return nc, P, dumps


def kernel(**inputs):
    x = np.ascontiguousarray(inputs["L0_attn_x"], dtype=np.float32)
    shared = {}
    for k, v in _host_consts().items():
        shared[k] = np.ascontiguousarray(v, np.float32)
    for k, v in _layout_weights(inputs).items():
        shared[k] = np.ascontiguousarray(v, np.float32)
    nc, _, _ = build_program()
    in_maps = []
    for c in range(NCORES):
        m = dict(shared)
        m["x"] = np.ascontiguousarray(x[IPC * c:IPC * (c + 1)])
        in_maps.append(m)
    res = run_bass_kernel_spmd(nc, in_maps, core_ids=list(range(NCORES)))
    return np.concatenate([r["out"] for r in res.results], axis=0)


# version 17
# speedup vs baseline: 1.1775x; 1.0039x over previous
import numpy as np
from contextlib import ExitStack

import concourse.bass as bass
import concourse.mybir as mybir
from concourse.bass_utils import run_bass_kernel_spmd

F32 = mybir.dt.float32
BF16 = mybir.dt.bfloat16
AF = mybir.ActivationFunctionType
ALU = mybir.AluOpType
AX = mybir.AxisListType

B, T, D = 32, 2048, 1024
NCORES = 8
IPC = B // NCORES
NPG = 8
NPF = 4
NKT = T // 128
QC = 512
NQC = T // QC
HID = 2730
HID2 = 5460
WIN = 512
QK_SCALE = 8.0
CHW = [128] * 21 + [42]
NCH = len(CHW)

_W1_PAIR_OFF = []
_off = 0
for _w in CHW:
    _W1_PAIR_OFF.append(_off)
    _off += 2 * _w
assert _off == HID2


def _rotary_tables():
    half = np.arange(0, 64, 2, dtype=np.float64)
    inv_freq = 1.0 / (10000.0 ** (half / 64.0))
    t = np.arange(T, dtype=np.float64)
    freqs = t[:, None] * inv_freq[None, :]
    freqs = np.concatenate([freqs, freqs], axis=1)
    base = (half + 0.4 * 64.0) / (1.4 * 64.0)
    power = (t - T // 2) / float(WIN // 2)
    scale = base[None, :] ** power[:, None]
    scale = np.concatenate([scale, scale], axis=1)
    cos, sin = np.cos(freqs), np.sin(freqs)
    Aq = (cos * scale).T
    Sq = (sin * scale).T
    Ak = (cos / scale).T
    Sk = (sin / scale).T

    def mkC(S):
        C = np.empty_like(S)
        C[0:32] = S[32:64]
        C[32:64] = -S[0:32]
        return C

    def dup(a):
        return np.concatenate([a, a], axis=0).astype(np.float32)

    return dup(Aq), dup(mkC(Sq)), dup(Ak), dup(mkC(Sk))


def _host_consts():
    c = {}
    c["ident"] = np.eye(128, dtype=np.float32)
    id2 = np.zeros((128, 64), np.float32)
    id2[0:64] = np.eye(64)
    id2[64:128] = np.eye(64)
    c["ident2"] = id2
    P = np.zeros((128, 128), np.float32)
    for p in range(128):
        blk = (p // 64) * 64
        P[blk + ((p - blk + 32) % 64), p] = 1.0
    c["perm"] = P
    c["rAq"], c["rCq"], c["rAk"], c["rCk"] = _rotary_tables()
    i = np.arange(128)
    c["mlow"] = (i[None, :] <= i[:, None]).astype(np.float32)
    c["mup"] = (i[None, :] >= i[:, None]).astype(np.float32)
    c["ones_row"] = np.ones((1, 128), np.float32)
    c["negones_row"] = -np.ones((1, 128), np.float32)
    oc = np.ones((128, 3), np.float32)
    oc[64:, 1] = 0.0
    oc[:64, 2] = 0.0
    c["onescols"] = oc
    c["epscols"] = np.array([[1e-5, 1e-24]], np.float32)
    return c


def _w1_cols():
    order = []
    off = 0
    for w in CHW:
        order.append(("a", off, w))
        order.append(("g", HID + off, w))
        off += w
    cols = []
    for _, start, w in order:
        cols.extend(range(start, start + w))
    return np.array(cols, np.int64)


_W1_COLS = _w1_cols()


def _layout_weights(inp):
    L = {}
    Wq, Wk, Wv = inp["L0_attn_Wq"], inp["L0_attn_Wk"], inp["L0_attn_Wv"]
    WQK = np.zeros((128, NPG, 128), np.float32)
    WV = np.zeros((128, NPG, 64), np.float32)
    for h in range(16):
        par, p = h % 2, h // 2
        WQK[64 * par:64 * par + 64, p, 0:64] = Wq[h]
        WQK[64 * par:64 * par + 64, p, 64:128] = Wk[h]
        WV[64 * par:64 * par + 64, p, :] = Wv[h]
    L["wqk_g"] = WQK.reshape(128, NPG * 128)
    L["wv_g"] = WV.reshape(128, NPG * 64)
    for nm, arr in (("bq", inp["L0_attn_bq"]), ("bk", inp["L0_attn_bk"]),
                    ("bv", inp["L0_attn_bv"])):
        t = np.zeros((128, 16), np.float32)
        t[0:64] = arr.T
        t[64:128] = arr.T
        L[nm] = t
    L["wo_g"] = inp["L0_attn_Wo"]
    L["bo_t"] = np.ascontiguousarray(inp["L0_attn_bo"].reshape(8, 128).T)
    wqkv = inp["L0_ffn_wqkv"]
    pm = np.zeros((2, NPF, 1024, 384), np.float32)
    for l in range(2):
        for p in range(NPF):
            pm[l, p, :, 0:128] = wqkv[l][:, 128 * p:128 * p + 128]
            pm[l, p, :, 128:256] = wqkv[l][:, 512 + 128 * p:512 + 128 * p + 128]
            pm[l, p, :, 256:384] = wqkv[l][:, 1024 + 128 * p:1024 + 128 * p + 128]
    L["wqkv_pm"] = pm
    L["wo_f"] = inp["L0_ffn_wo"]
    L["w1_pm"] = np.ascontiguousarray(inp["L0_ffn_w1"][:, :, _W1_COLS])
    b1 = inp["L0_ffn_b1"][:, _W1_COLS]
    b1c = np.zeros((2, 128, 2 * NCH), np.float32)
    for l in range(2):
        off = 0
        for j, w in enumerate(CHW):
            b1c[l, 0:w, 2 * j] = b1[l, off:off + w]
            b1c[l, 0:w, 2 * j + 1] = b1[l, off + w:off + 2 * w]
            off += 2 * w
    L["b1c"] = b1c
    L["w2_n"] = inp["L0_ffn_w2"]
    L["b2_t"] = np.stack([np.ascontiguousarray(inp["L0_ffn_b2"][l].reshape(8, 128).T)
                          for l in range(2)])
    L["lg_t"] = np.stack([np.ascontiguousarray(inp["L0_ffn_ln_g"][l].reshape(8, 128).T)
                          for l in range(2)])
    L["lb_t"] = np.stack([np.ascontiguousarray(inp["L0_ffn_ln_b"][l].reshape(8, 128).T)
                          for l in range(2)])
    ebr = np.zeros((1, 8 * 128), np.float32)
    qs, ks = inp["L0_ffn_q_scale"], inp["L0_ffn_k_scale"]
    for l in range(2):
        for qi, arr in ((0, qs[l]), (1, ks[l])):
            for par in range(2):
                i = ((l * 2 + qi) * 2 + par) * 128
                ebr[0, i + 64 * par:i + 64 * par + 64] = arr
    L["ebr"] = ebr
    qr = np.zeros((1, 256), np.float32)
    qr[0, 0:64] = qs[0]; qr[0, 64:128] = ks[0]
    qr[0, 128:192] = qs[1]; qr[0, 192:256] = ks[1]
    L["qkrows"] = qr
    return L


class Prog:
    CENG = ("tensor", "vector", "scalar", "gpsimd", "sync")

    SAME_ENGINE_SYNC = True
    SAME_ENGINE_RAW_SYNC = False

    def __init__(self, nc):
        self.nc = nc
        self.ops = {e: [] for e in self.CENG}
        self.sem = {}
        self.cnt = {}
        self.lastwait = {e: {} for e in self.CENG}
        self.bufw = {}
        self.bufr = {}
        self.psum_banks = {}
        self.nops = 0

    def add_sem(self, name, handle):
        self.sem[name] = handle
        self.cnt[name] = 0

    @staticmethod
    def _esem(eng):
        return {"tensor": "sT", "vector": "sV", "scalar": "sS",
                "gpsimd": "sP", "sync": "sY"}[eng]

    def _wait(self, eng, semname, val):
        if val <= 0:
            return
        if self.lastwait[eng].get(semname, 0) >= val:
            return
        self.lastwait[eng][semname] = val
        sem = self.sem[semname]
        self.ops[eng].append(lambda E, s=sem, v=val: E.wait_ge(s, v))
        self.nops += 1

    def bank_keys(self, t):
        nm = t.name
        if nm not in self.psum_banks:
            m = self.nc.lookup_mloc(t)
            bpp = int(np.prod(t.shape[1:])) * (4 if t.dtype == F32 else 2)
            nb = (bpp + 2047) // 2048
            self.psum_banks[nm] = tuple(f"bank{m.bank + i}" for i in range(nb))
        return self.psum_banks[nm]

    def emit(self, eng, fns, reads=(), writes=(), sem=None, inc=None, psr=(),
             psw=()):
        if callable(fns):
            fns = [fns]
        semname = sem or self._esem(eng)
        amount = inc if inc is not None else (16 if semname in ("dx", "dw", "do") else 1)
        reads = list(reads)
        writes = list(writes)
        for t in psr:
            reads.extend(self.bank_keys(t))
        for t in psw:
            writes.extend(self.bank_keys(t))
        own = None if self.SAME_ENGINE_SYNC else self._esem(eng)
        rown = None if self.SAME_ENGINE_RAW_SYNC else self._esem(eng)
        for k in reads:
            w = self.bufw.get(k)
            if w is not None and w[0] != rown:
                self._wait(eng, w[0], w[1])
            if k.startswith("bank"):
                for s, v in self.bufr.get(k, {}).items():
                    if s != semname:
                        self._wait(eng, s, v)
        for k in writes:
            w = self.bufw.get(k)
            if w is not None and w[0] != own:
                self._wait(eng, w[0], w[1])
            for s, v in self.bufr.get(k, {}).items():
                if s != own:
                    self._wait(eng, s, v)
        sem_h = self.sem[semname]
        n = len(fns)
        for i, fn in enumerate(fns):
            if i == n - 1:
                self.ops[eng].append(
                    lambda E, f=fn, s=sem_h, a=amount: f(E).then_inc(s, a))
            else:
                self.ops[eng].append(lambda E, f=fn: f(E))
        self.nops += n
        self.cnt[semname] += amount
        val = self.cnt[semname]
        for k in writes:
            self.bufw[k] = (semname, val)
            self.bufr[k] = {}
        for k in reads:
            self.bufr.setdefault(k, {})[semname] = val
        return semname, val

    def replay(self, block):
        for eng in self.CENG:
            oplist = self.ops[eng]

            def run(E, oplist=oplist):
                for f in oplist:
                    f(E)
            getattr(block, eng)(run)


HKEYS = [f"H{t}" for t in range(8)]


def build_program(n_items=IPC, dump=(), dev_stage=None):
    nc = bass.Bass("TRN2", target_bir_lowering=False)
    es = ExitStack()
    Et = es.enter_context

    d = {}
    d["x"] = nc.dram_tensor("x", [IPC, T, D], F32, kind="ExternalInput").ap()
    d["out"] = nc.dram_tensor("out", [IPC, T, D], F32, kind="ExternalOutput").ap()
    for nm, shape in [
        ("ident", [128, 128]), ("ident2", [128, 64]), ("perm", [128, 128]),
        ("rAq", [128, T]), ("rCq", [128, T]), ("rAk", [128, T]), ("rCk", [128, T]),
        ("mlow", [128, 128]), ("mup", [128, 128]),
        ("ones_row", [1, 128]), ("negones_row", [1, 128]), ("onescols", [128, 3]),
        ("wqk_g", [128, NPG * 128]), ("wv_g", [128, NPG * 64]),
        ("bq", [128, 16]), ("bk", [128, 16]), ("bv", [128, 16]),
        ("wo_g", [1024, 1024]), ("bo_t", [128, 8]),
        ("wqkv_pm", [2, NPF, 1024, 384]), ("wo_f", [2, 512, 1024]),
        ("w1_pm", [2, 1024, HID2]), ("b1c", [2, 128, 2 * NCH]),
        ("w2_n", [2, HID, 1024]), ("b2_t", [2, 128, 8]),
        ("lg_t", [2, 128, 8]), ("lb_t", [2, 128, 8]),
        ("ebr", [1, 8 * 128]), ("qkrows", [1, 256]), ("epscols", [1, 2]),
    ]:
        d[nm] = nc.dram_tensor(nm, shape, F32, kind="ExternalInput").ap()
    d["x1sp"] = nc.dram_tensor("x1sp", [8, 128, T], F32, kind="Internal").ap()
    d["w1b"] = nc.dram_tensor("w1b", [2, 1024, HID2], BF16, kind="Internal").ap()
    d["w2b"] = nc.dram_tensor("w2b", [2, HID, 1024], BF16, kind="Internal").ap()
    dumps = {}
    for nm in dump:
        shp = [128, T] if nm.startswith("t_") else [8, 128, T]
        dumps[nm] = nc.dram_tensor("dump_" + nm, shp, F32,
                                   kind="ExternalOutput").ap()

    X = [Et(nc.sbuf_tensor(f"X{t}", [128, T], F32)) for t in range(8)]
    H = [Et(nc.sbuf_tensor(f"H{t}", [128, T], BF16)) for t in range(8)]
    OFF = [Et(nc.sbuf_tensor(f"OF{t}", [128, T], BF16)) for t in range(4)]
    STG = Et(nc.sbuf_tensor("STG", [128, T], F32))
    Qp = Et(nc.sbuf_tensor("Qp", [128, T], BF16))
    Kp = Et(nc.sbuf_tensor("Kp", [128, T], BF16))
    V = Et(nc.sbuf_tensor("V", [128, T], BF16))
    VTe = Et(nc.sbuf_tensor("VTe", [128, NKT * 72], BF16))
    VTo = Et(nc.sbuf_tensor("VTo", [128, NKT * 72], BF16))
    EX = Et(nc.sbuf_tensor("EX", [128, 2048], BF16))
    GLU = Et(nc.sbuf_tensor("GLU", [128, 6 * 512], BF16))
    S0 = Et(nc.sbuf_tensor("S0", [128, 4096], BF16))
    S1 = Et(nc.sbuf_tensor("S1", [128, 4096], BF16))
    RA = {nm: Et(nc.sbuf_tensor("t" + nm, [128, T], BF16))
          for nm in ("rAq", "rCq", "rAk", "rCk")}
    IDf = Et(nc.sbuf_tensor("IDf", [128, 128], F32))
    ID2 = Et(nc.sbuf_tensor("ID2", [128, 64], BF16))
    PRM = Et(nc.sbuf_tensor("PRM", [128, 128], BF16))
    MLW = Et(nc.sbuf_tensor("MLW", [128, 128], BF16))
    MUP = Et(nc.sbuf_tensor("MUP", [128, 128], BF16))
    ONR = Et(nc.sbuf_tensor("ONR", [1, 128], BF16))
    ONRF = Et(nc.sbuf_tensor("ONRF", [1, 128], F32))
    NONR = Et(nc.sbuf_tensor("NONR", [1, 128], BF16))
    OCL = Et(nc.sbuf_tensor("OCL", [128, 3], BF16))
    OCLF = Et(nc.sbuf_tensor("OCLF", [128, 3], F32))
    WQKG = Et(nc.sbuf_tensor("WQKG", [128, NPG * 128], BF16))
    WVG = Et(nc.sbuf_tensor("WVG", [128, NPG * 64], BF16))
    BQ = Et(nc.sbuf_tensor("BQ", [128, 16], F32))
    BK = Et(nc.sbuf_tensor("BK", [128, 16], F32))
    BV = Et(nc.sbuf_tensor("BV", [128, 16], F32))
    BO = Et(nc.sbuf_tensor("BO", [128, 8], F32))
    B1C = Et(nc.sbuf_tensor("B1C", [128, 4 * NCH], F32))
    B2T = Et(nc.sbuf_tensor("B2T", [128, 16], F32))
    LGT = Et(nc.sbuf_tensor("LGT", [128, 16], F32))
    LBT = Et(nc.sbuf_tensor("LBT", [128, 16], F32))
    EBR = Et(nc.sbuf_tensor("EBR", [1, 8 * 128], BF16))
    QKR = Et(nc.sbuf_tensor("QKR", [1, 256], F32))
    EPS = Et(nc.sbuf_tensor("EPS", [1, 2], F32))
    ROWS = Et(nc.sbuf_tensor("ROWS", [1, 4 * QC], F32))
    RRF = Et(nc.sbuf_tensor("RRF", [1, 2 * QC], F32))
    ROWSB = Et(nc.sbuf_tensor("ROWSB", [1, 2 * QC], BF16))
    ROWX = Et(nc.sbuf_tensor("ROWX", [1, 16], F32))
    ROWXB = Et(nc.sbuf_tensor("ROWXB", [1, 16], BF16))
    NEGC = Et(nc.sbuf_tensor("NEGC", [128, 2], F32))
    TMPF = Et(nc.sbuf_tensor("TMPF", [128, QC], F32))
    TMPF2 = Et(nc.sbuf_tensor("TMPF2", [128, QC], F32))
    TMPB = Et(nc.sbuf_tensor("TMPB", [128, QC], BF16))

    sems = {}
    for s in ("sT", "sV", "sS", "sP", "sY", "dx", "dw", "do"):
        sems[s] = Et(nc.semaphore(s))

    block = Et(nc.Block())
    P = Prog(nc)
    for s, h in sems.items():
        P.add_sem(s, h)

    def mm(out_ap, lhsT_ap, rhs_ap, start=True, stop=True, skip=False):
        return lambda E: E.matmul(out_ap, lhsT_ap, rhs_ap, start=start,
                                  stop=stop, skip_group_check=skip)

    def psum(name, shape, dtype=F32):
        return nc.psum_tensor(name, shape, dtype)

    STGK = ["stgA", "stgB"]

    def load_consts():
        for dst, src in [(IDf[:], d["ident"][:]), (BQ[:], d["bq"][:]),
                         (BK[:], d["bk"][:]), (BV[:], d["bv"][:]),
                         (BO[:], d["bo_t"][:]),
                         (B1C[:, 0:2 * NCH], d["b1c"][0]),
                         (B1C[:, 2 * NCH:4 * NCH], d["b1c"][1]),
                         (B2T[:, 0:8], d["b2_t"][0]), (B2T[:, 8:16], d["b2_t"][1]),
                         (LGT[:, 0:8], d["lg_t"][0]), (LGT[:, 8:16], d["lg_t"][1]),
                         (LBT[:, 0:8], d["lb_t"][0]), (LBT[:, 8:16], d["lb_t"][1]),
                         (QKR[:], d["qkrows"][:]), (EPS[:], d["epscols"][:]),
                         (OCLF[:], d["onescols"][:]),
                         (ONRF[:], d["ones_row"][:])]:
            P.emit("sync", lambda E, a=dst, b=src: E.dma_start(out=a, in_=b),
                   writes=["const"], sem="dx")
        for dst, src in [(ID2[:], d["ident2"][:]), (PRM[:], d["perm"][:]),
                         (MLW[:], d["mlow"][:]), (MUP[:], d["mup"][:]),
                         (ONR[:], d["ones_row"][:]), (NONR[:], d["negones_row"][:]),
                         (OCL[:], d["onescols"][:]),
                         (WQKG[:], d["wqk_g"][:]), (WVG[:], d["wv_g"][:]),
                         (EBR[:], d["ebr"][:]),
                         (RA["rAq"][:], d["rAq"][:]), (RA["rCq"][:], d["rCq"][:]),
                         (RA["rAk"][:], d["rAk"][:]), (RA["rCk"][:], d["rCk"][:])]:
            P.emit("gpsimd", lambda E, a=dst, b=src: E.dma_start(out=a, in_=b),
                   writes=["const"], sem="dw")
        vte3 = VTe[:].rearrange("p (k c) -> p k c", c=72)
        vto3 = VTo[:].rearrange("p (k c) -> p k c", c=72)
        P.emit("vector", lambda E: E.memset(vte3[:, :, 64:65], 1.0),
               writes=["VT0_ones"])
        P.emit("vector", lambda E: E.memset(vto3[:, :, 64:65], 1.0),
               writes=["VT1_ones"])

    def calc_negc():
        with psum("pcng", [128, 512]) as pc:
            for l in range(2):
                P.emit("vector", lambda E, l=l: E.reduce_max(
                    ROWX[:, 4 * l:4 * l + 1], QKR[:, 128 * l:128 * l + 64],
                    axis=AX.X, apply_absolute_value=True),
                    reads=["const"], writes=["rowx"])
                P.emit("vector", lambda E, l=l: E.reduce_max(
                    ROWX[:, 4 * l + 1:4 * l + 2], QKR[:, 128 * l + 64:128 * l + 128],
                    axis=AX.X, apply_absolute_value=True),
                    reads=["const"], writes=["rowx"])
                P.emit("vector", lambda E, l=l: E.tensor_tensor(
                    ROWX[:, 4 * l + 2:4 * l + 3], ROWX[:, 4 * l:4 * l + 1],
                    ROWX[:, 4 * l + 1:4 * l + 2], ALU.mult),
                    reads=["rowx"], writes=["rowx"])
                P.emit("vector", lambda E, l=l: E.tensor_scalar(
                    ROWXB[:, l:l + 1], ROWX[:, 4 * l + 2:4 * l + 3],
                    QK_SCALE, None, ALU.mult, ALU.bypass),
                    reads=["rowx"], writes=["rowxb"])
                P.emit("tensor", mm(pc[:, 0:1], NONR[:], ROWXB[:, l:l + 1]),
                       reads=["rowxb", "const"], psw=[pc])
                P.emit("vector", lambda E, l=l: E.tensor_copy(
                    NEGC[:, l:l + 1], pc[:, 0:1]),
                    psr=[pc], writes=["negc"])

    def phase_in(it):
        stg2 = STG[:].rearrange("p (f c) -> p f c", f=2)
        with psum(f"ptri0_{it}", [128, 512]) as pt0, \
             psum(f"ptri1_{it}", [128, 512]) as pt1:
            for st in range(16):
                half = st % 2
                hk = STGK[half]
                src_ap = d["x"][it, 128 * st:128 * (st + 1), :]
                P.emit("sync", lambda E, b=src_ap, half=half: E.dma_start(
                    out=stg2[:, half, :], in_=b), writes=[hk], sem="dx")
                for cg in range(2):
                    pt = pt0 if cg == 0 else pt1
                    fns = [lambda E, ct=4 * cg + ci, half=half, pt=pt, ci=ci:
                           E.transpose(pt[:, 128 * ci:128 * (ci + 1)],
                                       stg2[:, half, 128 * ct:128 * ct + 128],
                                       IDf[:])
                           for ci in range(4)]
                    P.emit("tensor", fns, reads=[hk, "const"], psw=[pt])
                    for ci in range(4):
                        ct = 4 * cg + ci
                        P.emit("vector", lambda E, ct=ct, st=st, pt=pt, ci=ci:
                               E.tensor_copy(X[ct][:, 128 * st:128 * (st + 1)],
                                             pt[:, 128 * ci:128 * (ci + 1)]),
                               psr=[pt], writes=[f"X{ct}"])

    def cast_H(shifted):
        for t in range(8):
            if not shifted or t < 4:
                P.emit("vector", lambda E, t=t: E.tensor_copy(H[t][:], X[t][:]),
                       reads=[f"X{t}"], writes=[f"H{t}"])
            else:
                P.emit("vector", lambda E, t=t: E.tensor_copy(
                    H[t][:, 1:T], X[t][:, 0:T - 1]),
                    reads=[f"X{t}"], writes=[f"H{t}"])
                P.emit("vector", lambda E, t=t: E.memset(H[t][:, 0:1], 0.0),
                       writes=[f"H{t}"])

    def v_transpose(tag):
        vte3 = VTe[:].rearrange("p (k c) -> p k c", c=72)
        vto3 = VTo[:].rearrange("p (k c) -> p k c", c=72)
        with psum("ptv" + tag, [128, 1024], BF16) as ptv:
            for par, vdst3 in enumerate([vte3, vto3]):
                fns = []
                for kt in range(NKT):
                    if par == 0:
                        fns.append(lambda E, kt=kt: E.transpose(
                            ptv[:, 64 * kt:64 * (kt + 1)],
                            V[0:64, 128 * kt:128 * (kt + 1)], ID2[0:64, :]))
                    else:
                        fns.append(lambda E, kt=kt: E.transpose(
                            ptv[:, 64 * kt:64 * (kt + 1)],
                            V[64:128, 128 * kt:128 * (kt + 1)], ID2[64:128, :]))
                P.emit("tensor", fns, reads=["V", "const"], psw=[ptv])
                P.emit("vector", lambda E, v3=vdst3: E.tensor_copy(
                    v3[:, :, 0:64], ptv[:].rearrange("p (k c) -> p k c", c=64)),
                    psr=[ptv], writes=[f"VT{par}"])

    F32R = mybir.dt.float32r

    def attn_qc(o_dst, o_key, ktl_list, q0, nq_of, negc_ap, mask_of, itq,
                psc, pav_e, pav_o, pending, exp_scale=1.0, av_of=None):
        if av_of is None:
            av_of = nq_of
        av_started = [False, False]
        ngroups = 0
        for gi in range(0, len(ktl_list), 2):
            grp = ktl_list[gi:gi + 2]
            offs = []
            o = 0
            for kt, ktl in grp:
                qs, nv = nq_of(ktl)
                offs.append((kt, ktl, qs, nv, o))
                o += nv
            for par in range(2):
                b0 = 64 * par
                pskey = f"psc{par}"
                fns = [mm(psc[:, 1024 * par + oo:1024 * par + oo + nv],
                          Kp[b0:b0 + 64, 128 * kt:128 * (kt + 1)],
                          Qp[b0:b0 + 64, q0 + qs:q0 + qs + nv])
                       for kt, ktl, qs, nv, oo in offs]
                P.emit("tensor", fns, reads=["Qp", "Kp"],
                       writes=[P.bank_keys(psc)[2 * par],
                               P.bank_keys(psc)[2 * par + 1]])
                if negc_ap is not None:
                    fn = lambda E, par=par, o=o: E.activation(
                        EX[:, 1024 * par:1024 * par + o],
                        psc[:, 1024 * par:1024 * par + o], AF.Exp,
                        bias=negc_ap, scale=exp_scale)
                else:
                    fn = lambda E, par=par, o=o: E.activation(
                        EX[:, 1024 * par:1024 * par + o],
                        psc[:, 1024 * par:1024 * par + o], AF.Exp,
                        scale=exp_scale)
                P.emit("scalar", [fn], reads=["negc",
                                              P.bank_keys(psc)[2 * par],
                                              P.bank_keys(psc)[2 * par + 1]],
                       writes=[f"EX{par}"])
            for par in range(2):
                for kt, ktl, qs, nv, oo in offs:
                    mk = mask_of(ktl)
                    if mk is None:
                        continue
                    Mt, qb = mk
                    bo = 1024 * par + oo + (qb - qs)
                    P.emit("vector",
                           lambda E, bo=bo, Mt=Mt: E.tensor_tensor(
                               EX[:, bo:bo + 128], EX[:, bo:bo + 128],
                               Mt[:], ALU.mult),
                           reads=[f"EX{par}", "const"], writes=[f"EX{par}"])
            for par, pav, vt, vtk in ((0, pav_e, VTe, "VT0"),
                                      (1, pav_o, VTo, "VT1")):
                fns = []
                for kt, ktl, qs, nv, oo in offs:
                    st = not av_started[par]
                    av_started[par] = True
                    aqs, anv = av_of(ktl)
                    eo = 1024 * par + oo + aqs - qs
                    fns.append(mm(pav[0:65, aqs:aqs + anv],
                                  vt[:, 72 * kt:72 * kt + 65],
                                  EX[:, eo:eo + anv],
                                  start=st, stop=False, skip=True))
                P.emit("tensor", fns, reads=[f"EX{par}", vtk], psw=[pav])
            ngroups += 1
            if ngroups == 4 and pending:
                for fn in pending:
                    fn()
                pending = []
        for fn in pending:
            fn()

        def mk_combine(par, pav):
            tav = TMPF[0:65, :] if par == 0 else TMPF2[0:65, :]
            tkey = "tmpf" if par == 0 else "tmpf2"
            rrb = ROWSB[:, 0:QC] if par == 0 else ROWSB[:, QC:2 * QC]
            rbk = "rowsbA" if par == 0 else "rowsbB"

            def emit_combine():
                P.emit("scalar", lambda E: E.copy(tav, pav[0:65, 0:QC]),
                       psr=[pav], writes=[tkey])

                rrf = RRF[:, 0:QC] if par == 0 else RRF[:, QC:2 * QC]
                P.emit("scalar", lambda E, tav=tav, rrf=rrf: E.activation(
                    rrf, tav[64:65, :], AF.Ln),
                    reads=[tkey], writes=[rbk + "f"])
                P.emit("scalar", lambda E, rrb=rrb, rrf=rrf: E.activation(
                    rrb, rrf, AF.Exp, scale=-1.0),
                    reads=[rbk + "f"], writes=[rbk])
                P.emit("tensor", mm(pav[0:64, 0:QC], ONR[0:1, 0:64], rrb),
                       reads=[rbk, "const"], psw=[pav])
                P.emit("vector", lambda E, par=par: E.tensor_tensor(
                    o_dst(par), tav[0:64, :], pav[0:64, 0:QC], ALU.mult),
                    reads=[tkey], psr=[pav], writes=[o_key])
            return emit_combine
        return [mk_combine(0, pav_e), mk_combine(1, pav_o)]

    def global_pair(it, pair):
        hkey = f"H{pair}"
        with psum(f"pqe{it}{pair}", [128, T]) as pqe, \
             psum(f"pqo{it}{pair}", [128, T]) as pqo:
            for par, pq in ((0, pqe), (1, pqo)):
                b0 = 64 * par
                fns = [mm(pq[:, QC * q:QC * (q + 1)],
                          WQKG[b0:b0 + 64, 128 * pair:128 * (pair + 1)],
                          H[pair][b0:b0 + 64, QC * q:QC * (q + 1)])
                       for q in range(NQC)]
                P.emit("tensor", fns, reads=[hkey, "const"], psw=[pq])
            for par, pq in ((0, pqe), (1, pqo)):
                h = 2 * pair + par
                P.emit("scalar", lambda E, pq=pq, par=par, h=h: E.activation(
                    Qp[64 * par:64 * par + 64, :], pq[0:64, :], AF.Identity,
                    bias=BQ[0:64, h:h + 1]),
                    psr=[pq], reads=["const"], writes=["Qp"])
                P.emit("scalar", lambda E, pq=pq, par=par, h=h: E.activation(
                    Kp[64 * par:64 * par + 64, :], pq[64:128, :], AF.Identity,
                    bias=BK[64:128, h:h + 1]),
                    psr=[pq], reads=["const"], writes=["Kp"])
        with psum(f"pve{it}{pair}", [64, QC]) as pve, \
             psum(f"pvo{it}{pair}", [128, QC]) as pvo:
            for q in range(NQC):
                P.emit("tensor", [mm(pve[:, :],
                                     WVG[0:64, 64 * pair:64 * (pair + 1)],
                                     H[pair][0:64, QC * q:QC * (q + 1)])],
                       reads=[hkey, "const"], psw=[pve])
                P.emit("scalar", lambda E, q=q, h=2 * pair: E.activation(
                    V[0:64, QC * q:QC * (q + 1)], pve[:, :], AF.Identity,
                    bias=BV[0:64, h:h + 1]),
                    psr=[pve], reads=["const"], writes=["V"])
                P.emit("tensor", [mm(pvo[64:128, :],
                                     WVG[64:128, 64 * pair:64 * (pair + 1)],
                                     H[pair][64:128, QC * q:QC * (q + 1)])],
                       reads=[hkey, "const"], psw=[pvo])
                P.emit("scalar", lambda E, q=q, h=2 * pair + 1: E.activation(
                    V[64:128, QC * q:QC * (q + 1)], pvo[64:128, :], AF.Identity,
                    bias=BV[64:128, h:h + 1]),
                    psr=[pvo], reads=["const"], writes=["V"])
        v_transpose(f"g{it}_{pair}")
        with psum(f"gsc{it}{pair}", [128, 2048]) as psc, \
             psum(f"gae{it}{pair}", [65, 512]) as pavAe, \
             psum(f"gao{it}{pair}", [65, 512]) as pavAo, \
             psum(f"gbe{it}{pair}", [65, 512]) as pavBe, \
             psum(f"gbo{it}{pair}", [65, 512]) as pavBo:
            pending = []
            for q in range(NQC):
                def o_dst(par, q=q, pair=pair):
                    return H[pair][64 * par:64 * par + 64, QC * q:QC * (q + 1)]
                pe, po = (pavAe, pavAo) if q % 2 == 0 else (pavBe, pavBo)
                pending = attn_qc(o_dst, hkey, [(kt, kt) for kt in range(NKT)],
                                  QC * q, lambda ktl: (0, QC), None,
                                  lambda ktl: None, itq=f"g{it}_{pair}_{q}",
                                  psc=psc, pav_e=pe, pav_o=po,
                                  pending=pending, exp_scale=0.125)
            for fn in pending:
                fn()

    class Stream:
        def __init__(self):
            self.slots = [S0, S1]
            self.queue = []
            self.next_issue = 0
            self.assign = {}

        def push(self, name, fns):
            if callable(fns):
                fns = [fns]
            self.queue.append((name, fns))

        def issue_upto(self, n):
            while self.next_issue < min(n, len(self.queue)):
                i = self.next_issue
                name, fns = self.queue[i]
                slot = self.slots[i % 2]
                skey = f"slot{i % 2}"
                self.assign[name] = (slot, skey)
                for fn in fns:
                    P.emit("gpsimd", lambda E, fn=fn, slot=slot: fn(E, slot),
                           writes=[skey], sem="dw")
                self.next_issue += 1

        def use(self, name):
            idx = self.names.index(name)
            self.issue_upto(idx + 2)
            return self.assign[name]

        @property
        def names(self):
            return [n for n, _ in self.queue]

    def wo_global_q(it, stream, pw, pw2, q):
        if True:
            if True:
                for ch in range(4):
                    slot, skey = stream.use(f"wo{q}_{ch}")
                    for oc in (2 * ch, 2 * ch + 1):
                        pt = pw if (oc + q) % 2 == 0 else pw2
                        c0 = 128 * (oc - 2 * ch)
                        fns = [mm(pt[:, :],
                                  slot[:, 256 * ic + c0:256 * ic + c0 + 128],
                                  H[ic][:, QC * q:QC * (q + 1)],
                                  start=(ic == 0), stop=(ic == 7))
                               for ic in range(8)]
                        P.emit("tensor", fns, reads=[skey] + HKEYS, psw=[pt])
                        P.emit("vector", lambda E, pt=pt, oc=oc, q=q:
                               E.scalar_tensor_tensor(
                                   X[oc][:, QC * q:QC * (q + 1)], pt[:, :],
                                   BO[:, oc:oc + 1], X[oc][:, QC * q:QC * (q + 1)],
                                   ALU.add, ALU.add),
                               psr=[pt], reads=["const", f"X{oc}"],
                               writes=[f"X{oc}"])

    def module_ln(it, tag, spill, weave=None):
        mu = ROWS[:, 0:QC]
        rstd = ROWS[:, QC:2 * QC]
        tmp = ROWS[:, 2 * QC:3 * QC]
        with psum(f"plrA{it}{tag}", [1, 1024]) as plrA, \
             psum(f"plrB{it}{tag}", [1, 1024]) as plrB, \
             psum(f"pm1A{it}{tag}", [128, QC]) as pm1A, \
             psum(f"pm2A{it}{tag}", [128, QC]) as pm2A:
            pm1B, pm2B = pm1A, pm2A
            def stats(q):
                plr = plrA if q % 2 == 0 else plrB
                for t in range(8):
                    cb = V[:, 0:QC] if t % 2 == 0 else V[:, QC:2 * QC]
                    cbk = "V" if t % 2 == 0 else "V"
                    sq = TMPB if t % 2 == 0 else V[:, 2 * QC:3 * QC]
                    sqk = "tmpb" if t % 2 == 0 else "V"
                    P.emit("vector", lambda E, t=t, q=q, cb=cb: E.tensor_copy(
                        cb, X[t][:, QC * q:QC * (q + 1)]),
                        reads=[f"X{t}"], writes=[cbk])
                    P.emit("tensor", [mm(plr[:, 0:512], OCL[:, 0:1], cb,
                                         start=(t == 0), stop=(t == 7))],
                           reads=[cbk, "const"], psw=[plr])
                    P.emit("vector", lambda E, cb=cb, sq=sq: E.tensor_tensor(
                        sq[:, 0:QC], cb, cb, ALU.mult),
                        reads=[cbk], writes=[sqk])
                    P.emit("tensor", [mm(plr[:, 512:1024], OCL[:, 0:1],
                                         sq[:, 0:QC], start=(t == 0),
                                         stop=(t == 7))],
                           reads=[sqk, "const"], psw=[plr])

            def finish(q):
                plr = plrA if q % 2 == 0 else plrB
                pb1 = pm1A if q % 2 == 0 else pm1B
                pb2 = pm2A if q % 2 == 0 else pm2B
                P.emit("vector", lambda E, plr=plr: E.tensor_scalar(
                    mu, plr[:, 0:512], 1.0 / D, None, ALU.mult, ALU.bypass),
                    psr=[plr], writes=["rows"])
                P.emit("vector", lambda E: E.tensor_tensor(
                    tmp, mu, mu, ALU.mult), reads=["rows"], writes=["rows"])
                P.emit("vector", lambda E, plr=plr: E.scalar_tensor_tensor(
                    rstd, plr[:, 512:1024], 1.0 / D, tmp, ALU.mult, ALU.subtract),
                    psr=[plr], reads=["rows"], writes=["rows"])
                P.emit("scalar", lambda E: E.activation(
                    rstd, rstd, AF.Ln, bias=EPS[:, 0:1]),
                    reads=["rows", "const"], writes=["rows"])
                P.emit("scalar", lambda E: E.activation(
                    rstd, rstd, AF.Exp, scale=-0.5),
                    reads=["rows"], writes=["rows"])
                P.emit("tensor", mm(pb1[:, :], ONRF[:], mu),
                       reads=["rows", "const"], psw=[pb1])
                P.emit("tensor", mm(pb2[:, :], ONRF[:], rstd),
                       reads=["rows", "const"], psw=[pb2])
                for t in range(8):
                    P.emit("vector", lambda E, t=t, q=q, pb1=pb1: E.tensor_tensor(
                        X[t][:, QC * q:QC * (q + 1)], X[t][:, QC * q:QC * (q + 1)],
                        pb1[:, :], ALU.subtract),
                        reads=[f"X{t}"], psr=[pb1], writes=[f"X{t}"])
                    P.emit("vector", lambda E, t=t, q=q, pb2=pb2: E.tensor_tensor(
                        X[t][:, QC * q:QC * (q + 1)], X[t][:, QC * q:QC * (q + 1)],
                        pb2[:, :], ALU.mult),
                        reads=[f"X{t}"], psr=[pb2], writes=[f"X{t}"])
                if spill:
                    for t in range(8):
                        P.emit("sync", lambda E, t=t, q=q: E.dma_start(
                            out=d["x1sp"][t][:, QC * q:QC * (q + 1)],
                            in_=X[t][:, QC * q:QC * (q + 1)]),
                            reads=[f"X{t}"], writes=["x1d"], sem="dx")

            if weave is None:
                stats(0)
                for q in range(1, NQC):
                    stats(q)
                    finish(q - 1)
                finish(NQC - 1)
            else:
                for q in range(NQC):
                    weave(q)
                    stats(q)
                    if q > 0:
                        finish(q - 1)
                finish(NQC - 1)

    def inner_ln(it, l):
        lgv = LGT[:, 8 * l:8 * l + 8]
        lbv = LBT[:, 8 * l:8 * l + 8]
        mu = ROWS[:, 0:QC]
        rstd = ROWS[:, QC:2 * QC]
        tmp = ROWS[:, 2 * QC:3 * QC]
        with psum(f"pilA{it}{l}", [1, 1024]) as plrA, \
             psum(f"pilB{it}{l}", [1, 1024]) as plrB, \
             psum(f"pi1A{it}{l}", [128, QC]) as pi1A, \
             psum(f"pi2A{it}{l}", [128, QC]) as pi2A, \
             psum(f"pi1B{it}{l}", [128, QC]) as pi1B, \
             psum(f"pi2B{it}{l}", [128, QC]) as pi2B:
            def stats(q):
                plr = plrA if q % 2 == 0 else plrB
                for t in range(8):
                    cb = V[:, 0:QC] if t % 2 == 0 else V[:, QC:2 * QC]
                    cbk = "V" if t % 2 == 0 else "V"
                    sq = TMPB if t % 2 == 0 else V[:, 2 * QC:3 * QC]
                    sqk = "tmpb" if t % 2 == 0 else "V"
                    if t < 4:
                        cl, n0 = 0, 512
                        P.emit("scalar", lambda E, t=t, q=q, cb=cb: E.copy(
                            cb, X[t][:, QC * q:QC * (q + 1)]),
                            reads=[f"X{t}"], writes=[cbk])
                    elif q == 0:
                        cl, n0 = 1, 511
                        P.emit("scalar", lambda E, t=t, cb=cb: E.copy(
                            cb[:, 1:QC], X[t][:, 0:QC - 1]),
                            reads=[f"X{t}"], writes=[cbk])
                    else:
                        cl, n0 = 0, 512
                        P.emit("scalar", lambda E, t=t, q=q, cb=cb: E.copy(
                            cb, X[t][:, QC * q - 1:QC * (q + 1) - 1]),
                            reads=[f"X{t}"], writes=[cbk])
                    P.emit("tensor", [mm(plr[:, cl:cl + n0], OCL[:, 0:1],
                                         cb[:, cl:cl + n0],
                                         start=(t == 0), stop=(t == 7))],
                           reads=[cbk, "const"], psw=[plr])
                    P.emit("vector", lambda E, cb=cb, sq=sq, cl=cl, n0=n0:
                           E.tensor_tensor(sq[:, cl:cl + n0], cb[:, cl:cl + n0],
                                           cb[:, cl:cl + n0], ALU.mult),
                           reads=[cbk], writes=[sqk])
                    P.emit("tensor", [mm(plr[:, 512 + cl:512 + cl + n0],
                                         OCL[:, 0:1], sq[:, cl:cl + n0],
                                         start=(t == 0), stop=(t == 7))],
                           reads=[sqk, "const"], psw=[plr])

            def finish(q):
                plr = plrA if q % 2 == 0 else plrB
                pb1 = pi1A if q % 2 == 0 else pi1B
                pb2 = pi2A if q % 2 == 0 else pi2B
                P.emit("vector", lambda E, plr=plr: E.tensor_scalar(
                    mu, plr[:, 0:512], 1.0 / D, None, ALU.mult, ALU.bypass),
                    psr=[plr], writes=["rows"])
                P.emit("vector", lambda E: E.tensor_tensor(
                    tmp, mu, mu, ALU.mult), reads=["rows"], writes=["rows"])
                P.emit("vector", lambda E, plr=plr: E.scalar_tensor_tensor(
                    rstd, plr[:, 512:1024], 1.0 / D, tmp, ALU.mult, ALU.subtract),
                    psr=[plr], reads=["rows"], writes=["rows"])
                P.emit("scalar", lambda E: E.activation(
                    rstd, rstd, AF.Ln, bias=EPS[:, 0:1]),
                    reads=["rows", "const"], writes=["rows"])
                P.emit("scalar", lambda E: E.activation(
                    rstd, rstd, AF.Exp, scale=-0.5), reads=["rows"], writes=["rows"])
                P.emit("vector", lambda E: E.tensor_copy(ROWSB[:, 0:512], mu),
                       reads=["rows"], writes=["rowsbA"])
                P.emit("vector", lambda E: E.tensor_copy(ROWSB[:, 512:1024], rstd),
                       reads=["rows"], writes=["rowsbB"])
                P.emit("tensor", mm(pb1[:, :], ONR[:], ROWSB[:, 0:512]),
                       reads=["rowsbA", "const"], psw=[pb1])
                P.emit("tensor", mm(pb2[:, :], ONR[:], ROWSB[:, 512:1024]),
                       reads=["rowsbB", "const"], psw=[pb2])
                for t in range(8):
                    dst = H[t][:, QC * q:QC * (q + 1)]
                    if t < 4:
                        P.emit("vector", lambda E, t=t, q=q, pb1=pb1:
                               E.tensor_tensor(
                                   TMPF[:], X[t][:, QC * q:QC * (q + 1)],
                                   pb1[:, :], ALU.subtract),
                               reads=[f"X{t}"], psr=[pb1], writes=["tmpf"])
                        P.emit("vector", lambda E, pb2=pb2: E.tensor_tensor(
                            TMPF[:], TMPF[:], pb2[:, :], ALU.mult),
                            reads=["tmpf"], psr=[pb2], writes=["tmpf"])
                    else:
                        if q == 0:
                            P.emit("vector", lambda E, t=t, pb1=pb1:
                                   E.tensor_tensor(
                                       TMPF[:, 1:QC], X[t][:, 0:QC - 1],
                                       pb1[:, 1:QC], ALU.subtract),
                                   reads=[f"X{t}"], psr=[pb1], writes=["tmpf"])
                            P.emit("vector", lambda E, pb2=pb2: E.tensor_tensor(
                                TMPF[:, 1:QC], TMPF[:, 1:QC], pb2[:, 1:QC],
                                ALU.mult),
                                reads=["tmpf"], psr=[pb2], writes=["tmpf"])
                            P.emit("vector", lambda E, pb1=pb1: E.tensor_scalar(
                                TMPF[:, 0:1], pb1[:, 0:1], -1.0, None,
                                ALU.mult, ALU.bypass),
                                psr=[pb1], writes=["tmpf"])
                            P.emit("vector", lambda E, pb2=pb2: E.tensor_tensor(
                                TMPF[:, 0:1], TMPF[:, 0:1], pb2[:, 0:1],
                                ALU.mult),
                                reads=["tmpf"], psr=[pb2], writes=["tmpf"])
                        else:
                            P.emit("vector", lambda E, t=t, q=q, pb1=pb1:
                                   E.tensor_tensor(
                                       TMPF[:], X[t][:, QC * q - 1:QC * (q + 1) - 1],
                                       pb1[:, :], ALU.subtract),
                                   reads=[f"X{t}"], psr=[pb1], writes=["tmpf"])
                            P.emit("vector", lambda E, pb2=pb2: E.tensor_tensor(
                                TMPF[:], TMPF[:], pb2[:, :], ALU.mult),
                                reads=["tmpf"], psr=[pb2], writes=["tmpf"])
                    P.emit("vector", lambda E, dst=dst, t=t: E.tensor_scalar(
                        dst, TMPF[:], lgv[:, t:t + 1], lbv[:, t:t + 1],
                        ALU.mult, ALU.add),
                        reads=["tmpf", "const"], writes=[f"H{t}"])

            stats(0)
            for q in range(1, NQC):
                stats(q)
                finish(q - 1)
            finish(NQC - 1)

    def ffn_layer(it, l, stream):
        cast_H(shifted=True)
        dbg = (l == 0 and "hs" in dumps)
        if dbg:
            for t in range(8):
                P.emit("gpsimd", lambda E, t=t: E.dma_start(
                    out=dumps["hs"][t], in_=H[t][:]), reads=[f"H{t}"], sem="do")
        for pair in range(NPF):
            slot, skey = stream.use(f"qkv{l}{pair}")
            with psum(f"pj{it}{l}{pair}", [128, QC]) as pj1, \
                 psum(f"pj2{it}{l}{pair}", [128, QC]) as pj2:
                for oi, dstsel in ((0, "q"), (1, "k"), (2, "v")):
                    for q in range(NQC):
                        pt = pj1 if (oi * NQC + q) % 2 == 0 else pj2
                        fns = [mm(pt[:, :],
                                  slot[:, 384 * ic + 128 * oi:
                                       384 * ic + 128 * oi + 128],
                                  H[ic][:, QC * q:QC * (q + 1)],
                                  start=(ic == 0), stop=(ic == 7))
                               for ic in range(8)]
                        P.emit("tensor", fns, reads=[skey] + HKEYS, psw=[pt])
                        if dstsel == "q":
                            P.emit("scalar", lambda E, pt=pt, q=q: E.copy(
                                Qp[:, QC * q:QC * (q + 1)], pt[:, :]),
                                psr=[pt], writes=["Qp"])
                        elif dstsel == "k":
                            P.emit("scalar", lambda E, pt=pt, q=q: E.copy(
                                Kp[:, QC * q:QC * (q + 1)], pt[:, :]),
                                psr=[pt], writes=["Kp"])
                        else:
                            P.emit("scalar", lambda E, pt=pt, q=q: E.copy(
                                V[0:64, QC * q:QC * (q + 1)], pt[0:64, :]),
                                psr=[pt], writes=["V"])
                            P.emit("scalar", lambda E, pt=pt, q=q: E.copy(
                                V[64:128, QC * q:QC * (q + 1)], pt[64:128, :]),
                                psr=[pt], writes=["V"])
            for qki, (buf, bkey, Atab, Ctab) in enumerate(
                    [(Qp, "Qp", RA["rAq"], RA["rCq"]),
                     (Kp, "Kp", RA["rAk"], RA["rCk"])]):
                with psum(f"pl2{it}{l}{pair}{qki}", [1, 1024]) as pl2, \
                     psum(f"pbc{it}{l}{pair}{qki}", [128, QC]) as pbc, \
                     psum(f"psw{it}{l}{pair}{qki}", [128, QC]) as psw_:
                    for q in range(NQC):
                        P.emit("vector", lambda E, buf=buf, q=q: E.tensor_tensor(
                            TMPB[:], buf[:, QC * q:QC * (q + 1)],
                            buf[:, QC * q:QC * (q + 1)], ALU.mult),
                            reads=[bkey], writes=["tmpb"])
                        P.emit("tensor", [mm(pl2[:, 0:512], OCL[:, 1:2], TMPB[:])],
                               reads=["tmpb", "const"], psw=[pl2])
                        P.emit("tensor", [mm(pl2[:, 512:1024], OCL[:, 2:3], TMPB[:])],
                               reads=["tmpb", "const"], psw=[pl2])
                        for par in range(2):
                            seg = pl2[:, 512 * par:512 * par + 512]
                            dst = ROWS[:, 0:512] if par == 0 else ROWS[:, 512:1024]
                            P.emit("scalar", lambda E, seg=seg, dst=dst: E.activation(
                                dst, seg, AF.Ln, bias=EPS[:, 1:2]),
                                psr=[pl2], reads=["const"], writes=["rows"])
                            P.emit("scalar", lambda E, dst=dst: E.activation(
                                dst, dst, AF.Exp, scale=-0.5),
                                reads=["rows"], writes=["rows"])
                            P.emit("vector",
                                   lambda E, dst=dst, par=par: E.tensor_copy(
                                       ROWSB[:, 512 * par:512 * par + 512], dst),
                                   reads=["rows"], writes=["rowsb"])
                        ebi = ((l * 2 + qki) * 2) * 128
                        P.emit("tensor",
                               [mm(pbc[:, :], EBR[0:1, ebi:ebi + 128],
                                   ROWSB[:, 0:512], start=True, stop=False,
                                   skip=True),
                                mm(pbc[:, :], EBR[0:1, ebi + 128:ebi + 256],
                                   ROWSB[:, 512:1024], start=False, stop=True,
                                   skip=True)],
                               reads=["rowsb", "const"], psw=[pbc])
                        P.emit("vector", lambda E, buf=buf, q=q: E.tensor_tensor(
                            TMPB[:], buf[:, QC * q:QC * (q + 1)], pbc[:, :],
                            ALU.mult),
                            reads=[bkey], psr=[pbc], writes=["tmpb"])
                        P.emit("vector", lambda E, q=q, Atab=Atab: E.tensor_tensor(
                            TMPF[:], TMPB[:], Atab[:, QC * q:QC * (q + 1)],
                            ALU.mult),
                            reads=["tmpb", "const"], writes=["tmpf"])
                        P.emit("vector", lambda E, q=q, Ctab=Ctab: E.tensor_tensor(
                            TMPB[:], TMPB[:], Ctab[:, QC * q:QC * (q + 1)],
                            ALU.mult),
                            reads=["tmpb", "const"], writes=["tmpb"])
                        P.emit("tensor", mm(psw_[:, :], PRM[:], TMPB[:]),
                               reads=["tmpb", "const"], psw=[psw_])
                        P.emit("vector", lambda E, buf=buf, q=q: E.tensor_tensor(
                            buf[:, QC * q:QC * (q + 1)], TMPF[:], psw_[:, :],
                            ALU.add),
                            reads=["tmpf"], psr=[psw_], writes=[bkey])
            if dbg and pair == 0 and "t_qp" in dumps:
                for nm, tsr, keys in (("t_qp", Qp, ["Qp"]), ("t_kp", Kp, ["Kp"]),
                                      ("t_vv", V, ["V", "V"])):
                    P.emit("gpsimd", lambda E, nm=nm, tsr=tsr:
                           E.dma_start(out=dumps[nm][:, 0:tsr.shape[1]],
                                       in_=tsr[:]), reads=keys, sem="do")
            v_transpose(f"f{it}{l}{pair}")
            with psum(f"fsc{it}{l}{pair}", [128, 2048]) as psc, \
                 psum(f"fae{it}{l}{pair}", [65, 512]) as pavAe, \
                 psum(f"fao{it}{l}{pair}", [65, 512]) as pavAo, \
                 psum(f"fbe{it}{l}{pair}", [65, 512]) as pavBe, \
                 psum(f"fbo{it}{l}{pair}", [65, 512]) as pavBo:
                pending = []
                for w in range(4):
                    if w == 0:
                        ktl_list = [(kt, kt + 4) for kt in range(4)]
                    else:
                        ktl_list = [(4 * (w - 1) + i, i) for i in range(8)]

                    def nq_of(ktl):
                        return (0, QC)

                    def av_of(ktl):
                        if ktl <= 3:
                            return (0, 128 * (ktl + 1))
                        return (128 * (ktl - 4), QC - 128 * (ktl - 4))

                    def mask_of(ktl):
                        if ktl <= 3:
                            return (MLW, 128 * ktl)
                        return (MUP, 128 * (ktl - 4))

                    def o_dst(par, pair=pair, w=w):
                        return OFF[pair][64 * par:64 * par + 64,
                                         WIN * w:WIN * (w + 1)]

                    pe, po = (pavAe, pavAo) if w % 2 == 0 else (pavBe, pavBo)
                    pending = attn_qc(o_dst, f"OF{pair}", ktl_list, WIN * w,
                                      nq_of, NEGC[:, l:l + 1], mask_of,
                                      itq=f"f{it}{l}{pair}{w}",
                                      psc=psc, pav_e=pe, pav_o=po,
                                      pending=pending, exp_scale=QK_SCALE,
                                      av_of=av_of)
                for fn in pending:
                    fn()
        with psum(f"pwf{it}{l}", [128, QC]) as pw, \
             psum(f"pwf2{it}{l}", [128, QC]) as pw2:
            for q in range(NQC):
                for oc in range(8):
                    slot, skey = stream.use(f"wof{l}_{q}_{oc // 4}")
                    pt = pw if (oc + q) % 2 == 0 else pw2
                    c0 = 128 * (oc % 4)
                    fns = [mm(pt[:, :],
                              slot[:, 512 * ic + c0:512 * ic + c0 + 128],
                              OFF[ic][:, QC * q:QC * (q + 1)],
                              start=(ic == 0), stop=(ic == 3))
                           for ic in range(4)]
                    P.emit("tensor", fns,
                           reads=[skey] + [f"OF{i}" for i in range(4)], psw=[pt])
                    P.emit("vector", lambda E, pt=pt, oc=oc, q=q: E.tensor_tensor(
                        X[oc][:, QC * q:QC * (q + 1)],
                        X[oc][:, QC * q:QC * (q + 1)], pt[:, :], ALU.add),
                        psr=[pt], reads=[f"X{oc}"], writes=[f"X{oc}"])
        if dbg and "t_o0f" in dumps:
            P.emit("gpsimd", lambda E: E.dma_start(
                out=dumps["t_o0f"][:], in_=OFF[0][:]), reads=["OF0"], sem="do")
        if dbg:
            dump_X("xwo")
        inner_ln(it, l)
        if dbg and "hln" in dumps:
            for t in range(8):
                P.emit("gpsimd", lambda E, t=t: E.dma_start(
                    out=dumps["hln"][t], in_=H[t][:]), reads=[f"H{t}"], sem="do")
        w1_w2(it, l, stream)

    def w1_w2(it, l, stream):
        b1v = B1C[:, 2 * NCH * l:2 * NCH * (l + 1)]
        b2v = B2T[:, 8 * l:8 * l + 8]

        def glu_ap(j, wj):
            if j < 16:
                return OFF[j // 4][0:wj, 512 * (j % 4):512 * (j % 4) + 512], \
                    f"OF{j // 4}"
            return GLU[0:wj, 512 * (j - 16):512 * (j - 16) + 512], "glu"

        for q in range(NQC):
            with psum(f"pa{it}{l}{q}", [128, QC]) as pa, \
                 psum(f"pg{it}{l}{q}", [128, QC]) as pg, \
                 psum(f"pa2{it}{l}{q}", [128, QC]) as pa2, \
                 psum(f"pg2{it}{l}{q}", [128, QC]) as pg2:
                for j, wj in enumerate(CHW):
                    jj, sub = j // 2, j % 2
                    slot, skey = stream.use(f"w1_{l}_{q}_{jj}")
                    wid = 2 * CHW[2 * jj] + 2 * CHW[2 * jj + 1]
                    sb0 = 2 * CHW[2 * jj] * sub
                    pta = pa if j % 2 == 0 else pa2
                    ptg = pg if j % 2 == 0 else pg2
                    fa = [mm(pta[0:wj, :],
                             slot[:, wid * ic + sb0:wid * ic + sb0 + wj],
                             H[ic][:, QC * q:QC * (q + 1)],
                             start=(ic == 0), stop=(ic == 7)) for ic in range(8)]
                    fg = [mm(ptg[0:wj, :],
                             slot[:, wid * ic + sb0 + wj:wid * ic + sb0 + 2 * wj],
                             H[ic][:, QC * q:QC * (q + 1)],
                             start=(ic == 0), stop=(ic == 7)) for ic in range(8)]
                    P.emit("tensor", fa, reads=[skey] + HKEYS, psw=[pta])
                    P.emit("tensor", fg, reads=[skey] + HKEYS, psw=[ptg])
                    P.emit("scalar", lambda E, ptg=ptg, wj=wj, j=j: E.activation(
                        TMPB[0:wj, :], ptg[0:wj, :], AF.Gelu,
                        bias=b1v[0:wj, 2 * j + 1:2 * j + 2]),
                        psr=[ptg], reads=["const"], writes=["tmpb"])
                    P.emit("vector", lambda E, pta=pta, wj=wj, j=j: E.tensor_scalar(
                        TMPF[0:wj, :], pta[0:wj, :], b1v[0:wj, 2 * j:2 * j + 1],
                        None, ALU.add, ALU.bypass),
                        psr=[pta], reads=["const"], writes=["tmpf"])
                    gap, gkey = glu_ap(j, wj)
                    P.emit("vector", lambda E, gap=gap, wj=wj: E.tensor_tensor(
                        gap, TMPF[0:wj, :], TMPB[0:wj, :], ALU.mult),
                        reads=["tmpf", "tmpb"], writes=[gkey])
            with ExitStack() as es2:
                pws = [es2.enter_context(psum(f"pw2_{it}{l}{q}_{o}", [128, QC]))
                       for o in range(8)]
                for bi in range(8):
                    slot, skey = stream.use(f"w2_{l}_{q}_{bi}")
                    js = list(range(3 * bi, min(3 * bi + 3, NCH)))
                    for oc in range(8):
                        fns = []
                        rkeys = [skey]
                        for ji, j in enumerate(js):
                            wj = CHW[j]
                            gap, gkey = glu_ap(j, wj)
                            rkeys.append(gkey)
                            fns.append(mm(
                                pws[oc][:, :],
                                slot[0:wj, 1024 * ji + 128 * oc:
                                     1024 * ji + 128 * oc + 128],
                                gap,
                                start=(bi == 0 and ji == 0), stop=False,
                                skip=True))
                        P.emit("tensor", fns, reads=rkeys, psw=[pws[oc]])
                for oc in range(8):
                    P.emit("vector", lambda E, oc=oc, q=q: E.scalar_tensor_tensor(
                        X[oc][:, QC * q:QC * (q + 1)], pws[oc][:, :],
                        b2v[:, oc:oc + 1], X[oc][:, QC * q:QC * (q + 1)],
                        ALU.add, ALU.add),
                        psr=[pws[oc]], reads=["const", f"X{oc}"], writes=[f"X{oc}"])

    def finale(it):
        for t in range(8):
            P.emit("sync", lambda E, t=t: E.dma_start(out=STG[:], in_=d["x1sp"][t]),
                   writes=STGK, sem="dx")
            P.emit("vector", lambda E, t=t: E.tensor_tensor(
                X[t][:], X[t][:], STG[:], ALU.add),
                reads=[f"X{t}"] + STGK, writes=[f"X{t}"])
        module_ln(it, "fin", spill=False)
        stg2 = STG[:].rearrange("p (f c) -> p f c", f=2)
        with psum(f"pot{it}", [128, 512]) as pt0, \
             psum(f"pot2{it}", [128, 512]) as pt1:
            for st in range(NKT):
                half = st % 2
                hk = STGK[half]
                for cg in range(2):
                    pt = pt0 if cg == 0 else pt1
                    fns = [lambda E, ct=4 * cg + ci, ci=ci, st=st, pt=pt:
                           E.transpose(pt[:, 128 * ci:128 * (ci + 1)],
                                       X[ct][:, 128 * st:128 * (st + 1)], IDf[:])
                           for ci in range(4)]
                    P.emit("tensor", fns,
                           reads=[f"X{4 * cg + i}" for i in range(4)] + ["const"],
                           psw=[pt])
                    P.emit("vector", lambda E, pt=pt, half=half, cg=cg:
                           E.tensor_copy(stg2[:, half, 512 * cg:512 * (cg + 1)],
                                         pt[:, :]),
                           psr=[pt], writes=[hk])
                P.emit("sync", lambda E, st=st, half=half, it=it: E.dma_start(
                    out=d["out"][it, 128 * st:128 * (st + 1), :],
                    in_=stg2[:, half, :]),
                    reads=[hk], writes=[], sem="do")

    def dump_X(nm):
        if nm not in dumps:
            return
        for t in range(8):
            P.emit("sync", lambda E, t=t, nm=nm: E.dma_start(
                out=dumps[nm][t], in_=X[t][:]),
                reads=[f"X{t}"], sem="do")

    def convert_w1w2():
        ci = [0]

        def conv(src_ap, dst_ap, nt, no, wkey):
            slot = S0 if ci[0] % 2 == 0 else S1
            skey = f"slot{ci[0] % 2}"
            ci[0] += 1
            sv = slot[:, 0:nt * no].rearrange("p (t o) -> p t o", o=no)
            P.emit("gpsimd", lambda E: E.dma_start(
                out=sv, in_=src_ap), writes=[skey], sem="dw")
            P.emit("sync", lambda E: E.dma_start(
                out=dst_ap, in_=sv), reads=[skey],
                writes=[wkey], sem="dx")
        for l in range(2):
            for c in range(15):
                c0 = 364 * c
                conv(d["w1_pm"][l][:, c0:c0 + 364].rearrange(
                        "(t p) o -> p t o", p=128),
                     d["w1b"][l][:, c0:c0 + 364].rearrange(
                        "(t p) o -> p t o", p=128),
                     8, 364, f"w1b{l}")
            for c in range(7):
                r0 = 384 * c
                conv(d["w2_n"][l][r0:r0 + 384, :].rearrange(
                        "(c p) o -> p c o", p=128),
                     d["w2b"][l][r0:r0 + 384, :].rearrange(
                        "(c p) o -> p c o", p=128),
                     3, 1024, f"w2b{l}")
            slot = S0 if ci[0] % 2 == 0 else S1
            skey = f"slot{ci[0] % 2}"
            ci[0] += 1
            P.emit("gpsimd", lambda E, slot=slot, l=l: E.dma_start(
                out=slot[0:42, 0:1024], in_=d["w2_n"][l][2688:2730, :]),
                writes=[skey], sem="dw")
            P.emit("sync", lambda E, slot=slot, l=l: E.dma_start(
                out=d["w2b"][l][2688:2730, :], in_=slot[0:42, 0:1024]),
                reads=[skey], writes=[f"w2b{l}"], sem="dx")

    load_consts()
    convert_w1w2()
    calc_negc()
    for it in range(n_items):
        stream = Stream()
        for q in range(NQC):
            for ch in range(4):
                stream.push(f"wo{q}_{ch}", lambda E, slot, ch=ch: E.dma_start(
                    out=slot[:, 0:2048].rearrange("p (t o) -> p t o", o=256),
                    in_=d["wo_g"][:, 256 * ch:256 * (ch + 1)].rearrange(
                        "(t p) o -> p t o", p=128)))
        for l in range(2):
            for pair in range(NPF):
                stream.push(f"qkv{l}{pair}",
                            lambda E, slot, l=l, pair=pair: E.dma_start(
                                out=slot[:, 0:8 * 384].rearrange(
                                    "p (t o) -> p t o", o=384),
                                in_=d["wqkv_pm"][l, pair].rearrange(
                                    "(t p) o -> p t o", p=128)))
            for q in range(NQC):
                for hh in range(2):
                    stream.push(f"wof{l}_{q}_{hh}",
                                lambda E, slot, l=l, hh=hh: E.dma_start(
                                    out=slot[:, 0:2048].rearrange(
                                        "p (t o) -> p t o", o=512),
                                    in_=d["wo_f"][l][:, 512 * hh:512 * (hh + 1)]
                                    .rearrange("(t p) o -> p t o", p=128)))
            for q in range(NQC):
                for jj in range(11):
                    j0, j1 = 2 * jj, 2 * jj + 1
                    o0 = _W1_PAIR_OFF[j0]
                    wid = 2 * CHW[j0] + 2 * CHW[j1]
                    stream.push(f"w1_{l}_{q}_{jj}",
                                lambda E, slot, l=l, o0=o0, wid=wid: E.dma_start(
                                    out=slot[:, 0:8 * wid].rearrange(
                                        "p (t o) -> p t o", o=wid),
                                    in_=d["w1b"][l][:, o0:o0 + wid].rearrange(
                                        "(t p) o -> p t o", p=128)))
                for bi in range(7):
                    r0 = 384 * bi
                    stream.push(f"w2_{l}_{q}_{bi}",
                                lambda E, slot, l=l, r0=r0: E.dma_start(
                                    out=slot[:, 0:3072].rearrange(
                                        "p (c o) -> p c o", o=1024),
                                    in_=d["w2b"][l][r0:r0 + 384, :].rearrange(
                                        "(c p) o -> p c o", p=128)))
                stream.push(f"w2_{l}_{q}_7", lambda E, slot, l=l: E.dma_start(
                    out=slot[0:42, 0:1024], in_=d["w2b"][l][2688:2730, :]))
        stream.issue_upto(2)
        phase_in(it)
        cast_H(shifted=False)
        if dev_stage == "gpair0":
            global_pair(it, 0)
            for nm, tsr, keys in (("t_qp", Qp, ["Qp"]), ("t_kp", Kp, ["Kp"]),
                                  ("t_vv", V, ["V", "V"]),
                                  ("t_vte", VTe, ["VT0"]),
                                  ("t_o0", H[0], ["H0"]),
                                  ("t_x0", X[0], ["X0"])):
                if nm in dumps:
                    w = tsr.shape[1]
                    P.emit("gpsimd", lambda E, a=dumps[nm][:, 0:w], b=tsr[:]:
                           E.dma_start(out=a, in_=b), reads=keys, sem="do")
            break
        for pair in range(NPG):
            global_pair(it, pair)
        if dev_stage == "gattn":
            for t in range(8):
                if "o_all" in dumps:
                    P.emit("gpsimd", lambda E, t=t: E.dma_start(
                        out=dumps["o_all"][t], in_=H[t][:]),
                        reads=[f"H{t}"], sem="do")
        with psum(f"pwo{it}", [128, QC]) as pw, \
             psum(f"pwo2{it}", [128, QC]) as pw2:
            if dev_stage == "gattn":
                for q in range(NQC):
                    wo_global_q(it, stream, pw, pw2, q)
                dump_X("post_attn")
                break

            def weave(q, pw=pw, pw2=pw2):
                wo_global_q(it, stream, pw, pw2, q)
            module_ln(it, "l1", spill=True, weave=weave)
        dump_X("post_ln1")
        for l in range(2):
            ffn_layer(it, l, stream)
            dump_X(f"post_ffn{l}")
        finale(it)
    P._wait("sync", "do", P.cnt["do"])
    P.replay(block)
    es.close()
    return nc, P, dumps


def kernel(**inputs):
    x = np.ascontiguousarray(inputs["L0_attn_x"], dtype=np.float32)
    shared = {}
    for k, v in _host_consts().items():
        shared[k] = np.ascontiguousarray(v, np.float32)
    for k, v in _layout_weights(inputs).items():
        shared[k] = np.ascontiguousarray(v, np.float32)
    nc, _, _ = build_program()
    in_maps = []
    for c in range(NCORES):
        m = dict(shared)
        m["x"] = np.ascontiguousarray(x[IPC * c:IPC * (c + 1)])
        in_maps.append(m)
    res = run_bass_kernel_spmd(nc, in_maps, core_ids=list(range(NCORES)))
    return np.concatenate([r["out"] for r in res.results], axis=0)
